# version 05
# baseline (speedup 1.0000x reference)
import contextlib
import numpy as np

import concourse.bass as bass
import concourse.mybir as mybir
from concourse.bass_utils import run_bass_kernel_spmd

F32 = mybir.dt.float32
BF16 = mybir.dt.bfloat16
I8 = mybir.dt.int8
I32 = mybir.dt.int32
AF = mybir.ActivationFunctionType
ALU = mybir.AluOpType
AXX = mybir.AxisListType.X

P, D, H = 128, 1024, 4096
B, T, NCORE = 2, 8192, 8
TC = (B * T) // NCORE
KC, HC, NTT = D // P, H // P, 4
BIGNEG = -1.0e30
BIGOFF = 1.0e18
EPS32 = float(np.float32(1e-5))
FAT_IT = 12
LOC_IT = 18


class Eng:
    def __init__(self, name, chain=True):
        self.name, self.ops, self.n, self.sem = name, [], 0, None
        self.chain = chain

    def do(self, fn, waits=(), inc=True, chain=None):
        waits = tuple(w for w in waits if w is not None)
        prev, me = self.n, self
        ch = self.chain if chain is None else chain

        def op(e):
            if prev and ch:
                e.wait_ge(me.sem, prev)
            for (s, v) in waits:
                if s is not me.sem:
                    e.wait_ge(s, v)
            ins = fn(e)
            if inc:
                ins.then_inc(me.sem, 1)
        self.ops.append(op)
        if inc:
            self.n += 1
        return (self.sem, self.n)

    def tok(self):
        return (self.sem, self.n)


def build_program(debug=False):
    kw = {"kind": "ExternalOutput"} if debug else {}
    nc = bass.Bass("TRN2", target_bir_lowering=False, debug=False)

    xs = nc.dram_tensor("x", [D, TC], F32, kind="ExternalInput").ap()
    wiT = nc.dram_tensor("wiT", [D, D], F32, kind="ExternalInput").ap()
    wfT = nc.dram_tensor("wfT", [D, D], F32, kind="ExternalInput").ap()
    wgT = nc.dram_tensor("wgT", [D, D], F32, kind="ExternalInput").ap()
    woT = nc.dram_tensor("woT", [D, D], F32, kind="ExternalInput").ap()
    w1T = nc.dram_tensor("w1T", [D, H], F32, kind="ExternalInput").ap()
    w2T = nc.dram_tensor("w2T", [H, D], F32, kind="ExternalInput").ap()
    aux = nc.dram_tensor("aux", [P, 32], F32, kind="ExternalInput").ap()
    out = nc.dram_tensor("out", [D, TC], F32, kind="ExternalOutput").ap()

    s_h0 = nc.dram_tensor("s_h0", [D, TC], F32, **kw).ap()
    s_xn16 = nc.dram_tensor("s_xn16", [D, TC], BF16, **kw).ap()
    s_P = nc.dram_tensor("s_P", [D, TC], F32, **kw).ap()
    s_si = nc.dram_tensor("s_si", [D, TC], F32).ap()
    s_w1q = nc.dram_tensor("s_w1q", [D, H], BF16, **kw).ap()
    s_w2q = nc.dram_tensor("s_w2q", [H, D], BF16).ap()
    s_hg = nc.dram_tensor("s_hg", [H, TC], F32, **kw).ap()
    s_sat = nc.dram_tensor("s_sat", [D, TC], F32, **kw).ap()
    s_bnc = nc.dram_tensor("s_bnc", [1, 65536], F32).ap()
    s_cons = nc.dram_tensor("s_cons", [P, 256], F32).ap()

    cc_car_i = nc.dram_tensor("cc_car_i", [P, 16], F32)
    cc_car_o = nc.dram_tensor("cc_car_o", [4 * P, 16], F32)
    cc_q1_i = nc.dram_tensor("cc_q1_i", [P, 512], F32)
    cc_q1_o = nc.dram_tensor("cc_q1_o", [NCORE * P, 512], F32)
    cc_q2_i = nc.dram_tensor("cc_q2_i", [P, 1024], F32)
    cc_q2_o = nc.dram_tensor("cc_q2_o", [NCORE * P, 1024], F32)
    cc_w1_i = nc.dram_tensor("cc_w1_i", [P, 520], F32)
    cc_w1_o = nc.dram_tensor("cc_w1_o", [NCORE * P, 520], F32)
    cc_w2_i = nc.dram_tensor("cc_w2_i", [P, 520], F32)
    cc_w2_o = nc.dram_tensor("cc_w2_o", [NCORE * P, 520], F32)

    ctx = contextlib.ExitStack()

    def sb(name, shape, dt=F32):
        return ctx.enter_context(nc.sbuf_tensor(name, shape, dt))

    ARC = 43008
    arena = sb("arena", [P, ARC])
    auxT = sb("auxT", [P, 32])
    onesD = sb("onesD", [P, 1])
    zcol = sb("zcol", [P, 1])
    S0 = sb("S0", [P, 512])
    S1 = sb("S1", [P, 512])
    M0 = sb("M0", [P, 512])
    M1 = sb("M1", [P, 512])
    SS = sb("SS", [P, 512])
    scr = sb("scr", [1, 64])
    scrI = sb("scrI", [1, 8], I32)
    scrU = sb("scrU", [1, 16], mybir.dt.uint8)
    cmu8 = sb("cmu8", [P, 4], mybir.dt.uint8)
    scrW = sb("scrW", [P, 16])
    cons = sb("cons", [P, 2048])
    brow = sb("brow", [1, 256])
    m8s = sb("m8s", [P, 8])
    carry = sb("carry", [P, 96])
    stg = sb("stg", [32, 64])
    ex2 = sb("ex2", [P, 256])

    pb = [ctx.enter_context(nc.psum_tensor(f"pb{i}", [P, 512], F32))
          for i in range(8)]
    sems = {k: ctx.enter_context(nc.semaphore("sem_" + k))
            for k in ("dma", "pe", "act", "dve", "pool", "cc")}

    SY, TE, SC_, VE, GP = (Eng(n, chain=(n != "tensor"))
                           for n in ("sync", "tensor", "scalar",
                                     "vector", "gpsimd"))
    TE.sem, SC_.sem, VE.sem, GP.sem, SY.sem = (
        sems["pe"], sems["act"], sems["dve"], sems["pool"], sems["dma"])

    ndma = [0]

    def dma(dst, src, waits=(), slow=False):
        waits = tuple(w for w in waits if w is not None)

        def op(e):
            for (s, v) in waits:
                e.wait_ge(s, v)
            e.dma_start(out=dst, in_=src,
                        allow_slow_non_contiguous=slow).then_inc(
                sems["dma"], 16)
        SY.ops.append(op)
        ndma[0] += 1
        return (sems["dma"], 16 * ndma[0])

    ncc = [0]

    def collective(groups, in_t, out_t, waits=()):
        waits = tuple(w for w in waits if w is not None)

        def op(e):
            for (s, v) in waits:
                e.wait_ge(s, v)
            e.collective_compute("AllGather", ALU.bypass, replica_groups=groups,
                                 ins=[in_t.ap().opt()], outs=[out_t.ap().opt()]
                                 ).then_inc(sems["cc"], 1)
        GP.ops.append(op)
        ncc[0] += 1
        return (sems["cc"], ncc[0])

    def A(off, cols):
        assert 0 <= off and off + cols <= ARC, f"arena [{off},{off + cols})"
        return arena[:, off:off + cols]

    def Ab(off, bf16_cols):
        return A(off, bf16_cols // 2).bitcast(BF16)

    def Ai(off, i8_cols):
        return A(off, i8_cols // 4).bitcast(I8)

    XS = [i * 2048 for i in range(9)]
    HS = [18432 + i * 2048 for i in range(8)]
    RW = 34816
    RQ = 41472
    RW_A, RW_B, RW_C = RW, RW + 4096, RW + 6144

    ve = lambda fn, waits=(): VE.do(fn, waits=waits)
    ac = lambda fn, waits=(): SC_.do(fn, waits=waits)
    te = lambda fn, waits=(), chain=None: TE.do(fn, waits=waits, chain=chain)

    t_aux = dma(auxT[:], aux)
    ve(lambda e: e.memset(onesD[:], 1.0 / D))
    ve(lambda e: e.memset(zcol[:], 0.0))
    ve(lambda e: e.memset(scr[:], 0.0))
    ve(lambda e: e.memset(scrW[:], 0.0))
    ve(lambda e: e.memset(carry[:], 0.0))
    ve(lambda e: e.memset(stg[:], 0.0))
    ve(lambda e: e.memset(cons[:], BIGNEG))
    ve(lambda e: e.memset(ex2[:], BIGNEG))
    ve(lambda e: e.memset(m8s[:], 0.0))
    t_cm = ve(lambda e: e.tensor_copy(cmu8[:, 0:3], auxT[:, 24:27]),
              waits=[t_aux])
    t_init = VE.tok()

    normw_c = lambda k: auxT[:, k:k + 1]
    lng_c = lambda k: auxT[:, 8 + k:8 + k + 1]
    lnb_c = lambda k: auxT[:, 16 + k:16 + k + 1]
    cmask_c = lambda q: auxT[:, 24 + q:24 + q + 1]
    eps_col = auxT[:, 27:28]
    q995_c = auxT[0:1, 28:29]

    def st_mu(tt):
        S = S0 if tt < 3 else S1
        r = 32 * tt if tt < 3 else 0
        return S[r:r + 1, :]

    def st_m2(tt):
        M = M0 if tt < 3 else M1
        r = 32 * tt if tt < 3 else 0
        return M[r:r + 1, :]

    def emit_mean_stats(src_ap_fn, src_tok, sq_off):
        last_mu = last_m2 = None
        tq = None
        for k in range(KC):
            w = [src_tok[k]]
            if last_m2 is not None:
                w.append(last_m2)
            tq = ac(lambda e, k_=k, s_=sq_off: e.activation(
                out=A(s_, 2048), in_=src_ap_fn(k_), func=AF.Square), waits=w)
            for tt in range(NTT):
                bmu, rmu = (4, 32 * tt) if tt < 3 else (5, 0)
                last_mu = te(lambda e, k_=k, t_=tt, b_=bmu, r_=rmu: e.matmul(
                    pb[b_][r_:r_ + 1, :], onesD[:],
                    src_ap_fn(k_)[:, t_ * 512:(t_ + 1) * 512],
                    start=(k_ == 0), stop=(k_ == KC - 1)),
                    waits=[src_tok[k], t_init])
                bm2, rm2 = (6, 32 * tt) if tt < 3 else (7, 0)
                last_m2 = te(lambda e, k_=k, t_=tt, b_=bm2, r_=rm2, s_=sq_off:
                             e.matmul(pb[b_][r_:r_ + 1, :], onesD[:],
                                      A(s_, 2048)[:, t_ * 512:(t_ + 1) * 512],
                                      start=(k_ == 0), stop=(k_ == KC - 1)),
                             waits=[tq])
        return last_mu, last_m2

    def evac_stats(tok_mu, tok_m2):
        toks = [ve(lambda e: e.tensor_copy(S0[0:96, :], pb[4][0:96, :]),
                   waits=[tok_mu]),
                ve(lambda e: e.tensor_copy(S1[0:32, :], pb[5][0:32, :]),
                   waits=[tok_mu]),
                ve(lambda e: e.tensor_copy(M0[0:96, :], pb[6][0:96, :]),
                   waits=[tok_m2]),
                ve(lambda e: e.tensor_copy(M1[0:32, :], pb[7][0:32, :]),
                   waits=[tok_m2])]
        return toks

    def coef_seq(w0):
        done = []
        for (Smu, Sm2, r) in ((S0, M0, 96), (S1, M1, 32)):
            t1 = ve(lambda e, m_=Smu, r_=r: e.tensor_tensor(
                SS[0:r_, :], m_[0:r_, :], m_[0:r_, :], ALU.mult), waits=w0)
            t2 = ve(lambda e, v_=Sm2, r_=r: e.tensor_tensor(
                v_[0:r_, :], v_[0:r_, :], SS[0:r_, :], ALU.subtract),
                waits=[t1])
            t3 = ac(lambda e, v_=Sm2, r_=r: e.activation(
                out=v_[0:r_, :], in_=v_[0:r_, :], func=AF.Ln,
                bias=eps_col[0:r_, :]), waits=[t2, t_aux])
            t4 = ac(lambda e, v_=Sm2, r_=r: e.activation(
                out=v_[0:r_, :], in_=v_[0:r_, :], func=AF.Exp, scale=-0.5),
                waits=[t3])
            t5 = ve(lambda e, m_=Smu, v_=Sm2, r_=r: e.tensor_tensor(
                m_[0:r_, :], m_[0:r_, :], v_[0:r_, :], ALU.mult), waits=[t4])
            t6 = ve(lambda e, m_=Smu, r_=r: e.tensor_scalar(
                m_[0:r_, :], m_[0:r_, :], -1.0, None, ALU.mult), waits=[t5])
            done.append(t6)
        return done

    nbnc = [0]

    def bcast_rows(rows, dst_ap, w0, cols=512):
        off = nbnc[0]
        nbnc[0] += max(len(rows) * cols, 8)
        assert nbnc[0] <= 32768
        toks = [dma(s_bnc[0:1, off + i * cols: off + (i + 1) * cols], r,
                    waits=w0) for i, r in enumerate(rows)]
        return dma(dst_ap, s_bnc[0:1, off:off + len(rows) * cols]
                   .to_broadcast([P, len(rows) * cols]), waits=toks)

    t_x = [dma(A(XS[k], 2048), xs[k * P:(k + 1) * P, :], waits=[t_init])
           for k in range(KC)]

    mu1, m21 = emit_mean_stats(lambda k: A(XS[k], 2048), t_x, XS[8])
    ev1 = evac_stats(mu1, m21)
    cf1 = coef_seq(ev1)
    a1_rep, b1_rep = A(RW_A, 2048), A(RW_A + 2048, 2048)
    t_a1 = bcast_rows([st_m2(tt) for tt in range(NTT)], a1_rep, cf1)
    t_b1 = bcast_rows([st_mu(tt) for tt in range(NTT)], b1_rep, cf1)

    t_xn, t_x16 = {}, {}
    for k in range(KC):
        t1 = ve(lambda e, k_=k: e.tensor_tensor(
            A(XS[k_], 2048), A(XS[k_], 2048), a1_rep, ALU.mult),
            waits=[t_a1, t_x[k]])
        t2 = ve(lambda e, k_=k: e.tensor_tensor(
            A(XS[k_], 2048), A(XS[k_], 2048), b1_rep, ALU.add),
            waits=[t_b1, t1])
        t_xn[k] = t2
        c16 = Ab(HS[4 + k // 2] + (k % 2) * 1024, 2048)
        t3 = ve(lambda e, c_=c16, k_=k: e.tensor_copy(c_, A(XS[k_], 2048)),
                waits=[t2])
        t_x16[k] = t3
        dma(s_xn16[k * P:(k + 1) * P, :], c16, waits=[t3])

    mu2, m22 = emit_mean_stats(lambda k: A(XS[k], 2048),
                               [t_xn[k] for k in range(KC)], XS[8])
    ev2 = evac_stats(mu2, m22)
    cf2 = coef_seq(ev2)
    a2_rep, b2_rep = A(RW_B, 2048), A(RW_A + 2048, 2048)
    t_a2 = bcast_rows([st_m2(tt) for tt in range(NTT)], a2_rep,
                      cf2 + [t_b1, t_xn[KC - 1]])
    t_b2 = bcast_rows([st_mu(tt) for tt in range(NTT)], b2_rep, cf2)

    red1q1 = A(RQ, 1024)
    contq1 = A(RQ + 1024, 512)
    t_h0, t_m8 = {}, None
    for k in range(KC):
        hwrk = A(XS[8], 2048)
        w = [t_a2, t_b2, t_xn[k], t_x16[k]]
        if k > 0:
            w += [t_m8, t_h0[k - 1]]
        t1 = ve(lambda e, k_=k, h_=hwrk: e.tensor_tensor(
            h_, A(XS[k_], 2048), a2_rep, ALU.mult), waits=w)
        t2 = ve(lambda e, h_=hwrk: e.tensor_tensor(h_, h_, b2_rep, ALU.add),
                waits=[t1])
        t3 = ve(lambda e, k_=k, h_=hwrk: e.tensor_scalar(
            h_, h_, lng_c(k_), lnb_c(k_), ALU.mult, ALU.add),
            waits=[t2, t_aux])
        t_h0[k] = t3
        dma(s_h0[k * P:(k + 1) * P, :], hwrk, waits=[t3])
        ta = ac(lambda e, k_=k, h_=hwrk: e.activation(
            out=A(XS[k_], 2048), in_=h_, func=AF.Abs), waits=[t3])
        for w8 in range(16):
            ta = ve(lambda e, k_=k, w_=w8: e.max(
                out=red1q1[:, k_ * P + w_ * 8: k_ * P + (w_ + 1) * 8],
                in_=A(XS[k_], 2048)[:, w_ * 128:(w_ + 1) * 128]), waits=[ta])
        t_m8 = ta

    t_c1 = t_m8
    for w8 in range(64):
        t_c1 = ve(lambda e, w_=w8: e.max(
            out=contq1[:, w_ * 8:(w_ + 1) * 8],
            in_=red1q1[:, w_ * 16:(w_ + 1) * 16]), waits=[t_c1])

    t_ci1 = dma(cc_q1_i.ap(), contq1, waits=[t_c1])
    t_gc1 = collective([list(range(NCORE))], cc_q1_i, cc_q1_o, waits=[t_ci1])
    q1segs = [(RW_A + r * 512, 512) for r in range(NCORE)]

    def emit_projection(wsrc, rhs_ap_fn, rhs_tok, evac_fn, strip_off, pre=None,
                        interleave=None, first_mm_waits=()):
        evt = {}
        last_strip = None
        for do in range(KC):
            strip = A(strip_off, 1024)
            castt = Ab(strip_off + 1024, 1024)
            tl = dma(strip.rearrange("p (k j) -> p k j", j=P),
                     wsrc[:, do * P:(do + 1) * P].rearrange(
                         "(k p) j -> p k j", p=P), waits=[pre, last_strip])
            tc_ = ve(lambda e, s_=strip, c_=castt: e.tensor_copy(c_, s_),
                     waits=[tl])
            for k in range(KC):
                for tt in range(NTT):
                    w = [tc_, rhs_tok]
                    if k == 0 and do == 0:
                        w.extend(first_mm_waits)
                    if k == 0 and do > 0:
                        w.append(evt[(do - 1, tt)])
                    mm = te(lambda e, k_=k, t_=tt, c_=castt: e.matmul(
                        pb[t_][:, :], c_[:, k_ * P:(k_ + 1) * P],
                        rhs_ap_fn(k_, t_),
                        start=(k_ == 0), stop=(k_ == KC - 1)), waits=w)
                    if k == KC - 1:
                        evt[(do, tt)] = evac_fn(do, tt, mm)
            last_strip = TE.tok()
            if interleave is not None:
                interleave(do)
        return evt

    def xn16_ap(k, tt):
        return Ab(HS[4 + k // 2] + (k % 2) * 1024, 2048)[:, tt * 512:(tt + 1) * 512]

    f_ev = {}

    def f_evac(do, tt, mm):
        tok = ac(lambda e, d_=do, t_=tt: e.activation(
            out=A(XS[d_], 2048)[:, t_ * 512:(t_ + 1) * 512],
            in_=pb[t_][:, :], func=AF.Sigmoid),
            waits=[mm, t_m8, t_h0[KC - 1]])
        f_ev[(do, tt)] = tok
        return tok

    emit_projection(wfT, xn16_ap, t_x16[KC - 1], f_evac, XS[8],
                    pre=t_h0[KC - 1])
    t_f = {do: f_ev[(do, NTT - 1)] for do in range(KC)}

    def emit_gamma(wsrc, rows, cols, strip_offs, scr_slot, pre=None,
                   first_wait=()):
        tw = 2048 if cols % 2048 == 0 else 1024
        tiles = [(rc, cc) for rc in range(rows // P) for cc in range(cols // tw)]
        lastmm = None
        for i, (rc, cc) in enumerate(tiles):
            so = strip_offs[i % 2]
            w = [t_init, pre]
            if i >= 2:
                w.append(lastmm)
            tl = dma(A(so, tw),
                     wsrc[rc * P:(rc + 1) * P, cc * tw:(cc + 1) * tw],
                     waits=w)
            tab = ac(lambda e, s_=so, tw_=tw: e.activation(
                out=A(s_, tw_), in_=A(s_, tw_), func=AF.Abs), waits=[tl])
            for q in range(tw // 512):
                lastmm = te(lambda e, s_=so, q_=q, i_=i, tw_=tw: e.matmul(
                    pb[6][0:1, :], onesD[:],
                    A(s_, tw_)[:, q_ * 512:(q_ + 1) * 512],
                    start=(i_ == 0 and q_ == 0),
                    stop=(i_ == len(tiles) - 1 and q_ == tw_ // 512 - 1)),
                    waits=[tab] + (list(first_wait) if i == 0 and q == 0
                                   else []))
        tev = ve(lambda e: e.tensor_copy(SS[0:1, :], pb[6][0:1, :]),
                 waits=[lastmm])
        tr = ve(lambda e: e.tensor_reduce(scr[0:1, 40:41], SS[0:1, :], AXX,
                                          ALU.add), waits=[tev])
        tg = ve(lambda e, s_=scr_slot, n_=(rows * cols // D): e.tensor_scalar(
            scr[0:1, s_:s_ + 1], scr[0:1, 40:41], 1.0 / n_, EPS32,
            ALU.mult, ALU.add), waits=[tr])
        return tg

    t_g1w = emit_gamma(w1T, D, H, (HS[0], HS[1]), 0, first_wait=ev2)

    def emit_ternarize(wsrc, dst, rows, cols, strip_offs, tmp_f, tmp_i, tmp_b,
                       g_slot, tg, rg_col_idx):
        trg = ve(lambda e, g_=g_slot: e.reciprocal(
            scr[0:1, 44:45], scr[0:1, g_:g_ + 1]), waits=[tg])
        trgb = bcast_rows([scr[0:1, 44:45]],
                          scrW[:, rg_col_idx:rg_col_idx + 1], [trg], cols=1)
        tw = 2048 if cols % 2048 == 0 else 1024
        tiles = [(rc, cc) for rc in range(rows // P) for cc in range(cols // tw)]
        lastq = None
        for i, (rc, cc) in enumerate(tiles):
            so = strip_offs[i % 2]
            w = [trgb]
            if i >= 2:
                w.append(lastq)
            tl = dma(A(so, tw),
                     wsrc[rc * P:(rc + 1) * P, cc * tw:(cc + 1) * tw],
                     waits=w)
            tmin = ve(lambda e, s_=so, tw_=tw: e.tensor_scalar(
                A(tmp_f, tw_), A(s_, tw_), scrW[:, rg_col_idx:rg_col_idx + 1],
                1.49, ALU.mult, ALU.min), waits=[tl])
            ti8 = ve(lambda e, tw_=tw: e.tensor_scalar(
                Ai(tmp_i, tw_ * 4)[:, 0:tw_], A(tmp_f, tw_), -1.49, None,
                ALU.max), waits=[tmin])
            tb = ve(lambda e, tw_=tw: e.tensor_copy(
                Ab(tmp_b, tw_ * 2)[:, 0:tw_], Ai(tmp_i, tw_ * 4)[:, 0:tw_]),
                waits=[ti8])
            lastq = dma(dst[rc * P:(rc + 1) * P, cc * tw:(cc + 1) * tw],
                        Ab(tmp_b, tw * 2)[:, 0:tw], waits=[tb])
        return lastq

    t_w1q = emit_ternarize(w1T, s_w1q, D, H, (HS[0], HS[1]), RW_B, RW_C,
                           RW_A, 0, t_g1w, 0)

    si_ev = {}
    t_hloc, t_Pst, t_scans, t_inp = {}, {}, [], {}
    HSmap = {0: HS[0], 1: HS[1], 2: HS[2], 3: HS[3], 4: HS[6], 5: HS[7],
             6: HS[4], 7: HS[5]}
    si_gate = {}

    def si_evac(do, tt, mm):
        w = [mm, t_w1q]
        if do > 0:
            w.append(si_gate[do - 1])
        tok = ac(lambda e, t_=tt: e.activation(
            out=A(XS[8], 2048)[:, t_ * 512:(t_ + 1) * 512],
            in_=pb[t_][:, :], func=AF.Silu), waits=w)
        si_ev[(do, tt)] = tok
        return tok

    def scan_chain(do):
        w = [si_ev[(do, tt)] for tt in range(NTT)] + [t_ci1]
        t2 = dma(s_si[do * P:(do + 1) * P, :], A(XS[8], 2048), waits=w)
        si_gate[do] = t2
        t_inp[do] = t2

    emit_projection(wiT, xn16_ap, t_x16[KC - 1], si_evac, RQ, pre=t_ci1,
                    interleave=scan_chain,
                    first_mm_waits=[f_ev[(KC - 1, tt)] for tt in range(NTT)])

    t_wi_done = TE.tok()
    for do in range(KC):
        tsl = dma(A(XS[8], 2048), s_si[do * P:(do + 1) * P, :],
                  waits=[t_inp[do], t_wi_done] +
                  ([t_hloc[do - 1]] if do else []))
        tu = ve(lambda e, d_=do: e.tensor_scalar(
            A(RW_B, 2048), A(XS[d_], 2048), -1.0, 1.0, ALU.mult, ALU.add),
            waits=[tsl, t_f[do]])
        ti2 = ve(lambda e, d_=do: e.tensor_tensor(
            A(HSmap[d_], 2048), A(XS[8], 2048), A(RW_B, 2048), ALU.mult),
            waits=[tu])
        t3 = ve(lambda e, d_=do: e.tensor_tensor_scan(
            A(HSmap[d_], 2048), A(XS[d_], 2048), A(HSmap[d_], 2048), 0.0,
            ALU.mult, ALU.add), waits=[ti2])
        t4 = ve(lambda e, d_=do: e.tensor_tensor_scan(
            A(RW_B, 2048), A(XS[d_], 2048), A(XS[d_], 2048), 1.0,
            ALU.mult, ALU.bypass), waits=[t3])
        t_Pst[do] = dma(s_P[do * P:(do + 1) * P, :], A(RW_B, 2048), waits=[t4])
        t5 = ve(lambda e, d_=do: e.tensor_copy(
            carry[:, d_:d_ + 1], A(RW_B, 2048)[:, 2047:2048]), waits=[t4])
        t6 = ve(lambda e, d_=do: e.tensor_copy(
            carry[:, 8 + d_:8 + d_ + 1], A(HSmap[d_], 2048)[:, 2047:2048]),
            waits=[t5])
        t_hloc[do] = t3
        t_scans.append(t6)

    t_car_i = dma(cc_car_i.ap(), carry[:, 0:16], waits=t_scans)
    t_car_g = collective([[0, 1, 2, 3], [4, 5, 6, 7]], cc_car_i, cc_car_o,
                         waits=[t_car_i])
    t_car_l = [dma(carry[:, 16 + 16 * q:32 + 16 * q],
                   cc_car_o.ap()[q * P:(q + 1) * P, :], waits=[t_car_g])
               for q in range(4)]
    tC = ve(lambda e: e.memset(carry[:, 80:96], 0.0), waits=t_car_l)
    for q in range(3):
        t1 = ve(lambda e, q_=q: e.tensor_tensor(
            carry[:, 88:96], carry[:, 16 + 16 * q_:24 + 16 * q_],
            carry[:, 80:88], ALU.mult), waits=[tC])
        t2 = ve(lambda e, q_=q: e.tensor_tensor(
            carry[:, 88:96], carry[:, 88:96],
            carry[:, 24 + 16 * q_:32 + 16 * q_], ALU.add), waits=[t1])
        tC = ve(lambda e, q_=q: e.select(
            carry[:, 80:88], cmu8[:, q_:q_ + 1].to_broadcast([P, 8]),
            carry[:, 88:96], carry[:, 80:88]), waits=[t2, t_cm])

    if debug:
        dbg_car = nc.dram_tensor("dbg_car", [P, 96], F32,
                                 kind="ExternalOutput").ap()
        t_dbgcar = dma(dbg_car, carry[:, 0:96], waits=[tC])
    t_hfix = {}
    prev = None
    for do in range(KC):
        tl = dma(A(XS[8], 2048), s_P[do * P:(do + 1) * P, :],
                 waits=[t_Pst[do], tC, prev])
        t_hfix[do] = ve(lambda e, d_=do: e.scalar_tensor_tensor(
            A(XS[d_], 2048), A(XS[8], 2048), carry[:, 80 + d_:81 + d_],
            A(HSmap[d_], 2048), ALU.mult, ALU.add), waits=[tl, tC])
        prev = t_hfix[do]

    t_dbghf = {}
    if debug:
        dbg_hf = nc.dram_tensor("dbg_hf", [D, TC], F32,
                                kind="ExternalOutput").ap()
        for k in range(KC):
            t_dbghf[k] = dma(dbg_hf[k * P:(k + 1) * P, :], A(XS[k], 2048),
                             waits=[t_hfix[k]])
    t_hfix_g = {k: t_hfix[k] for k in range(KC)}
    t_hfix_g[0] = t_g1w if t_g1w[1] > t_hfix[0][1] and t_g1w[0] is t_hfix[0][0] else t_hfix[0]
    mu3, m23 = emit_mean_stats(lambda k: A(XS[k], 2048),
                               [t_hfix[KC - 1]] * KC, XS[8])
    ev3a = ve(lambda e: e.tensor_copy(M0[0:96, :], pb[6][0:96, :]), waits=[m23])
    ev3b = ve(lambda e: e.tensor_copy(M1[0:32, :], pb[7][0:32, :]), waits=[m23])
    if debug:
        dbg_m2 = nc.dram_tensor("dbg_m2", [P, 1024], F32,
                                kind="ExternalOutput").ap()
        tdm = dma(dbg_m2[:, 0:512], M0[:, :], waits=[ev3a, ev3b])
        tdm2 = dma(dbg_m2[:, 512:1024], M1[:, :], waits=[ev3a, ev3b])
        ev3a = tdm
        ev3b = tdm2
    rr_done = []
    for (Mt, r) in ((M0, 96), (M1, 32)):
        t3 = ac(lambda e, v_=Mt, r_=r: e.activation(
            out=v_[0:r_, :], in_=v_[0:r_, :], func=AF.Ln,
            bias=eps_col[0:r_, :]), waits=[ev3a, ev3b, t_aux])
        t4 = ac(lambda e, v_=Mt, r_=r: e.activation(
            out=v_[0:r_, :], in_=v_[0:r_, :], func=AF.Exp, scale=-0.5),
            waits=[t3])
        rr_done.append(t4)

    def lax_dance(n_val, m_slot, hw_slot):
        t0 = ve(lambda e: e.memset(scr[0:1, 30:31], float(n_val)))
        t1 = ve(lambda e: e.tensor_scalar(
            scr[0:1, 31:32], scr[0:1, 30:31], 1.0, None, ALU.subtract),
            waits=[t0])
        t2 = ve(lambda e: e.tensor_tensor(
            scr[0:1, 32:33], q995_c, scr[0:1, 31:32], ALU.mult),
            waits=[t1, t_aux])
        ti = ve(lambda e: e.tensor_copy(scrI[0:1, 0:1], scr[0:1, 32:33]),
                waits=[t2])
        tf = ve(lambda e: e.tensor_copy(scr[0:1, 33:34], scrI[0:1, 0:1]),
                waits=[ti])
        tg = ve(lambda e: e.tensor_scalar(
            scr[0:1, 35:36], scr[0:1, 33:34], scr[0:1, 32:33], None,
            ALU.is_gt), waits=[tf])
        tlow = ve(lambda e: e.tensor_tensor(
            scr[0:1, 33:34], scr[0:1, 33:34], scr[0:1, 35:36], ALU.subtract),
            waits=[tg])
        thw = ve(lambda e, h_=hw_slot: e.tensor_tensor(
            scr[0:1, h_:h_ + 1], scr[0:1, 32:33], scr[0:1, 33:34],
            ALU.subtract), waits=[tlow])
        tm = ve(lambda e: e.tensor_tensor(
            scr[0:1, 36:37], scr[0:1, 30:31], scr[0:1, 33:34], ALU.subtract),
            waits=[thw])
        return ve(lambda e, m_=m_slot: e.tensor_scalar(
            scr[0:1, m_:m_ + 1], scr[0:1, 36:37], 1.0, None, ALU.subtract),
            waits=[tm])

    nsel = [0]

    def fat_bisect(segs, m_slot, gather_toks, junk_off, iters=FAT_IT):
        base = 32768 + nsel[0] * 1024
        nsel[0] += 1
        assert base + 1024 <= 65536
        segs_ap = [A(o, c) for (o, c) in segs]
        w0 = list(gather_toks)
        tmax = ve(lambda e, s_=segs_ap[0]: e.tensor_reduce(
            scrW[:, 2:3], s_, AXX, ALU.max), waits=w0)
        for s_ in segs_ap[1:]:
            tr = ve(lambda e, ss_=s_: e.tensor_reduce(
                scrW[:, 3:4], ss_, AXX, ALU.max), waits=[tmax])
            tmax = ve(lambda e: e.tensor_tensor(
                scrW[:, 2:3], scrW[:, 2:3], scrW[:, 3:4], ALU.max), waits=[tr])
        tst = dma(s_bnc[0:1, base + 128:base + 256].rearrange("o p -> p o"),
                  scrW[:, 2:3], waits=[tmax])
        tld = dma(brow[0:1, 0:128], s_bnc[0:1, base + 128:base + 256],
                  waits=[tst])
        thi = ve(lambda e: e.tensor_reduce(scr[0:1, 50:51], brow[0:1, 0:128],
                                           AXX, ALU.max), waits=[tld])
        t0 = ve(lambda e: e.memset(scr[0:1, 51:52], 0.0), waits=[thi])
        t0 = ve(lambda e: e.tensor_copy(scr[0:1, 52:53], scr[0:1, 50:51]),
                waits=[t0])
        t0 = ve(lambda e, m_=m_slot: e.tensor_scalar(
            scr[0:1, 55:56], scr[0:1, m_:m_ + 1], 1.0, None, ALU.add),
            waits=[t0])
        tcnt = t0
        for it in range(iters):
            tcnt = ve(lambda e: e.tensor_scalar(
                scr[0:1, 53:54], scr[0:1, 51:52], scr[0:1, 52:53], 0.5,
                ALU.add, ALU.mult), waits=[tcnt])
            tb1 = dma(s_bnc[0:1, base:base + 1], scr[0:1, 53:54], waits=[tcnt])
            tb2 = dma(scrW[:, 4:5],
                      s_bnc[0:1, base:base + 1].to_broadcast([P, 1]),
                      waits=[tb1])
            tacc = ve(lambda e: e.memset(scrW[:, 5:6], 0.0), waits=[tb2])
            for si, s_ in enumerate(segs_ap):
                tpc = ve(lambda e, ss_=s_, jo_=junk_off: e.tensor_scalar(
                    Ai(jo_, ss_.shape[1]), ss_, scrW[:, 4:5], 0.0,
                    ALU.is_gt, ALU.add, accum_out=scrW[:, 6:7]),
                    waits=[tacc])
                tacc = ve(lambda e: e.tensor_tensor(
                    scrW[:, 5:6], scrW[:, 5:6], scrW[:, 6:7], ALU.add),
                    waits=[tpc])
            tb3 = dma(s_bnc[0:1, base + 384:base + 512]
                      .rearrange("o p -> p o"), scrW[:, 5:6], waits=[tacc])
            tb4 = dma(brow[0:1, 128:256],
                      s_bnc[0:1, base + 384:base + 512], waits=[tb3])
            tc = ve(lambda e: e.tensor_reduce(
                scr[0:1, 54:55], brow[0:1, 128:256], AXX, ALU.add),
                waits=[tb4])
            tg = ve(lambda e: e.tensor_scalar(
                scrU[0:1, 0:1], scr[0:1, 54:55], scr[0:1, 55:56], None,
                ALU.is_ge), waits=[tc])
            tl_ = ve(lambda e: e.select(scr[0:1, 51:52], scrU[0:1, 0:1],
                                        scr[0:1, 53:54], scr[0:1, 51:52]),
                     waits=[tg])
            tcnt = ve(lambda e: e.select(scr[0:1, 52:53], scrU[0:1, 0:1],
                                         scr[0:1, 52:53], scr[0:1, 53:54]),
                      waits=[tl_])
        return tcnt

    def band_setup(eps, tA_col_idx, tB_col_idx, w0):
        base = 32768 + nsel[0] * 1024
        nsel[0] += 1
        t1 = ve(lambda e: e.tensor_scalar(
            scr[0:1, 19:20], scr[0:1, 52:53], 1.0 - eps, None, ALU.mult),
            waits=w0)
        t2 = ve(lambda e: e.tensor_scalar(
            scr[0:1, 20:21], scr[0:1, 52:53], 1.0 + eps, None, ALU.mult),
            waits=[t1])
        tb1 = dma(s_bnc[0:1, base:base + 2], scr[0:1, 19:21], waits=[t2])
        ta_ = dma(scrW[:, tA_col_idx:tA_col_idx + 1],
                  s_bnc[0:1, base:base + 1].to_broadcast([P, 1]), waits=[tb1])
        tb_ = dma(scrW[:, tB_col_idx:tB_col_idx + 1],
                  s_bnc[0:1, base + 1:base + 2].to_broadcast([P, 1]),
                  waits=[tb1])
        return ta_, tb_

    def window_select(gath_segs, cnt_cols_ap, m_slot, out_slot, w0,
                      work_off):
        base = 32768 + nsel[0] * 1024
        nsel[0] += 1
        t1 = ve(lambda e: e.tensor_reduce(scrW[:, 5:6], cnt_cols_ap, AXX,
                                          ALU.add), waits=w0)
        tb3 = dma(s_bnc[0:1, base + 384:base + 512].rearrange("o p -> p o"),
                  scrW[:, 5:6], waits=[t1])
        tb4 = dma(brow[0:1, 128:256], s_bnc[0:1, base + 384:base + 512],
                  waits=[tb3])
        tcb = ve(lambda e: e.tensor_reduce(scr[0:1, 21:22],
                                           brow[0:1, 128:256], AXX, ALU.add),
                 waits=[tb4])
        tex = ve(lambda e: e.memset(A(work_off, 1024), BIGNEG), waits=w0)
        exo = [0]
        for (o, c) in gath_segs:
            for w8 in range(c // 64):
                d_ = exo[0]
                tex = ve(lambda e, o_=o, w_=w8, dd_=d_, c_=c: e.max(
                    out=A(work_off, 1024)[:, dd_:dd_ + 8],
                    in_=A(o_, c_)[:, w_ * 64:(w_ + 1) * 64]), waits=[tex])
                exo[0] += 8
        assert exo[0] <= 512
        t2x = ve(lambda e: e.memset(ex2[:], BIGNEG), waits=[tex])
        n2 = exo[0] // 16
        for w8 in range(n2):
            t2x = ve(lambda e, w_=w8: e.max(
                out=ex2[:, w_ * 8:(w_ + 1) * 8],
                in_=A(work_off, 1024)[:, w_ * 16:(w_ + 1) * 16]), waits=[t2x])
        tcs = dma(s_cons[:, 0:256], ex2[:], waits=[t2x])
        tcl = dma(cons[0:32, 0:1024],
                  s_cons.rearrange("(b a) c -> b (a c)", b=32), waits=[tcs])
        t0 = ve(lambda e: e.tensor_copy(scr[0:1, 57:58], scr[0:1, 19:20]),
                waits=[tcl, tcb])
        t0 = ve(lambda e: e.tensor_copy(scr[0:1, 58:59], scr[0:1, 20:21]),
                waits=[t0])
        t0 = ve(lambda e, m_=m_slot: e.tensor_scalar(
            scr[0:1, 55:56], scr[0:1, m_:m_ + 1], 1.0, None, ALU.add),
            waits=[t0])
        tloc = t0
        for it in range(LOC_IT):
            tloc = ve(lambda e: e.tensor_scalar(
                scr[0:1, 59:60], scr[0:1, 57:58], scr[0:1, 58:59], 0.5,
                ALU.add, ALU.mult), waits=[tloc])
            tr1 = ve(lambda e: e.tensor_copy(
                stg[0:1, 0:32], scr[0:1, 59:60].to_broadcast([1, 32])),
                waits=[tloc])
            tr2 = ve(lambda e: e.transpose(stg[0:32, 32:64], stg[0:32, 0:32]),
                     waits=[tr1])
            tpc = ve(lambda e: e.tensor_scalar(
                cons[0:32, 1024:2048], cons[0:32, 0:1024], stg[0:32, 32:33],
                0.0, ALU.is_gt, ALU.add, accum_out=stg[0:32, 33:34]),
                waits=[tr2])
            tr3 = ve(lambda e: e.tensor_copy(
                stg[0:32, 0:32], stg[0:32, 33:34].to_broadcast([32, 32])),
                waits=[tpc])
            tr4 = ve(lambda e: e.transpose(stg[0:32, 32:64], stg[0:32, 0:32]),
                     waits=[tr3])
            tr5 = ve(lambda e: e.tensor_reduce(
                scr[0:1, 60:61], stg[0:1, 32:64], AXX, ALU.add), waits=[tr4])
            tr6 = ve(lambda e: e.tensor_scalar(
                scr[0:1, 60:61], scr[0:1, 60:61], scr[0:1, 21:22], None,
                ALU.add), waits=[tr5])
            tg = ve(lambda e: e.tensor_scalar(
                scrU[0:1, 1:2], scr[0:1, 60:61], scr[0:1, 55:56], None,
                ALU.is_ge), waits=[tr6])
            tl_ = ve(lambda e: e.select(scr[0:1, 57:58], scrU[0:1, 1:2],
                                        scr[0:1, 59:60], scr[0:1, 57:58]),
                     waits=[tg])
            tloc = ve(lambda e: e.select(scr[0:1, 58:59], scrU[0:1, 1:2],
                                         scr[0:1, 58:59], scr[0:1, 59:60]),
                      waits=[tl_])
        tr1 = ve(lambda e: e.tensor_copy(
            stg[0:1, 0:32], scr[0:1, 58:59].to_broadcast([1, 32])),
            waits=[tloc])
        tr2 = ve(lambda e: e.transpose(stg[0:32, 32:64], stg[0:32, 0:32]),
                 waits=[tr1])
        tm1 = ve(lambda e: e.tensor_scalar(
            cons[0:32, 1024:2048], cons[0:32, 0:1024], stg[0:32, 32:33],
            -BIGOFF, ALU.is_gt, ALU.mult), waits=[tr2])
        tm2 = ve(lambda e: e.tensor_tensor(
            cons[0:32, 1024:2048], cons[0:32, 1024:2048], cons[0:32, 0:1024],
            ALU.add), waits=[tm1])
        tm3 = ve(lambda e: e.tensor_reduce(
            stg[0:32, 34:35], cons[0:32, 1024:2048], AXX, ALU.max),
            waits=[tm2])
        tm4 = ve(lambda e: e.tensor_copy(
            stg[0:32, 0:32], stg[0:32, 34:35].to_broadcast([32, 32])),
            waits=[tm3])
        tm5 = ve(lambda e: e.transpose(stg[0:32, 32:64], stg[0:32, 0:32]),
                 waits=[tm4])
        return ve(lambda e, o_=out_slot: e.tensor_reduce(
            scr[0:1, o_:o_ + 1], stg[0:1, 32:64], AXX, ALU.max), waits=[tm5])

    def band_correction(stream_src, nchunks, g_win, mask_i8_off, abs_off,
                        stream_off, exw, cc_in, cc_out, gseg_off, work_off,
                        tA_ci, tB_ci, pre_toks, cc_waits):
        tz = ve(lambda e: e.memset(scrW[:, 15:16], 0.0), waits=pre_toks)
        tz2 = ve(lambda e: e.memset(cons[:, :], BIGNEG), waits=pre_toks)
        prev = None
        tmx = tz2
        for k in range(nchunks):
            w = [tz, tz2] + list(pre_toks)
            if prev is not None:
                w.append(prev)
            tl = dma(A(stream_off, 2048),
                     stream_src[k * P:(k + 1) * P, :], waits=w)
            tab = ac(lambda e: e.activation(
                out=A(abs_off, 2048), in_=A(stream_off, 2048), func=AF.Abs),
                waits=[tl])
            tcm = ve(lambda e: e.tensor_scalar(
                Ai(mask_i8_off, 2048), A(abs_off, 2048),
                scrW[:, tB_ci:tB_ci + 1], 0.0, ALU.is_gt, ALU.add,
                accum_out=scrW[:, 6:7]), waits=[tab, tmx])
            tca = ve(lambda e: e.tensor_tensor(
                scrW[:, 15:16], scrW[:, 15:16], scrW[:, 6:7], ALU.add),
                waits=[tcm])
            ty = ve(lambda e: e.scalar_tensor_tensor(
                A(stream_off, 2048), Ai(mask_i8_off, 2048), -BIGOFF,
                A(abs_off, 2048), ALU.mult, ALU.add), waits=[tca])
            tmx = ty
            nw = 2048 // g_win
            for w8 in range(nw):
                tmx = ve(lambda e, k_=k, w_=w8, nw_=nw: e.max(
                    out=cons[:, (k_ * nw_ + w_) * 8:(k_ * nw_ + w_ + 1) * 8],
                    in_=A(stream_off, 2048)[:, w_ * g_win:(w_ + 1) * g_win]),
                    waits=[tmx])
            prev = tmx
        used = nchunks * (2048 // g_win) * 8
        tcp = tmx
        tzc = ve(lambda e: e.memset(A(exw, 512), BIGNEG), waits=[tcp])
        tcp = tzc
        for w8 in range(used // 16):
            tcp = ve(lambda e, w_=w8: e.max(
                out=A(exw, 512)[:, w_ * 8:(w_ + 1) * 8],
                in_=cons[:, w_ * 16:(w_ + 1) * 16]), waits=[tcp])
        t1 = dma(cc_in.ap()[:, 0:512], A(exw, 512), waits=[tcp])
        t2 = dma(cc_in.ap()[:, 512:513], scrW[:, 15:16], waits=[tcp],
                 slow=True)
        tg = collective([list(range(NCORE))], cc_in, cc_out,
                        waits=[t1, t2] + list(cc_waits))
        tok_loads = []
        for r in range(NCORE):
            tok_loads.append(dma(A(gseg_off + r * 512, 512),
                                 cc_out.ap()[r * P:(r + 1) * P, 0:512],
                                 waits=[tg]))
            tok_loads.append(dma(m8s[:, r:r + 1],
                                 cc_out.ap()[r * P:(r + 1) * P, 512:513],
                                 waits=[tg], slow=True))
        gsegs = [(gseg_off + r * 512, 512) for r in range(NCORE)]
        return gsegs, m8s[:, 0:8], tok_loads

    def scales_from_v(v_slot, g_slot, s_col_idx, gs_col_idx, w0):
        t1 = ve(lambda e, v_=v_slot: e.tensor_scalar(
            scr[0:1, 46:47], scr[0:1, v_:v_ + 1], EPS32, None, ALU.max),
            waits=w0)
        t2 = ve(lambda e: e.reciprocal(scr[0:1, 47:48], scr[0:1, 46:47]),
                waits=[t1])
        t3 = ve(lambda e: e.tensor_scalar(
            scr[0:1, 48:49], scr[0:1, 47:48], 127.0, None, ALU.mult),
            waits=[t2])
        t4 = ve(lambda e, g_=g_slot: e.tensor_tensor(
            scr[0:1, 49:50], scr[0:1, 46:47], scr[0:1, g_:g_ + 1], ALU.mult),
            waits=[t3])
        t5 = ve(lambda e: e.tensor_scalar(
            scr[0:1, 49:50], scr[0:1, 49:50], 1.0 / 127.0, None, ALU.mult),
            waits=[t4])
        tb = bcast_rows([scr[0:1, 48:49]], scrW[:, s_col_idx:s_col_idx + 1],
                        [t5], cols=1)
        return bcast_rows([scr[0:1, 49:50]],
                          scrW[:, gs_col_idx:gs_col_idx + 1], [tb], cols=1)

    t_g1 = [dma(A(RW_A + r * 512, 512), cc_q1_o.ap()[r * P:(r + 1) * P, :],
                waits=[t_gc1, t_h0[KC - 1], t_m8, t_w1q])
            for r in range(NCORE)]
    t_m1d = lax_dance(2 ** 24, 10, 11)
    t_fb1 = fat_bisect(q1segs, 10, t_g1 + [t_m1d], RW_C)
    tA1, tB1 = band_setup(0.008, 13, 14, [t_fb1])
    g1segs, cnt1_ap, t_bc1 = band_correction(
        s_h0, KC, 128, RW_C, RW_B, XS[8], RW_B + 1024, cc_w1_i, cc_w1_o,
        RW_A, RW_B, 13, 14,
        [tA1, tB1, m23, t_hfix[KC - 1], t_fb1], [t_car_g])
    t_q1v = window_select(g1segs, cnt1_ap, 10, 12, t_bc1 + [t_fb1],
                          RW_B + 2048 if False else RW_C)
    t_s1 = scales_from_v(12, 0, 9, 10, [t_q1v, t_g1w])

    rr_rep = A(RW_B, 2048)
    t_rr = bcast_rows([st_m2(tt) for tt in range(NTT)], rr_rep,
                      rr_done + [t_q1v, t_hfix[KC - 1]])
    if debug:
        dbg_rr = nc.dram_tensor("dbg_rr", [P, 2048], F32,
                                kind="ExternalOutput").ap()
        t_rr = dma(dbg_rr, rr_rep, waits=[t_rr])
    t_ha = {}
    for do in range(KC):
        t1 = ve(lambda e, d_=do: e.tensor_tensor(
            A(XS[d_], 2048), A(XS[d_], 2048), rr_rep, ALU.mult),
            waits=[t_rr, t_hfix[do], t_dbghf.get(do)])
        t_ha[do] = ve(lambda e, d_=do: e.tensor_scalar(
            A(XS[d_], 2048), A(XS[d_], 2048), normw_c(d_), None, ALU.mult),
            waits=[t1, t_aux])
    t_ha_all = [t_ha[d] for d in range(KC)]
    if debug:
        dbg_ha = nc.dram_tensor("dbg_ha", [D, TC], F32,
                                 kind="ExternalOutput").ap()
        t_dbgha = [dma(dbg_ha[k * P:(k + 1) * P, :], A(XS[k], 2048),
                       waits=[t_ha[k]]) for k in range(KC)]
        t_ha_all = t_ha_all + t_dbgha

    t_xq = {}
    for k in range(KC):
        w = [t_s1, t_hfix[KC - 1]]
        if k:
            w.append(t_xq[k - 1])
        tl = dma(A(XS[8], 2048), s_h0[k * P:(k + 1) * P, :], waits=w)
        ti = ve(lambda e: e.tensor_scalar(
            Ai(RW_C, 2048), A(XS[8], 2048), scrW[:, 9:10], None, ALU.mult),
            waits=[tl, t_s1])
        t_xq[k] = ve(lambda e, k_=k: e.tensor_copy(
            Ab(HS[0 + k_ // 2] + (k_ % 2) * 1024, 2048), Ai(RW_C, 2048)),
            waits=[ti])

    def xq_ap(k, tt):
        return Ab(HS[0 + k // 2] + (k % 2) * 1024, 2048)[:, tt * 512:(tt + 1) * 512]

    if debug:
        dbg_xq1 = nc.dram_tensor("dbg_xq1", [D, TC], BF16,
                                 kind="ExternalOutput").ap()
        t_dbgxq = [dma(dbg_xq1[k * P:(k + 1) * P, :],
                       Ab(HS[0 + k // 2] + (k % 2) * 1024, 2048),
                       waits=[t_xq[k]]) for k in range(KC)]
        t_xq[KC - 1] = t_dbgxq[-1]

    red1q2 = A(HS[4], 4096)
    contq2 = A(HS[6], 1024)
    t_mm1_ev, t_hg, mm1_last = {}, {}, {}
    t_red2 = [None]
    for h in range(HC):
        tl = dma(Ab(RW_B + (h % 2) * 512, 1024).rearrange(
                     "p (k j) -> p k j", j=P),
                 s_w1q[:, h * P:(h + 1) * P].rearrange("(k p) j -> p k j",
                                                       p=P),
                 waits=[t_w1q, t_rr, t_ha_all[-1]] +
                 ([mm1_last[h - 2]] if h >= 2 else []))
        for k in range(KC):
            for tt in range(NTT):
                w = [tl, t_xq[KC - 1]]
                if k == 0 and h > 0:
                    w.append(t_mm1_ev[(h - 1, tt)])
                mm = te(lambda e, k_=k, t_=tt, h_=h: e.matmul(
                    pb[t_][:, :],
                    Ab(RW_B + (h_ % 2) * 512, 1024)[:, k_ * P:(k_ + 1) * P],
                    xq_ap(k_, t_), start=(k_ == 0), stop=(k_ == KC - 1)),
                    waits=w)
                if k == KC - 1:
                    t_mm1_ev[(h, tt)] = ac(
                        lambda e, t_=tt: e.activation(
                            out=A(XS[8], 2048)[:, t_ * 512:(t_ + 1) * 512],
                            in_=pb[t_][:, :], func=AF.Gelu,
                            scale=scrW[:, 10:11]),
                        waits=[mm, t_s1] +
                        ([t_hg[h - 1], t_red2[0]] if (tt == 0 and h > 0)
                         else []))
        mm1_last[h] = TE.tok()
        tevs = [t_mm1_ev[(h, tt)] for tt in range(NTT)]
        t_hg[h] = dma(s_hg[h * P:(h + 1) * P, :], A(XS[8], 2048), waits=tevs)
        ta = ac(lambda e: e.activation(
            out=A(RW_A, 2048), in_=A(XS[8], 2048), func=AF.Abs),
            waits=tevs + [t_red2[0], t_q1v, t_xq[KC - 1]])
        for w8 in range(16):
            ta = ve(lambda e, h_=h, w_=w8: e.max(
                out=red1q2[:, h_ * P + w_ * 8:h_ * P + (w_ + 1) * 8],
                in_=A(RW_A, 2048)[:, w_ * 128:(w_ + 1) * 128]), waits=[ta])
        t_red2[0] = ta

    t_g2w = emit_gamma(w2T, H, D, (HS[6], HS[7]), 1, pre=t_xq[KC - 1],
                       first_wait=[ev3a, ev3b])
    t_w2q = emit_ternarize(w2T, s_w2q, H, D, (HS[6], HS[7]), RW_A + 2048,
                           RW_A + 3072, RW_A + 3328, 1, t_g2w, 1)

    t_c2 = (sems["dve"], max(t_red2[0][1], t_w2q[1]) if False else t_red2[0][1])
    t_c2 = t_red2[0]
    first = True
    for w8 in range(64):
        w = [t_c2] + ([t_w2q] if first else [])
        first = False
        t_c2 = ve(lambda e, w_=w8: e.max(
            out=contq2[:, w_ * 16:w_ * 16 + 8],
            in_=red1q2[:, w_ * 64:(w_ + 1) * 64]), waits=w)
        t_c2 = ve(lambda e, w_=w8: e.match_replace(
            out=red1q2[:, w_ * 64:(w_ + 1) * 64],
            in_to_replace=contq2[:, w_ * 16:w_ * 16 + 8],
            in_values=red1q2[:, w_ * 64:(w_ + 1) * 64],
            imm_value=BIGNEG), waits=[t_c2])
        t_c2 = ve(lambda e, w_=w8: e.max(
            out=contq2[:, w_ * 16 + 8:w_ * 16 + 16],
            in_=red1q2[:, w_ * 64:(w_ + 1) * 64]), waits=[t_c2])

    t_ci2 = dma(cc_q2_i.ap(), contq2, waits=[t_c2])
    t_gc2 = collective([list(range(NCORE))], cc_q2_i, cc_q2_o, waits=[t_ci2])
    q2segs = [(HS[2] + r * 1024, 1024) for r in range(NCORE)]
    t_g2 = [dma(A(HS[2] + r * 1024, 1024),
                cc_q2_o.ap()[r * P:(r + 1) * P, :],
                waits=[t_gc2, t_w2q, t_c2]) for r in range(NCORE)]

    t_m2d = lax_dance(2 ** 26, 13, 14)
    t_fb2 = fat_bisect(q2segs, 13, t_g2 + [t_m2d, t_ci2], HS[6])
    tA2, tB2 = band_setup(0.003, 13, 14, [t_fb2])
    g2segs, cnt2_ap, t_bc2 = band_correction(
        s_hg, HC, 512, HS[6], HS[7], XS[8], HS[7] + 1024, cc_w2_i, cc_w2_o,
        HS[2], HS[4], 13, 14,
        [tA2, tB2, t_fb2, t_hg[HC - 1]], [t_gc2])
    t_q2v = window_select(g2segs, cnt2_ap, 13, 15, t_bc2 + [t_fb2], HS[4])
    t_s2 = scales_from_v(15, 1, 11, 12, [t_q2v, t_g2w])

    xn16_ld2 = {}
    for k in range(KC):
        xn16_ld2[k] = dma(Ab(HS[0 + k // 2] + (k % 2) * 1024, 2048),
                          s_xn16[k * P:(k + 1) * P, :],
                          waits=[mm1_last[HC - 1], t_xq[KC - 1]] +
                          [t_mm1_ev[(HC - 1, tt)] for tt in range(NTT)])
    t_x16b = xn16_ld2[KC - 1]

    t_o = {}

    def g_evac2(do, tt, mm):
        t1 = ac(lambda e, t_=tt: e.activation(
            out=A(RW_C, 512), in_=pb[t_][:, :], func=AF.Silu),
            waits=[mm, t_q2v] + ([t_o[(do, tt - 1)]] if tt else
                                 ([t_o[(do - 1, NTT - 1)]] if do else [])))
        t2 = ve(lambda e, d_=do, t_=tt: e.tensor_tensor(
            A(RW_B + 1024, 512), A(RW_C, 512),
            A(XS[d_], 2048)[:, t_ * 512:(t_ + 1) * 512], ALU.mult),
            waits=[t1, t_ha[do]])
        t3 = ve(lambda e, d_=do, t_=tt: e.tensor_copy(
            Ab(XS[d_], 2048)[:, t_ * 512:(t_ + 1) * 512],
            A(RW_B + 1024, 512)), waits=[t2])
        t_o[(do, tt)] = t3
        return t3

    emit_projection(wgT, lambda k, tt: Ab(HS[0 + k // 2] + (k % 2) * 1024, 2048)
                    [:, tt * 512:(tt + 1) * 512],
                    t_x16b, g_evac2, RQ, pre=t_x16b)
    t_o_all = [t_o[(d, t)] for d in range(KC) for t in range(NTT)]
    t_o_last = t_o_all[-1]

    def o_ap(k, tt):
        return Ab(XS[k], 2048)[:, tt * 512:(tt + 1) * 512]

    t_sat = {}

    def wo_evac(do, tt, mm):
        t1 = ac(lambda e, t_=tt: e.activation(
            out=A(RW_C, 512), in_=pb[t_][:, :], func=AF.Copy),
            waits=[mm] + ([t_sat[(do, tt - 1)]] if tt else
                          ([t_sat[(do - 1, NTT - 1)]] if do else [])))
        tok = dma(s_sat[do * P:(do + 1) * P, tt * 512:(tt + 1) * 512],
                  A(RW_C, 512), waits=[t1])
        t_sat[(do, tt)] = tok
        return tok

    emit_projection(woT, o_ap, t_o_last, wo_evac, RQ, pre=t_o_last)
    t_sat_all = [t_sat[(d, t)] for d in range(KC) for t in range(NTT)]

    def hq_ap(h, tt):
        return Ab(XS[0 + h // 8] + (h % 8) * 256, 512)

    t_out_st = []
    t_hq = {}
    t_ti = {}
    t_c1 = {}
    t_mm2ev = {}
    w2strips = [XS[8], HS[0]]
    t_w2s = {}
    t_comb = {}
    for tt in range(NTT):
        for h in range(HC):
            w = [t_s2, t_hg[h], t_sat_all[-1]]
            if tt > 0:
                w.append(t_mm2ev[(tt - 1, KC - 1)])
            prev_ti = t_ti.get((h - 2, tt))
            if prev_ti is not None:
                w.append(prev_ti)
            hqstg = RW_A + (h % 2) * 512
            tl = dma(A(hqstg, 512),
                     s_hg[h * P:(h + 1) * P, tt * 512:(tt + 1) * 512],
                     waits=w)
            ti = ve(lambda e, s_=hqstg: e.tensor_scalar(
                Ai(RW_B, 512), A(s_, 512), scrW[:, 11:12], None, ALU.mult),
                waits=[tl, t_s2])
            t_ti[(h, tt)] = ti
            t_hq[(h, tt)] = ve(lambda e, h_=h, t_=tt: e.tensor_copy(
                hq_ap(h_, t_), Ai(RW_B, 512)), waits=[ti])
        hq_done = t_hq[(HC - 1, tt)]
        for dd in range(KC):
            sidx = dd % 2
            w = [t_w2q, hq_done]
            if (tt, dd - 2) in t_mm2ev:
                w.append(t_mm2ev[(tt, dd - 2)])
            if tt > 0 and dd < 2:
                w.append(t_mm2ev[(tt - 1, KC - 2 + dd)])
            if tt == 0 and dd < 2:
                w.append(t_sat_all[-1])
                w.append(t_xq[KC - 1])
            tl2 = dma(Ab(w2strips[sidx], 4096).rearrange(
                          "p (k j) -> p k j", j=P),
                      s_w2q[:, dd * P:(dd + 1) * P].rearrange(
                          "(k p) j -> p k j", p=P), waits=w)
            bank = dd % 2
            for hk in range(HC):
                mm = te(lambda e, hk_=hk, b_=bank, s_=sidx, t_=tt: e.matmul(
                    pb[b_][:, :],
                    Ab(w2strips[s_], 4096)[:, hk_ * P:(hk_ + 1) * P],
                    hq_ap(hk_, t_), start=(hk_ == 0), stop=(hk_ == HC - 1)),
                    waits=[tl2, hq_done] +
                    ([t_mm2ev[(tt, dd - 2)]] if (tt, dd - 2) in t_mm2ev
                     else []), chain=True)
            wg_ = [mm, t_s2]
            prev_c1 = t_c1.get((tt, dd - 1)) or (
                t_c1.get((tt - 1, KC - 1)) if dd == 0 else None)
            if prev_c1 is not None:
                wg_.append(prev_c1)
            tg_ = ac(lambda e, b_=bank: e.activation(
                out=A(RW_C, 512), in_=pb[b_][:, :], func=AF.Silu,
                scale=scrW[:, 12:13]), waits=wg_)
            t_mm2ev[(tt, dd)] = tg_
            prev_cmb = t_comb.get((tt, dd - 1)) or \
                t_comb.get((tt - 1, KC - 1)) or t_sat_all[-1]
            txl = dma(A(HS[1], 512), xs[dd * P:(dd + 1) * P,
                                        tt * 512:(tt + 1) * 512],
                      waits=[prev_cmb])
            tsl = dma(A(HS[1] + 512, 512),
                      s_sat[dd * P:(dd + 1) * P, tt * 512:(tt + 1) * 512],
                      waits=[t_sat[(dd, tt)], prev_cmb])
            tc1 = ve(lambda e: e.tensor_tensor(
                A(HS[1] + 1024, 512), A(RW_C, 512), A(HS[1] + 512, 512),
                ALU.mult), waits=[tg_, tsl] +
                ([t_out_st[-1]] if t_out_st else []))
            t_c1[(tt, dd)] = tc1
            tc2 = ve(lambda e: e.tensor_tensor(
                A(HS[1] + 1024, 512), A(HS[1] + 1024, 512), A(HS[1], 512),
                ALU.add), waits=[tc1, txl])
            t_comb[(tt, dd)] = tc2
            t_out_st.append(dma(out[dd * P:(dd + 1) * P,
                                    tt * 512:(tt + 1) * 512],
                                A(HS[1] + 1024, 512), waits=[tc2]))

    if debug:
        dbg_xq2 = nc.dram_tensor("dbg_xq2", [D, TC], BF16, kind="ExternalOutput").ap()
        for k in range(KC):
            off_ = 18432 + (k // 2) * 2048 + (k % 2) * 1024
            dma(dbg_xq2[k * P:(k + 1) * P, :],
                arena[:, off_:off_ + 1024].bitcast(BF16),
                waits=[mm1_last[HC - 1], t_hg[HC - 1]])
        dbgc1 = nc.dram_tensor("dbgc1", [P, 512], F32, kind="ExternalOutput").ap()
        t_out_st.append(dma(dbgc1, contq1, waits=[t_q1v]))
        dbgc2 = nc.dram_tensor("dbgc2", [P, 1024], F32, kind="ExternalOutput").ap()
        t_out_st.append(dma(dbgc2, contq2, waits=[t_q2v]))
        dbg = nc.dram_tensor("dbg", [2, 64], F32, kind="ExternalOutput").ap()
        t_out_st.append(dma(dbg[0:1, :], scr[0:1, :], waits=[t_comb[(NTT - 1, KC - 1)]]))
        t_out_st.append(dma(dbg[1:2, 0:16], scrW[0:1, 0:16], waits=[t_comb[(NTT - 1, KC - 1)]]))
    def fin(e):
        e.wait_ge(sems["dma"], t_out_st[-1][1])
    GP.ops.append(fin)

    with nc.Block() as block:
        @block.sync
        def _(sync):
            for op in SY.ops:
                op(sync)

        @block.tensor
        def _(eng):
            for op in TE.ops:
                op(eng)

        @block.scalar
        def _(eng):
            for op in SC_.ops:
                op(eng)

        @block.vector
        def _(eng):
            for op in VE.ops:
                op(eng)

        @block.gpsimd
        def _(eng):
            for op in GP.ops:
                op(eng)

    ctx.close()
    return nc


_NC_CACHE = [None]


def kernel(L0_attn_x, L0_attn_Wi, L0_attn_Wf, L0_attn_Wg, L0_attn_Wo,
           L0_attn_norm_w, L0_ffn_ln_g, L0_ffn_ln_b, L0_ffn_w1, L0_ffn_w2):
    x = np.asarray(L0_attn_x, dtype=np.float32)
    wi = np.ascontiguousarray(np.asarray(L0_attn_Wi, np.float32).T)
    wf = np.ascontiguousarray(np.asarray(L0_attn_Wf, np.float32).T)
    wg = np.ascontiguousarray(np.asarray(L0_attn_Wg, np.float32).T)
    wo = np.ascontiguousarray(np.asarray(L0_attn_Wo, np.float32).T)
    w1 = np.ascontiguousarray(np.asarray(L0_ffn_w1, np.float32).T)
    w2 = np.ascontiguousarray(np.asarray(L0_ffn_w2, np.float32).T)
    nw = np.asarray(L0_attn_norm_w, np.float32)
    lg = np.asarray(L0_ffn_ln_g, np.float32)
    lb = np.asarray(L0_ffn_ln_b, np.float32)

    in_maps = []
    for c in range(NCORE):
        b, r = c // 4, c % 4
        xsl = np.ascontiguousarray(x[b, r * TC:(r + 1) * TC, :].T)
        auxm = np.zeros((P, 32), np.float32)
        auxm[:, 0:8] = nw.reshape(KC, P).T
        auxm[:, 8:16] = lg.reshape(KC, P).T
        auxm[:, 16:24] = lb.reshape(KC, P).T
        for q in range(3):
            auxm[:, 24 + q] = 1.0 if q < r else 0.0
        auxm[:, 27] = np.float32(1e-5)
        auxm[:, 28] = np.float32(0.995)
        in_maps.append({"x": xsl, "wiT": wi, "wfT": wf, "wgT": wg, "woT": wo,
                        "w1T": w1, "w2T": w2, "aux": auxm})

    if _NC_CACHE[0] is None:
        _NC_CACHE[0] = build_program()
    res = run_bass_kernel_spmd(_NC_CACHE[0], in_maps, core_ids=list(range(NCORE)))
    y = np.empty((B, T, D), np.float32)
    for c in range(NCORE):
        b, r = c // 4, c % 4
        y[b, r * TC:(r + 1) * TC, :] = res.results[c]["out"].T
    return y


# version 08
# speedup vs baseline: 1.0467x; 1.0467x over previous
import contextlib
import numpy as np

import concourse.bass as bass
import concourse.mybir as mybir
from concourse.bass_utils import run_bass_kernel_spmd

F32 = mybir.dt.float32
BF16 = mybir.dt.bfloat16
I8 = mybir.dt.int8
I32 = mybir.dt.int32
AF = mybir.ActivationFunctionType
ALU = mybir.AluOpType
AXX = mybir.AxisListType.X

P, D, H = 128, 1024, 4096
B, T, NCORE = 2, 8192, 8
TC = (B * T) // NCORE
KC, HC, NTT = D // P, H // P, 4
BIGNEG = -1.0e30
BIGOFF = 1.0e18
EPS32 = float(np.float32(1e-5))
FAT_IT = 12
LOC_IT = 18


class Eng:
    def __init__(self, name, chain=True, lag=1):
        self.name, self.ops, self.n, self.sem = name, [], 0, None
        self.chain = chain
        self.lag = lag

    def do(self, fn, waits=(), inc=True, chain=None):
        waits = tuple(w for w in waits if w is not None)
        prev, me = self.n - (self.lag - 1), self
        ch = self.chain if chain is None else chain

        def op(e):
            if prev > 0 and ch:
                e.wait_ge(me.sem, prev)
            for (s, v) in waits:
                if s is not me.sem:
                    e.wait_ge(s, v)
            ins = fn(e)
            if inc:
                ins.then_inc(me.sem, 1)
        self.ops.append(op)
        if inc:
            self.n += 1
        return (self.sem, self.n)

    def tok(self):
        return (self.sem, self.n)


def build_program(debug=False):
    kw = {"kind": "ExternalOutput"} if debug else {}
    nc = bass.Bass("TRN2", target_bir_lowering=False, debug=False)

    xs = nc.dram_tensor("x", [D, TC], F32, kind="ExternalInput").ap()
    wiT = nc.dram_tensor("wiT", [D, D], F32, kind="ExternalInput").ap()
    wfT = nc.dram_tensor("wfT", [D, D], F32, kind="ExternalInput").ap()
    wgT = nc.dram_tensor("wgT", [D, D], F32, kind="ExternalInput").ap()
    woT = nc.dram_tensor("woT", [D, D], F32, kind="ExternalInput").ap()
    w1T = nc.dram_tensor("w1T", [D, H], F32, kind="ExternalInput").ap()
    w2T = nc.dram_tensor("w2T", [H, D], F32, kind="ExternalInput").ap()
    aux = nc.dram_tensor("aux", [P, 32], F32, kind="ExternalInput").ap()
    out = nc.dram_tensor("out", [D, TC], F32, kind="ExternalOutput").ap()

    s_h0 = nc.dram_tensor("s_h0", [D, TC], F32, **kw).ap()
    s_xn16 = nc.dram_tensor("s_xn16", [D, TC], BF16, **kw).ap()
    s_P = nc.dram_tensor("s_P", [D, TC], F32, **kw).ap()
    s_si = nc.dram_tensor("s_si", [D, TC], F32).ap()
    s_w1q = nc.dram_tensor("s_w1q", [D, H], BF16, **kw).ap()
    s_w2q = nc.dram_tensor("s_w2q", [H, D], BF16).ap()
    s_hg = nc.dram_tensor("s_hg", [H, TC], F32, **kw).ap()
    s_sat = nc.dram_tensor("s_sat", [D, TC], F32, **kw).ap()
    s_bnc = nc.dram_tensor("s_bnc", [1, 65536], F32).ap()
    s_cons = nc.dram_tensor("s_cons", [P, 256], F32).ap()

    cc_car_i = nc.dram_tensor("cc_car_i", [P, 16], F32)
    cc_car_o = nc.dram_tensor("cc_car_o", [4 * P, 16], F32)
    cc_q1_i = nc.dram_tensor("cc_q1_i", [P, 512], F32)
    cc_q1_o = nc.dram_tensor("cc_q1_o", [NCORE * P, 512], F32)
    cc_q2_i = nc.dram_tensor("cc_q2_i", [P, 1024], F32)
    cc_q2_o = nc.dram_tensor("cc_q2_o", [NCORE * P, 1024], F32)
    cc_w1_i = nc.dram_tensor("cc_w1_i", [P, 520], F32)
    cc_w1_o = nc.dram_tensor("cc_w1_o", [NCORE * P, 520], F32)
    cc_w2_i = nc.dram_tensor("cc_w2_i", [P, 520], F32)
    cc_w2_o = nc.dram_tensor("cc_w2_o", [NCORE * P, 520], F32)

    ctx = contextlib.ExitStack()

    def sb(name, shape, dt=F32):
        return ctx.enter_context(nc.sbuf_tensor(name, shape, dt))

    ARC = 43008
    arena = sb("arena", [P, ARC])
    auxT = sb("auxT", [P, 32])
    onesD = sb("onesD", [P, 1])
    zcol = sb("zcol", [P, 1])
    fbT = sb("fbT", [1, 1], BF16)
    S0 = sb("S0", [P, 512])
    S1 = sb("S1", [P, 512])
    M0 = sb("M0", [P, 512])
    M1 = sb("M1", [P, 512])
    SS = sb("SS", [P, 512])
    scr = sb("scr", [1, 64])
    scrI = sb("scrI", [1, 8], I32)
    scrU = sb("scrU", [1, 16], mybir.dt.uint8)
    cmu8 = sb("cmu8", [P, 4], mybir.dt.uint8)
    scrW = sb("scrW", [P, 16])
    cons = sb("cons", [P, 2048])
    brow = sb("brow", [1, 256])
    m8s = sb("m8s", [P, 8])
    carry = sb("carry", [P, 96])
    stg = sb("stg", [32, 64])
    ex2 = sb("ex2", [P, 256])

    pb = [ctx.enter_context(nc.psum_tensor(f"pb{i}", [P, 512], F32))
          for i in range(8)]
    sems = {k: ctx.enter_context(nc.semaphore("sem_" + k))
            for k in ("dma", "pe", "act", "dve", "pool", "cc")}

    SY, TE, SC_, VE, GP = (Eng(n, lag=(2 if n == "tensor" else 1))
                           for n in ("sync", "tensor", "scalar",
                                     "vector", "gpsimd"))
    TE.sem, SC_.sem, VE.sem, GP.sem, SY.sem = (
        sems["pe"], sems["act"], sems["dve"], sems["pool"], sems["dma"])

    ndma = [0]

    def dma(dst, src, waits=(), slow=False):
        waits = tuple(w for w in waits if w is not None)

        def op(e):
            for (s, v) in waits:
                e.wait_ge(s, v)
            e.dma_start(out=dst, in_=src,
                        allow_slow_non_contiguous=slow).then_inc(
                sems["dma"], 16)
        SY.ops.append(op)
        ndma[0] += 1
        return (sems["dma"], 16 * ndma[0])

    ncc = [0]

    def collective(groups, in_t, out_t, waits=()):
        waits = tuple(w for w in waits if w is not None)

        def op(e):
            for (s, v) in waits:
                e.wait_ge(s, v)
            e.collective_compute("AllGather", ALU.bypass, replica_groups=groups,
                                 ins=[in_t.ap().opt()], outs=[out_t.ap().opt()]
                                 ).then_inc(sems["cc"], 1)
        GP.ops.append(op)
        ncc[0] += 1
        return (sems["cc"], ncc[0])

    def A(off, cols):
        assert 0 <= off and off + cols <= ARC, f"arena [{off},{off + cols})"
        return arena[:, off:off + cols]

    def Ab(off, bf16_cols):
        return A(off, bf16_cols // 2).bitcast(BF16)

    def Ai(off, i8_cols):
        return A(off, i8_cols // 4).bitcast(I8)

    XS = [i * 2048 for i in range(9)]
    HS = [18432 + i * 2048 for i in range(8)]
    RW = 34816
    RQ = 41472
    RW_A, RW_B, RW_C = RW, RW + 4096, RW + 6144

    ve = lambda fn, waits=(): VE.do(fn, waits=waits)
    ac = lambda fn, waits=(): SC_.do(fn, waits=waits)
    te = lambda fn, waits=(), chain=None: TE.do(fn, waits=waits, chain=chain)

    def tdrain(*waits):
        return te(lambda e: e.matmul(pb[7][64:65, 0:1], fbT[0:1, 0:1],
                                     fbT[0:1, 0:1], start=True, stop=True),
                  waits=list(waits))

    t_aux = dma(auxT[:], aux)
    ve(lambda e: e.memset(onesD[:], 1.0 / D))
    ve(lambda e: e.memset(fbT[:], 0.0))
    ve(lambda e: e.memset(zcol[:], 0.0))
    ve(lambda e: e.memset(scr[:], 0.0))
    ve(lambda e: e.memset(scrW[:], 0.0))
    ve(lambda e: e.memset(carry[:], 0.0))
    ve(lambda e: e.memset(stg[:], 0.0))
    ve(lambda e: e.memset(cons[:], BIGNEG))
    ve(lambda e: e.memset(ex2[:], BIGNEG))
    ve(lambda e: e.memset(m8s[:], 0.0))
    t_cm = ve(lambda e: e.tensor_copy(cmu8[:, 0:3], auxT[:, 24:27]),
              waits=[t_aux])
    t_init = VE.tok()

    normw_c = lambda k: auxT[:, k:k + 1]
    lng_c = lambda k: auxT[:, 8 + k:8 + k + 1]
    lnb_c = lambda k: auxT[:, 16 + k:16 + k + 1]
    cmask_c = lambda q: auxT[:, 24 + q:24 + q + 1]
    eps_col = auxT[:, 27:28]
    q995_c = auxT[0:1, 28:29]

    def st_mu(tt):
        S = S0 if tt < 3 else S1
        r = 32 * tt if tt < 3 else 0
        return S[r:r + 1, :]

    def st_m2(tt):
        M = M0 if tt < 3 else M1
        r = 32 * tt if tt < 3 else 0
        return M[r:r + 1, :]

    def emit_mean_stats(src_ap_fn, src_tok, sq_off):
        last_mu = last_m2 = None
        tq = None
        for k in range(KC):
            w = [src_tok[k]]
            if last_m2 is not None:
                w.append(last_m2)
            tq = ac(lambda e, k_=k, s_=sq_off: e.activation(
                out=A(s_, 2048), in_=src_ap_fn(k_), func=AF.Square), waits=w)
            for tt in range(NTT):
                bmu, rmu = (4, 32 * tt) if tt < 3 else (5, 0)
                last_mu = te(lambda e, k_=k, t_=tt, b_=bmu, r_=rmu: e.matmul(
                    pb[b_][r_:r_ + 1, :], onesD[:],
                    src_ap_fn(k_)[:, t_ * 512:(t_ + 1) * 512],
                    start=(k_ == 0), stop=(k_ == KC - 1)),
                    waits=[src_tok[k], t_init])
                bm2, rm2 = (6, 32 * tt) if tt < 3 else (7, 0)
                last_m2 = te(lambda e, k_=k, t_=tt, b_=bm2, r_=rm2, s_=sq_off:
                             e.matmul(pb[b_][r_:r_ + 1, :], onesD[:],
                                      A(s_, 2048)[:, t_ * 512:(t_ + 1) * 512],
                                      start=(k_ == 0), stop=(k_ == KC - 1)),
                             waits=[tq])
        return last_mu, last_m2

    def evac_stats(tok_mu, tok_m2):
        toks = [ve(lambda e: e.tensor_copy(S0[0:96, :], pb[4][0:96, :]),
                   waits=[tok_mu]),
                ve(lambda e: e.tensor_copy(S1[0:32, :], pb[5][0:32, :]),
                   waits=[tok_mu]),
                ve(lambda e: e.tensor_copy(M0[0:96, :], pb[6][0:96, :]),
                   waits=[tok_m2]),
                ve(lambda e: e.tensor_copy(M1[0:32, :], pb[7][0:32, :]),
                   waits=[tok_m2])]
        return toks

    def coef_seq(w0):
        done = []
        for (Smu, Sm2, r) in ((S0, M0, 96), (S1, M1, 32)):
            t1 = ve(lambda e, m_=Smu, r_=r: e.tensor_tensor(
                SS[0:r_, :], m_[0:r_, :], m_[0:r_, :], ALU.mult), waits=w0)
            t2 = ve(lambda e, v_=Sm2, r_=r: e.tensor_tensor(
                v_[0:r_, :], v_[0:r_, :], SS[0:r_, :], ALU.subtract),
                waits=[t1])
            t3 = ac(lambda e, v_=Sm2, r_=r: e.activation(
                out=v_[0:r_, :], in_=v_[0:r_, :], func=AF.Ln,
                bias=eps_col[0:r_, :]), waits=[t2, t_aux])
            t4 = ac(lambda e, v_=Sm2, r_=r: e.activation(
                out=v_[0:r_, :], in_=v_[0:r_, :], func=AF.Exp, scale=-0.5),
                waits=[t3])
            t5 = ve(lambda e, m_=Smu, v_=Sm2, r_=r: e.tensor_tensor(
                m_[0:r_, :], m_[0:r_, :], v_[0:r_, :], ALU.mult), waits=[t4])
            t6 = ve(lambda e, m_=Smu, r_=r: e.tensor_scalar(
                m_[0:r_, :], m_[0:r_, :], -1.0, None, ALU.mult), waits=[t5])
            done.append(t6)
        return done

    nbnc = [0]

    def bcast_rows(rows, dst_ap, w0, cols=512):
        off = nbnc[0]
        nbnc[0] += max(len(rows) * cols, 8)
        assert nbnc[0] <= 32768
        toks = [dma(s_bnc[0:1, off + i * cols: off + (i + 1) * cols], r,
                    waits=w0) for i, r in enumerate(rows)]
        return dma(dst_ap, s_bnc[0:1, off:off + len(rows) * cols]
                   .to_broadcast([P, len(rows) * cols]), waits=toks)

    t_x = [dma(A(XS[k], 2048), xs[k * P:(k + 1) * P, :], waits=[t_init])
           for k in range(KC)]

    mu1, m21 = emit_mean_stats(lambda k: A(XS[k], 2048), t_x, XS[8])
    fst1 = tdrain(mu1, m21)
    ev1 = evac_stats(fst1, fst1)
    cf1 = coef_seq(ev1)
    a1_rep, b1_rep = A(RW_A, 2048), A(RW_A + 2048, 2048)
    t_a1 = bcast_rows([st_m2(tt) for tt in range(NTT)], a1_rep, cf1)
    t_b1 = bcast_rows([st_mu(tt) for tt in range(NTT)], b1_rep, cf1)

    t_xn, t_x16 = {}, {}
    for k in range(KC):
        t1 = ve(lambda e, k_=k: e.tensor_tensor(
            A(XS[k_], 2048), A(XS[k_], 2048), a1_rep, ALU.mult),
            waits=[t_a1, t_x[k]])
        t2 = ve(lambda e, k_=k: e.tensor_tensor(
            A(XS[k_], 2048), A(XS[k_], 2048), b1_rep, ALU.add),
            waits=[t_b1, t1])
        t_xn[k] = t2
        c16 = Ab(HS[4 + k // 2] + (k % 2) * 1024, 2048)
        t3 = ve(lambda e, c_=c16, k_=k: e.tensor_copy(c_, A(XS[k_], 2048)),
                waits=[t2])
        t_x16[k] = t3
        dma(s_xn16[k * P:(k + 1) * P, :], c16, waits=[t3])

    mu2, m22 = emit_mean_stats(lambda k: A(XS[k], 2048),
                               [t_xn[k] for k in range(KC)], XS[8])
    fst2 = tdrain(mu2, m22)
    ev2 = evac_stats(fst2, fst2)
    cf2 = coef_seq(ev2)
    a2_rep, b2_rep = A(RW_B, 2048), A(RW_A + 2048, 2048)
    t_a2 = bcast_rows([st_m2(tt) for tt in range(NTT)], a2_rep,
                      cf2 + [t_b1, t_xn[KC - 1]])
    t_b2 = bcast_rows([st_mu(tt) for tt in range(NTT)], b2_rep, cf2)

    red1q1 = A(RQ, 1024)
    contq1 = A(RQ + 1024, 512)
    t_h0, t_m8 = {}, None
    for k in range(KC):
        hwrk = A(XS[8], 2048)
        w = [t_a2, t_b2, t_xn[k], t_x16[k]]
        if k > 0:
            w += [t_m8, t_h0[k - 1]]
        t1 = ve(lambda e, k_=k, h_=hwrk: e.tensor_tensor(
            h_, A(XS[k_], 2048), a2_rep, ALU.mult), waits=w)
        t2 = ve(lambda e, h_=hwrk: e.tensor_tensor(h_, h_, b2_rep, ALU.add),
                waits=[t1])
        t3 = ve(lambda e, k_=k, h_=hwrk: e.tensor_scalar(
            h_, h_, lng_c(k_), lnb_c(k_), ALU.mult, ALU.add),
            waits=[t2, t_aux])
        t_h0[k] = t3
        dma(s_h0[k * P:(k + 1) * P, :], hwrk, waits=[t3])
        ta = ac(lambda e, k_=k, h_=hwrk: e.activation(
            out=A(XS[k_], 2048), in_=h_, func=AF.Abs), waits=[t3])
        for w8 in range(16):
            ta = ve(lambda e, k_=k, w_=w8: e.max(
                out=red1q1[:, k_ * P + w_ * 8: k_ * P + (w_ + 1) * 8],
                in_=A(XS[k_], 2048)[:, w_ * 128:(w_ + 1) * 128]), waits=[ta])
        t_m8 = ta

    t_c1 = t_m8
    for w8 in range(64):
        t_c1 = ve(lambda e, w_=w8: e.max(
            out=contq1[:, w_ * 8:(w_ + 1) * 8],
            in_=red1q1[:, w_ * 16:(w_ + 1) * 16]), waits=[t_c1])

    t_ci1 = dma(cc_q1_i.ap(), contq1, waits=[t_c1])
    t_gc1 = collective([list(range(NCORE))], cc_q1_i, cc_q1_o, waits=[t_ci1])
    q1segs = [(RW_A + r * 512, 512) for r in range(NCORE)]

    def emit_projection(wsrc, rhs_ap_fn, rhs_tok, evac_fn, strip_off, pre=None,
                        interleave=None, first_mm_waits=()):
        evt = {}
        last_strip = None
        for do in range(KC):
            strip = A(strip_off, 1024)
            castt = Ab(strip_off + 1024, 1024)
            tl = dma(strip.rearrange("p (k j) -> p k j", j=P),
                     wsrc[:, do * P:(do + 1) * P].rearrange(
                         "(k p) j -> p k j", p=P), waits=[pre, last_strip])
            tc_ = ve(lambda e, s_=strip, c_=castt: e.tensor_copy(c_, s_),
                     waits=[tl])
            for k in range(KC):
                for tt in range(NTT):
                    w = [tc_, rhs_tok]
                    if k == 0 and do == 0:
                        w.extend(first_mm_waits)
                    if k == 0 and do > 0:
                        w.append(evt[(do - 1, tt)])
                    mm = te(lambda e, k_=k, t_=tt, c_=castt: e.matmul(
                        pb[t_][:, :], c_[:, k_ * P:(k_ + 1) * P],
                        rhs_ap_fn(k_, t_),
                        start=(k_ == 0), stop=(k_ == KC - 1)), waits=w)
                    if k == KC - 1:
                        evt[(do, tt)] = evac_fn(do, tt, tdrain(mm))
            last_strip = TE.tok()
            if interleave is not None:
                interleave(do)
        return evt

    def xn16_ap(k, tt):
        return Ab(HS[4 + k // 2] + (k % 2) * 1024, 2048)[:, tt * 512:(tt + 1) * 512]

    f_ev = {}

    def f_evac(do, tt, mm):
        tok = ac(lambda e, d_=do, t_=tt: e.activation(
            out=A(XS[d_], 2048)[:, t_ * 512:(t_ + 1) * 512],
            in_=pb[t_][:, :], func=AF.Sigmoid),
            waits=[mm, t_m8, t_h0[KC - 1]])
        f_ev[(do, tt)] = tok
        return tok

    emit_projection(wfT, xn16_ap, t_x16[KC - 1], f_evac, XS[8],
                    pre=t_h0[KC - 1])
    t_f = {do: f_ev[(do, NTT - 1)] for do in range(KC)}

    def emit_gamma(wsrc, rows, cols, strip_offs, scr_slot, pre=None,
                   first_wait=()):
        tw = 2048 if cols % 2048 == 0 else 1024
        tiles = [(rc, cc) for rc in range(rows // P) for cc in range(cols // tw)]
        lastmm = None
        for i, (rc, cc) in enumerate(tiles):
            so = strip_offs[i % 2]
            w = [t_init, pre]
            if i >= 2:
                w.append(lastmm)
            tl = dma(A(so, tw),
                     wsrc[rc * P:(rc + 1) * P, cc * tw:(cc + 1) * tw],
                     waits=w)
            tab = ac(lambda e, s_=so, tw_=tw: e.activation(
                out=A(s_, tw_), in_=A(s_, tw_), func=AF.Abs), waits=[tl])
            for q in range(tw // 512):
                lastmm = te(lambda e, s_=so, q_=q, i_=i, tw_=tw: e.matmul(
                    pb[6][0:1, :], onesD[:],
                    A(s_, tw_)[:, q_ * 512:(q_ + 1) * 512],
                    start=(i_ == 0 and q_ == 0),
                    stop=(i_ == len(tiles) - 1 and q_ == tw_ // 512 - 1)),
                    waits=[tab] + (list(first_wait) if i == 0 and q == 0
                                   else []))
        tev = ve(lambda e: e.tensor_copy(SS[0:1, :], pb[6][0:1, :]),
                 waits=[tdrain(lastmm)])
        tr = ve(lambda e: e.tensor_reduce(scr[0:1, 40:41], SS[0:1, :], AXX,
                                          ALU.add), waits=[tev])
        tg = ve(lambda e, s_=scr_slot, n_=(rows * cols // D): e.tensor_scalar(
            scr[0:1, s_:s_ + 1], scr[0:1, 40:41], 1.0 / n_, EPS32,
            ALU.mult, ALU.add), waits=[tr])
        return tg

    t_g1w = emit_gamma(w1T, D, H, (HS[0], HS[1]), 0, first_wait=ev2)

    def emit_ternarize(wsrc, dst, rows, cols, strip_offs, tmp_f, tmp_i, tmp_b,
                       g_slot, tg, rg_col_idx):
        trg = ve(lambda e, g_=g_slot: e.reciprocal(
            scr[0:1, 44:45], scr[0:1, g_:g_ + 1]), waits=[tg])
        trgb = bcast_rows([scr[0:1, 44:45]],
                          scrW[:, rg_col_idx:rg_col_idx + 1], [trg], cols=1)
        tw = 2048 if cols % 2048 == 0 else 1024
        tiles = [(rc, cc) for rc in range(rows // P) for cc in range(cols // tw)]
        lastq = None
        for i, (rc, cc) in enumerate(tiles):
            so = strip_offs[i % 2]
            w = [trgb]
            if i >= 2:
                w.append(lastq)
            tl = dma(A(so, tw),
                     wsrc[rc * P:(rc + 1) * P, cc * tw:(cc + 1) * tw],
                     waits=w)
            tmin = ve(lambda e, s_=so, tw_=tw: e.tensor_scalar(
                A(tmp_f, tw_), A(s_, tw_), scrW[:, rg_col_idx:rg_col_idx + 1],
                1.49, ALU.mult, ALU.min), waits=[tl])
            ti8 = ve(lambda e, tw_=tw: e.tensor_scalar(
                Ai(tmp_i, tw_ * 4)[:, 0:tw_], A(tmp_f, tw_), -1.49, None,
                ALU.max), waits=[tmin])
            tb = ve(lambda e, tw_=tw: e.tensor_copy(
                Ab(tmp_b, tw_ * 2)[:, 0:tw_], Ai(tmp_i, tw_ * 4)[:, 0:tw_]),
                waits=[ti8])
            lastq = dma(dst[rc * P:(rc + 1) * P, cc * tw:(cc + 1) * tw],
                        Ab(tmp_b, tw * 2)[:, 0:tw], waits=[tb])
        return lastq

    t_w1q = emit_ternarize(w1T, s_w1q, D, H, (HS[0], HS[1]), RW_B, RW_C,
                           RW_A, 0, t_g1w, 0)

    si_ev = {}
    t_hloc, t_Pst, t_scans, t_inp = {}, {}, [], {}
    HSmap = {0: HS[0], 1: HS[1], 2: HS[2], 3: HS[3], 4: HS[6], 5: HS[7],
             6: HS[4], 7: HS[5]}
    si_gate = {}

    def si_evac(do, tt, mm):
        w = [mm, t_w1q]
        if do > 0:
            w.append(si_gate[do - 1])
        tok = ac(lambda e, t_=tt: e.activation(
            out=A(XS[8], 2048)[:, t_ * 512:(t_ + 1) * 512],
            in_=pb[t_][:, :], func=AF.Silu), waits=w)
        si_ev[(do, tt)] = tok
        return tok

    def scan_chain(do):
        w = [si_ev[(do, tt)] for tt in range(NTT)] + [t_ci1]
        t2 = dma(s_si[do * P:(do + 1) * P, :], A(XS[8], 2048), waits=w)
        si_gate[do] = t2
        t_inp[do] = t2

    emit_projection(wiT, xn16_ap, t_x16[KC - 1], si_evac, RQ, pre=t_ci1,
                    interleave=scan_chain,
                    first_mm_waits=[f_ev[(KC - 1, tt)] for tt in range(NTT)])

    t_wi_done = TE.tok()
    for do in range(KC):
        tsl = dma(A(XS[8], 2048), s_si[do * P:(do + 1) * P, :],
                  waits=[t_inp[do], t_wi_done] +
                  ([t_hloc[do - 1]] if do else []))
        tu = ve(lambda e, d_=do: e.tensor_scalar(
            A(RW_B, 2048), A(XS[d_], 2048), -1.0, 1.0, ALU.mult, ALU.add),
            waits=[tsl, t_f[do]])
        ti2 = ve(lambda e, d_=do: e.tensor_tensor(
            A(HSmap[d_], 2048), A(XS[8], 2048), A(RW_B, 2048), ALU.mult),
            waits=[tu])
        t3 = ve(lambda e, d_=do: e.tensor_tensor_scan(
            A(HSmap[d_], 2048), A(XS[d_], 2048), A(HSmap[d_], 2048), 0.0,
            ALU.mult, ALU.add), waits=[ti2])
        t4 = ve(lambda e, d_=do: e.tensor_tensor_scan(
            A(RW_B, 2048), A(XS[d_], 2048), A(XS[d_], 2048), 1.0,
            ALU.mult, ALU.bypass), waits=[t3])
        t_Pst[do] = dma(s_P[do * P:(do + 1) * P, :], A(RW_B, 2048), waits=[t4])
        t5 = ve(lambda e, d_=do: e.tensor_copy(
            carry[:, d_:d_ + 1], A(RW_B, 2048)[:, 2047:2048]), waits=[t4])
        t6 = ve(lambda e, d_=do: e.tensor_copy(
            carry[:, 8 + d_:8 + d_ + 1], A(HSmap[d_], 2048)[:, 2047:2048]),
            waits=[t5])
        t_hloc[do] = t3
        t_scans.append(t6)

    t_car_i = dma(cc_car_i.ap(), carry[:, 0:16], waits=t_scans)
    t_car_g = collective([[0, 1, 2, 3], [4, 5, 6, 7]], cc_car_i, cc_car_o,
                         waits=[t_car_i])
    t_car_l = [dma(carry[:, 16 + 16 * q:32 + 16 * q],
                   cc_car_o.ap()[q * P:(q + 1) * P, :], waits=[t_car_g])
               for q in range(4)]
    tC = ve(lambda e: e.memset(carry[:, 80:96], 0.0), waits=t_car_l)
    for q in range(3):
        t1 = ve(lambda e, q_=q: e.tensor_tensor(
            carry[:, 88:96], carry[:, 16 + 16 * q_:24 + 16 * q_],
            carry[:, 80:88], ALU.mult), waits=[tC])
        t2 = ve(lambda e, q_=q: e.tensor_tensor(
            carry[:, 88:96], carry[:, 88:96],
            carry[:, 24 + 16 * q_:32 + 16 * q_], ALU.add), waits=[t1])
        tC = ve(lambda e, q_=q: e.select(
            carry[:, 80:88], cmu8[:, q_:q_ + 1].to_broadcast([P, 8]),
            carry[:, 88:96], carry[:, 80:88]), waits=[t2, t_cm])

    if debug:
        dbg_car = nc.dram_tensor("dbg_car", [P, 96], F32,
                                 kind="ExternalOutput").ap()
        t_dbgcar = dma(dbg_car, carry[:, 0:96], waits=[tC])
    t_hfix = {}
    prev = None
    for do in range(KC):
        tl = dma(A(XS[8], 2048), s_P[do * P:(do + 1) * P, :],
                 waits=[t_Pst[do], tC, prev])
        t_hfix[do] = ve(lambda e, d_=do: e.scalar_tensor_tensor(
            A(XS[d_], 2048), A(XS[8], 2048), carry[:, 80 + d_:81 + d_],
            A(HSmap[d_], 2048), ALU.mult, ALU.add), waits=[tl, tC])
        prev = t_hfix[do]

    t_dbghf = {}
    if debug:
        dbg_hf = nc.dram_tensor("dbg_hf", [D, TC], F32,
                                kind="ExternalOutput").ap()
        for k in range(KC):
            t_dbghf[k] = dma(dbg_hf[k * P:(k + 1) * P, :], A(XS[k], 2048),
                             waits=[t_hfix[k]])
    t_hfix_g = {k: t_hfix[k] for k in range(KC)}
    t_hfix_g[0] = t_g1w if t_g1w[1] > t_hfix[0][1] and t_g1w[0] is t_hfix[0][0] else t_hfix[0]
    mu3, m23 = emit_mean_stats(lambda k: A(XS[k], 2048),
                               [t_hfix[KC - 1]] * KC, XS[8])
    fst3 = tdrain(m23)
    ev3a = ve(lambda e: e.tensor_copy(M0[0:96, :], pb[6][0:96, :]), waits=[fst3])
    ev3b = ve(lambda e: e.tensor_copy(M1[0:32, :], pb[7][0:32, :]), waits=[fst3])
    if debug:
        dbg_m2 = nc.dram_tensor("dbg_m2", [P, 1024], F32,
                                kind="ExternalOutput").ap()
        tdm = dma(dbg_m2[:, 0:512], M0[:, :], waits=[ev3a, ev3b])
        tdm2 = dma(dbg_m2[:, 512:1024], M1[:, :], waits=[ev3a, ev3b])
        ev3a = tdm
        ev3b = tdm2
    rr_done = []
    for (Mt, r) in ((M0, 96), (M1, 32)):
        t3 = ac(lambda e, v_=Mt, r_=r: e.activation(
            out=v_[0:r_, :], in_=v_[0:r_, :], func=AF.Ln,
            bias=eps_col[0:r_, :]), waits=[ev3a, ev3b, t_aux])
        t4 = ac(lambda e, v_=Mt, r_=r: e.activation(
            out=v_[0:r_, :], in_=v_[0:r_, :], func=AF.Exp, scale=-0.5),
            waits=[t3])
        rr_done.append(t4)

    def lax_dance(n_val, m_slot, hw_slot):
        t0 = ve(lambda e: e.memset(scr[0:1, 30:31], float(n_val)))
        t1 = ve(lambda e: e.tensor_scalar(
            scr[0:1, 31:32], scr[0:1, 30:31], 1.0, None, ALU.subtract),
            waits=[t0])
        t2 = ve(lambda e: e.tensor_tensor(
            scr[0:1, 32:33], q995_c, scr[0:1, 31:32], ALU.mult),
            waits=[t1, t_aux])
        ti = ve(lambda e: e.tensor_copy(scrI[0:1, 0:1], scr[0:1, 32:33]),
                waits=[t2])
        tf = ve(lambda e: e.tensor_copy(scr[0:1, 33:34], scrI[0:1, 0:1]),
                waits=[ti])
        tg = ve(lambda e: e.tensor_scalar(
            scr[0:1, 35:36], scr[0:1, 33:34], scr[0:1, 32:33], None,
            ALU.is_gt), waits=[tf])
        tlow = ve(lambda e: e.tensor_tensor(
            scr[0:1, 33:34], scr[0:1, 33:34], scr[0:1, 35:36], ALU.subtract),
            waits=[tg])
        thw = ve(lambda e, h_=hw_slot: e.tensor_tensor(
            scr[0:1, h_:h_ + 1], scr[0:1, 32:33], scr[0:1, 33:34],
            ALU.subtract), waits=[tlow])
        tm = ve(lambda e: e.tensor_tensor(
            scr[0:1, 36:37], scr[0:1, 30:31], scr[0:1, 33:34], ALU.subtract),
            waits=[thw])
        return ve(lambda e, m_=m_slot: e.tensor_scalar(
            scr[0:1, m_:m_ + 1], scr[0:1, 36:37], 1.0, None, ALU.subtract),
            waits=[tm])

    nsel = [0]

    def fat_bisect(segs, m_slot, gather_toks, junk_off, iters=FAT_IT):
        base = 32768 + nsel[0] * 1024
        nsel[0] += 1
        assert base + 1024 <= 65536
        segs_ap = [A(o, c) for (o, c) in segs]
        w0 = list(gather_toks)
        tmax = ve(lambda e, s_=segs_ap[0]: e.tensor_reduce(
            scrW[:, 2:3], s_, AXX, ALU.max), waits=w0)
        for s_ in segs_ap[1:]:
            tr = ve(lambda e, ss_=s_: e.tensor_reduce(
                scrW[:, 3:4], ss_, AXX, ALU.max), waits=[tmax])
            tmax = ve(lambda e: e.tensor_tensor(
                scrW[:, 2:3], scrW[:, 2:3], scrW[:, 3:4], ALU.max), waits=[tr])
        tst = dma(s_bnc[0:1, base + 128:base + 256].rearrange("o p -> p o"),
                  scrW[:, 2:3], waits=[tmax])
        tld = dma(brow[0:1, 0:128], s_bnc[0:1, base + 128:base + 256],
                  waits=[tst])
        thi = ve(lambda e: e.tensor_reduce(scr[0:1, 50:51], brow[0:1, 0:128],
                                           AXX, ALU.max), waits=[tld])
        t0 = ve(lambda e: e.memset(scr[0:1, 51:52], 0.0), waits=[thi])
        t0 = ve(lambda e: e.tensor_copy(scr[0:1, 52:53], scr[0:1, 50:51]),
                waits=[t0])
        t0 = ve(lambda e, m_=m_slot: e.tensor_scalar(
            scr[0:1, 55:56], scr[0:1, m_:m_ + 1], 1.0, None, ALU.add),
            waits=[t0])
        tcnt = t0
        for it in range(iters):
            tcnt = ve(lambda e: e.tensor_scalar(
                scr[0:1, 53:54], scr[0:1, 51:52], scr[0:1, 52:53], 0.5,
                ALU.add, ALU.mult), waits=[tcnt])
            tb1 = dma(s_bnc[0:1, base:base + 1], scr[0:1, 53:54], waits=[tcnt])
            tb2 = dma(scrW[:, 4:5],
                      s_bnc[0:1, base:base + 1].to_broadcast([P, 1]),
                      waits=[tb1])
            tacc = ve(lambda e: e.memset(scrW[:, 5:6], 0.0), waits=[tb2])
            for si, s_ in enumerate(segs_ap):
                tpc = ve(lambda e, ss_=s_, jo_=junk_off: e.tensor_scalar(
                    Ai(jo_, ss_.shape[1]), ss_, scrW[:, 4:5], 0.0,
                    ALU.is_gt, ALU.add, accum_out=scrW[:, 6:7]),
                    waits=[tacc])
                tacc = ve(lambda e: e.tensor_tensor(
                    scrW[:, 5:6], scrW[:, 5:6], scrW[:, 6:7], ALU.add),
                    waits=[tpc])
            tb3 = dma(s_bnc[0:1, base + 384:base + 512]
                      .rearrange("o p -> p o"), scrW[:, 5:6], waits=[tacc])
            tb4 = dma(brow[0:1, 128:256],
                      s_bnc[0:1, base + 384:base + 512], waits=[tb3])
            tc = ve(lambda e: e.tensor_reduce(
                scr[0:1, 54:55], brow[0:1, 128:256], AXX, ALU.add),
                waits=[tb4])
            tg = ve(lambda e: e.tensor_scalar(
                scrU[0:1, 0:1], scr[0:1, 54:55], scr[0:1, 55:56], None,
                ALU.is_ge), waits=[tc])
            tl_ = ve(lambda e: e.select(scr[0:1, 51:52], scrU[0:1, 0:1],
                                        scr[0:1, 53:54], scr[0:1, 51:52]),
                     waits=[tg])
            tcnt = ve(lambda e: e.select(scr[0:1, 52:53], scrU[0:1, 0:1],
                                         scr[0:1, 52:53], scr[0:1, 53:54]),
                      waits=[tl_])
        return tcnt

    def band_setup(eps, tA_col_idx, tB_col_idx, w0):
        base = 32768 + nsel[0] * 1024
        nsel[0] += 1
        t1 = ve(lambda e: e.tensor_scalar(
            scr[0:1, 19:20], scr[0:1, 52:53], 1.0 - eps, None, ALU.mult),
            waits=w0)
        t2 = ve(lambda e: e.tensor_scalar(
            scr[0:1, 20:21], scr[0:1, 52:53], 1.0 + eps, None, ALU.mult),
            waits=[t1])
        tb1 = dma(s_bnc[0:1, base:base + 2], scr[0:1, 19:21], waits=[t2])
        ta_ = dma(scrW[:, tA_col_idx:tA_col_idx + 1],
                  s_bnc[0:1, base:base + 1].to_broadcast([P, 1]), waits=[tb1])
        tb_ = dma(scrW[:, tB_col_idx:tB_col_idx + 1],
                  s_bnc[0:1, base + 1:base + 2].to_broadcast([P, 1]),
                  waits=[tb1])
        return ta_, tb_

    def window_select(gath_segs, cnt_cols_ap, m_slot, out_slot, w0,
                      work_off):
        base = 32768 + nsel[0] * 1024
        nsel[0] += 1
        t1 = ve(lambda e: e.tensor_reduce(scrW[:, 5:6], cnt_cols_ap, AXX,
                                          ALU.add), waits=w0)
        tb3 = dma(s_bnc[0:1, base + 384:base + 512].rearrange("o p -> p o"),
                  scrW[:, 5:6], waits=[t1])
        tb4 = dma(brow[0:1, 128:256], s_bnc[0:1, base + 384:base + 512],
                  waits=[tb3])
        tcb = ve(lambda e: e.tensor_reduce(scr[0:1, 21:22],
                                           brow[0:1, 128:256], AXX, ALU.add),
                 waits=[tb4])
        tex = ve(lambda e: e.memset(A(work_off, 1024), BIGNEG), waits=w0)
        exo = [0]
        for (o, c) in gath_segs:
            for w8 in range(c // 64):
                d_ = exo[0]
                tex = ve(lambda e, o_=o, w_=w8, dd_=d_, c_=c: e.max(
                    out=A(work_off, 1024)[:, dd_:dd_ + 8],
                    in_=A(o_, c_)[:, w_ * 64:(w_ + 1) * 64]), waits=[tex])
                exo[0] += 8
        assert exo[0] <= 512
        t2x = ve(lambda e: e.memset(ex2[:], BIGNEG), waits=[tex])
        n2 = exo[0] // 16
        for w8 in range(n2):
            t2x = ve(lambda e, w_=w8: e.max(
                out=ex2[:, w_ * 8:(w_ + 1) * 8],
                in_=A(work_off, 1024)[:, w_ * 16:(w_ + 1) * 16]), waits=[t2x])
        tcs = dma(s_cons[:, 0:256], ex2[:], waits=[t2x])
        tcl = dma(cons[0:32, 0:1024],
                  s_cons.rearrange("(b a) c -> b (a c)", b=32), waits=[tcs])
        t0 = ve(lambda e: e.tensor_copy(scr[0:1, 57:58], scr[0:1, 19:20]),
                waits=[tcl, tcb])
        t0 = ve(lambda e: e.tensor_copy(scr[0:1, 58:59], scr[0:1, 20:21]),
                waits=[t0])
        t0 = ve(lambda e, m_=m_slot: e.tensor_scalar(
            scr[0:1, 55:56], scr[0:1, m_:m_ + 1], 1.0, None, ALU.add),
            waits=[t0])
        tloc = t0
        for it in range(LOC_IT):
            tloc = ve(lambda e: e.tensor_scalar(
                scr[0:1, 59:60], scr[0:1, 57:58], scr[0:1, 58:59], 0.5,
                ALU.add, ALU.mult), waits=[tloc])
            tr1 = ve(lambda e: e.tensor_copy(
                stg[0:1, 0:32], scr[0:1, 59:60].to_broadcast([1, 32])),
                waits=[tloc])
            tr2 = ve(lambda e: e.transpose(stg[0:32, 32:64], stg[0:32, 0:32]),
                     waits=[tr1])
            tpc = ve(lambda e: e.tensor_scalar(
                cons[0:32, 1024:2048], cons[0:32, 0:1024], stg[0:32, 32:33],
                0.0, ALU.is_gt, ALU.add, accum_out=stg[0:32, 33:34]),
                waits=[tr2])
            tr3 = ve(lambda e: e.tensor_copy(
                stg[0:32, 0:32], stg[0:32, 33:34].to_broadcast([32, 32])),
                waits=[tpc])
            tr4 = ve(lambda e: e.transpose(stg[0:32, 32:64], stg[0:32, 0:32]),
                     waits=[tr3])
            tr5 = ve(lambda e: e.tensor_reduce(
                scr[0:1, 60:61], stg[0:1, 32:64], AXX, ALU.add), waits=[tr4])
            tr6 = ve(lambda e: e.tensor_scalar(
                scr[0:1, 60:61], scr[0:1, 60:61], scr[0:1, 21:22], None,
                ALU.add), waits=[tr5])
            tg = ve(lambda e: e.tensor_scalar(
                scrU[0:1, 1:2], scr[0:1, 60:61], scr[0:1, 55:56], None,
                ALU.is_ge), waits=[tr6])
            tl_ = ve(lambda e: e.select(scr[0:1, 57:58], scrU[0:1, 1:2],
                                        scr[0:1, 59:60], scr[0:1, 57:58]),
                     waits=[tg])
            tloc = ve(lambda e: e.select(scr[0:1, 58:59], scrU[0:1, 1:2],
                                         scr[0:1, 58:59], scr[0:1, 59:60]),
                      waits=[tl_])
        tr1 = ve(lambda e: e.tensor_copy(
            stg[0:1, 0:32], scr[0:1, 58:59].to_broadcast([1, 32])),
            waits=[tloc])
        tr2 = ve(lambda e: e.transpose(stg[0:32, 32:64], stg[0:32, 0:32]),
                 waits=[tr1])
        tm1 = ve(lambda e: e.tensor_scalar(
            cons[0:32, 1024:2048], cons[0:32, 0:1024], stg[0:32, 32:33],
            -BIGOFF, ALU.is_gt, ALU.mult), waits=[tr2])
        tm2 = ve(lambda e: e.tensor_tensor(
            cons[0:32, 1024:2048], cons[0:32, 1024:2048], cons[0:32, 0:1024],
            ALU.add), waits=[tm1])
        tm3 = ve(lambda e: e.tensor_reduce(
            stg[0:32, 34:35], cons[0:32, 1024:2048], AXX, ALU.max),
            waits=[tm2])
        tm4 = ve(lambda e: e.tensor_copy(
            stg[0:32, 0:32], stg[0:32, 34:35].to_broadcast([32, 32])),
            waits=[tm3])
        tm5 = ve(lambda e: e.transpose(stg[0:32, 32:64], stg[0:32, 0:32]),
                 waits=[tm4])
        return ve(lambda e, o_=out_slot: e.tensor_reduce(
            scr[0:1, o_:o_ + 1], stg[0:1, 32:64], AXX, ALU.max), waits=[tm5])

    def band_correction(stream_src, nchunks, g_win, mask_i8_off, abs_off,
                        stream_off, exw, cc_in, cc_out, gseg_off, work_off,
                        tA_ci, tB_ci, pre_toks, cc_waits):
        tz = ve(lambda e: e.memset(scrW[:, 15:16], 0.0), waits=pre_toks)
        tz2 = ve(lambda e: e.memset(cons[:, :], BIGNEG), waits=pre_toks)
        prev = None
        tmx = tz2
        for k in range(nchunks):
            w = [tz, tz2] + list(pre_toks)
            if prev is not None:
                w.append(prev)
            tl = dma(A(stream_off, 2048),
                     stream_src[k * P:(k + 1) * P, :], waits=w)
            tab = ac(lambda e: e.activation(
                out=A(abs_off, 2048), in_=A(stream_off, 2048), func=AF.Abs),
                waits=[tl])
            tcm = ve(lambda e: e.tensor_scalar(
                Ai(mask_i8_off, 2048), A(abs_off, 2048),
                scrW[:, tB_ci:tB_ci + 1], 0.0, ALU.is_gt, ALU.add,
                accum_out=scrW[:, 6:7]), waits=[tab, tmx])
            tca = ve(lambda e: e.tensor_tensor(
                scrW[:, 15:16], scrW[:, 15:16], scrW[:, 6:7], ALU.add),
                waits=[tcm])
            ty = ve(lambda e: e.scalar_tensor_tensor(
                A(stream_off, 2048), Ai(mask_i8_off, 2048), -BIGOFF,
                A(abs_off, 2048), ALU.mult, ALU.add), waits=[tca])
            tmx = ty
            nw = 2048 // g_win
            for w8 in range(nw):
                tmx = ve(lambda e, k_=k, w_=w8, nw_=nw: e.max(
                    out=cons[:, (k_ * nw_ + w_) * 8:(k_ * nw_ + w_ + 1) * 8],
                    in_=A(stream_off, 2048)[:, w_ * g_win:(w_ + 1) * g_win]),
                    waits=[tmx])
            prev = tmx
        used = nchunks * (2048 // g_win) * 8
        tcp = tmx
        tzc = ve(lambda e: e.memset(A(exw, 512), BIGNEG), waits=[tcp])
        tcp = tzc
        for w8 in range(used // 16):
            tcp = ve(lambda e, w_=w8: e.max(
                out=A(exw, 512)[:, w_ * 8:(w_ + 1) * 8],
                in_=cons[:, w_ * 16:(w_ + 1) * 16]), waits=[tcp])
        t1 = dma(cc_in.ap()[:, 0:512], A(exw, 512), waits=[tcp])
        t2 = dma(cc_in.ap()[:, 512:513], scrW[:, 15:16], waits=[tcp],
                 slow=True)
        tg = collective([list(range(NCORE))], cc_in, cc_out,
                        waits=[t1, t2] + list(cc_waits))
        tok_loads = []
        for r in range(NCORE):
            tok_loads.append(dma(A(gseg_off + r * 512, 512),
                                 cc_out.ap()[r * P:(r + 1) * P, 0:512],
                                 waits=[tg]))
            tok_loads.append(dma(m8s[:, r:r + 1],
                                 cc_out.ap()[r * P:(r + 1) * P, 512:513],
                                 waits=[tg], slow=True))
        gsegs = [(gseg_off + r * 512, 512) for r in range(NCORE)]
        return gsegs, m8s[:, 0:8], tok_loads

    def scales_from_v(v_slot, g_slot, s_col_idx, gs_col_idx, w0):
        t1 = ve(lambda e, v_=v_slot: e.tensor_scalar(
            scr[0:1, 46:47], scr[0:1, v_:v_ + 1], EPS32, None, ALU.max),
            waits=w0)
        t2 = ve(lambda e: e.reciprocal(scr[0:1, 47:48], scr[0:1, 46:47]),
                waits=[t1])
        t3 = ve(lambda e: e.tensor_scalar(
            scr[0:1, 48:49], scr[0:1, 47:48], 127.0, None, ALU.mult),
            waits=[t2])
        t4 = ve(lambda e, g_=g_slot: e.tensor_tensor(
            scr[0:1, 49:50], scr[0:1, 46:47], scr[0:1, g_:g_ + 1], ALU.mult),
            waits=[t3])
        t5 = ve(lambda e: e.tensor_scalar(
            scr[0:1, 49:50], scr[0:1, 49:50], 1.0 / 127.0, None, ALU.mult),
            waits=[t4])
        tb = bcast_rows([scr[0:1, 48:49]], scrW[:, s_col_idx:s_col_idx + 1],
                        [t5], cols=1)
        return bcast_rows([scr[0:1, 49:50]],
                          scrW[:, gs_col_idx:gs_col_idx + 1], [tb], cols=1)

    t_g1 = [dma(A(RW_A + r * 512, 512), cc_q1_o.ap()[r * P:(r + 1) * P, :],
                waits=[t_gc1, t_h0[KC - 1], t_m8, t_w1q])
            for r in range(NCORE)]
    t_m1d = lax_dance(2 ** 24, 10, 11)
    t_fb1 = fat_bisect(q1segs, 10, t_g1 + [t_m1d], RW_C)
    tA1, tB1 = band_setup(0.008, 13, 14, [t_fb1])
    g1segs, cnt1_ap, t_bc1 = band_correction(
        s_h0, KC, 128, RW_C, RW_B, XS[8], RW_B + 1024, cc_w1_i, cc_w1_o,
        RW_A, RW_B, 13, 14,
        [tA1, tB1, m23, t_hfix[KC - 1], t_fb1], [t_car_g])
    t_q1v = window_select(g1segs, cnt1_ap, 10, 12, t_bc1 + [t_fb1],
                          RW_B + 2048 if False else RW_C)
    t_s1 = scales_from_v(12, 0, 9, 10, [t_q1v, t_g1w])

    rr_rep = A(RW_B, 2048)
    t_rr = bcast_rows([st_m2(tt) for tt in range(NTT)], rr_rep,
                      rr_done + [t_q1v, t_hfix[KC - 1]])
    if debug:
        dbg_rr = nc.dram_tensor("dbg_rr", [P, 2048], F32,
                                kind="ExternalOutput").ap()
        t_rr = dma(dbg_rr, rr_rep, waits=[t_rr])
    t_ha = {}
    for do in range(KC):
        t1 = ve(lambda e, d_=do: e.tensor_tensor(
            A(XS[d_], 2048), A(XS[d_], 2048), rr_rep, ALU.mult),
            waits=[t_rr, t_hfix[do], t_dbghf.get(do)])
        t_ha[do] = ve(lambda e, d_=do: e.tensor_scalar(
            A(XS[d_], 2048), A(XS[d_], 2048), normw_c(d_), None, ALU.mult),
            waits=[t1, t_aux])
    t_ha_all = [t_ha[d] for d in range(KC)]
    if debug:
        dbg_ha = nc.dram_tensor("dbg_ha", [D, TC], F32,
                                 kind="ExternalOutput").ap()
        t_dbgha = [dma(dbg_ha[k * P:(k + 1) * P, :], A(XS[k], 2048),
                       waits=[t_ha[k]]) for k in range(KC)]
        t_ha_all = t_ha_all + t_dbgha

    t_xq = {}
    for k in range(KC):
        w = [t_s1, t_hfix[KC - 1]]
        if k:
            w.append(t_xq[k - 1])
        tl = dma(A(XS[8], 2048), s_h0[k * P:(k + 1) * P, :], waits=w)
        ti = ve(lambda e: e.tensor_scalar(
            Ai(RW_C, 2048), A(XS[8], 2048), scrW[:, 9:10], None, ALU.mult),
            waits=[tl, t_s1])
        t_xq[k] = ve(lambda e, k_=k: e.tensor_copy(
            Ab(HS[0 + k_ // 2] + (k_ % 2) * 1024, 2048), Ai(RW_C, 2048)),
            waits=[ti])

    def xq_ap(k, tt):
        return Ab(HS[0 + k // 2] + (k % 2) * 1024, 2048)[:, tt * 512:(tt + 1) * 512]

    if debug:
        dbg_xq1 = nc.dram_tensor("dbg_xq1", [D, TC], BF16,
                                 kind="ExternalOutput").ap()
        t_dbgxq = [dma(dbg_xq1[k * P:(k + 1) * P, :],
                       Ab(HS[0 + k // 2] + (k % 2) * 1024, 2048),
                       waits=[t_xq[k]]) for k in range(KC)]
        t_xq[KC - 1] = t_dbgxq[-1]

    red1q2 = A(HS[4], 4096)
    contq2 = A(HS[6], 1024)
    t_mm1_ev, t_hg, mm1_last = {}, {}, {}
    t_red2 = [None]
    for h in range(HC):
        tl = dma(Ab(RW_B + (h % 2) * 512, 1024).rearrange(
                     "p (k j) -> p k j", j=P),
                 s_w1q[:, h * P:(h + 1) * P].rearrange("(k p) j -> p k j",
                                                       p=P),
                 waits=[t_w1q, t_rr, t_ha_all[-1]] +
                 ([mm1_last[h - 2]] if h >= 2 else []))
        for k in range(KC):
            for tt in range(NTT):
                w = [tl, t_xq[KC - 1]]
                if k == 0 and h > 0:
                    w.append(t_mm1_ev[(h - 1, tt)])
                mm = te(lambda e, k_=k, t_=tt, h_=h: e.matmul(
                    pb[t_][:, :],
                    Ab(RW_B + (h_ % 2) * 512, 1024)[:, k_ * P:(k_ + 1) * P],
                    xq_ap(k_, t_), start=(k_ == 0), stop=(k_ == KC - 1)),
                    waits=w)
                if k == KC - 1:
                    t_mm1_ev[(h, tt)] = ac(
                        lambda e, t_=tt: e.activation(
                            out=A(XS[8], 2048)[:, t_ * 512:(t_ + 1) * 512],
                            in_=pb[t_][:, :], func=AF.Gelu,
                            scale=scrW[:, 10:11]),
                        waits=[tdrain(mm), t_s1] +
                        ([t_hg[h - 1], t_red2[0]] if (tt == 0 and h > 0)
                         else []))
        mm1_last[h] = TE.tok()
        tevs = [t_mm1_ev[(h, tt)] for tt in range(NTT)]
        t_hg[h] = dma(s_hg[h * P:(h + 1) * P, :], A(XS[8], 2048), waits=tevs)
        ta = ac(lambda e: e.activation(
            out=A(RW_A, 2048), in_=A(XS[8], 2048), func=AF.Abs),
            waits=tevs + [t_red2[0], t_q1v, t_xq[KC - 1]])
        for w8 in range(16):
            ta = ve(lambda e, h_=h, w_=w8: e.max(
                out=red1q2[:, h_ * P + w_ * 8:h_ * P + (w_ + 1) * 8],
                in_=A(RW_A, 2048)[:, w_ * 128:(w_ + 1) * 128]), waits=[ta])
        t_red2[0] = ta

    t_g2w = emit_gamma(w2T, H, D, (HS[6], HS[7]), 1, pre=t_xq[KC - 1],
                       first_wait=[ev3a, ev3b])
    t_w2q = emit_ternarize(w2T, s_w2q, H, D, (HS[6], HS[7]), RW_A + 2048,
                           RW_A + 3072, RW_A + 3328, 1, t_g2w, 1)

    t_c2 = (sems["dve"], max(t_red2[0][1], t_w2q[1]) if False else t_red2[0][1])
    t_c2 = t_red2[0]
    first = True
    for w8 in range(64):
        w = [t_c2] + ([t_w2q] if first else [])
        first = False
        t_c2 = ve(lambda e, w_=w8: e.max(
            out=contq2[:, w_ * 16:w_ * 16 + 8],
            in_=red1q2[:, w_ * 64:(w_ + 1) * 64]), waits=w)
        t_c2 = ve(lambda e, w_=w8: e.match_replace(
            out=red1q2[:, w_ * 64:(w_ + 1) * 64],
            in_to_replace=contq2[:, w_ * 16:w_ * 16 + 8],
            in_values=red1q2[:, w_ * 64:(w_ + 1) * 64],
            imm_value=BIGNEG), waits=[t_c2])
        t_c2 = ve(lambda e, w_=w8: e.max(
            out=contq2[:, w_ * 16 + 8:w_ * 16 + 16],
            in_=red1q2[:, w_ * 64:(w_ + 1) * 64]), waits=[t_c2])

    t_ci2 = dma(cc_q2_i.ap(), contq2, waits=[t_c2])
    t_gc2 = collective([list(range(NCORE))], cc_q2_i, cc_q2_o, waits=[t_ci2])
    q2segs = [(HS[2] + r * 1024, 1024) for r in range(NCORE)]
    t_g2 = [dma(A(HS[2] + r * 1024, 1024),
                cc_q2_o.ap()[r * P:(r + 1) * P, :],
                waits=[t_gc2, t_w2q, t_c2]) for r in range(NCORE)]

    t_m2d = lax_dance(2 ** 26, 13, 14)
    t_fb2 = fat_bisect(q2segs, 13, t_g2 + [t_m2d, t_ci2], HS[6])
    tA2, tB2 = band_setup(0.003, 13, 14, [t_fb2])
    g2segs, cnt2_ap, t_bc2 = band_correction(
        s_hg, HC, 512, HS[6], HS[7], XS[8], HS[7] + 1024, cc_w2_i, cc_w2_o,
        HS[2], HS[4], 13, 14,
        [tA2, tB2, t_fb2, t_hg[HC - 1]], [t_gc2])
    t_q2v = window_select(g2segs, cnt2_ap, 13, 15, t_bc2 + [t_fb2], HS[4])
    t_s2 = scales_from_v(15, 1, 11, 12, [t_q2v, t_g2w])

    xn16_ld2 = {}
    for k in range(KC):
        xn16_ld2[k] = dma(Ab(HS[0 + k // 2] + (k % 2) * 1024, 2048),
                          s_xn16[k * P:(k + 1) * P, :],
                          waits=[mm1_last[HC - 1], t_xq[KC - 1]] +
                          [t_mm1_ev[(HC - 1, tt)] for tt in range(NTT)])
    t_x16b = xn16_ld2[KC - 1]

    t_o = {}

    def g_evac2(do, tt, mm):
        t1 = ac(lambda e, t_=tt: e.activation(
            out=A(RW_C, 512), in_=pb[t_][:, :], func=AF.Silu),
            waits=[mm, t_q2v] + ([t_o[(do, tt - 1)]] if tt else
                                 ([t_o[(do - 1, NTT - 1)]] if do else [])))
        t2 = ve(lambda e, d_=do, t_=tt: e.tensor_tensor(
            A(RW_B + 1024, 512), A(RW_C, 512),
            A(XS[d_], 2048)[:, t_ * 512:(t_ + 1) * 512], ALU.mult),
            waits=[t1, t_ha[do]])
        t3 = ve(lambda e, d_=do, t_=tt: e.tensor_copy(
            Ab(XS[d_], 2048)[:, t_ * 512:(t_ + 1) * 512],
            A(RW_B + 1024, 512)), waits=[t2])
        t_o[(do, tt)] = t3
        return t3

    emit_projection(wgT, lambda k, tt: Ab(HS[0 + k // 2] + (k % 2) * 1024, 2048)
                    [:, tt * 512:(tt + 1) * 512],
                    t_x16b, g_evac2, RQ, pre=t_x16b)
    t_o_all = [t_o[(d, t)] for d in range(KC) for t in range(NTT)]
    t_o_last = t_o_all[-1]

    def o_ap(k, tt):
        return Ab(XS[k], 2048)[:, tt * 512:(tt + 1) * 512]

    t_sat = {}

    def wo_evac(do, tt, mm):
        t1 = ac(lambda e, t_=tt: e.activation(
            out=A(RW_C, 512), in_=pb[t_][:, :], func=AF.Copy),
            waits=[mm] + ([t_sat[(do, tt - 1)]] if tt else
                          ([t_sat[(do - 1, NTT - 1)]] if do else [])))
        tok = dma(s_sat[do * P:(do + 1) * P, tt * 512:(tt + 1) * 512],
                  A(RW_C, 512), waits=[t1])
        t_sat[(do, tt)] = tok
        return tok

    emit_projection(woT, o_ap, t_o_last, wo_evac, RQ, pre=t_o_last)
    t_sat_all = [t_sat[(d, t)] for d in range(KC) for t in range(NTT)]

    def hq_ap(h, tt):
        return Ab(XS[0 + h // 8] + (h % 8) * 256, 512)

    t_out_st = []
    t_hq = {}
    t_ti = {}
    t_c1 = {}
    t_mm2ev = {}
    w2strips = [XS[8], HS[0]]
    t_w2s = {}
    t_comb = {}
    for tt in range(NTT):
        for h in range(HC):
            w = [t_s2, t_hg[h], t_sat_all[-1]]
            if tt > 0:
                w.append(t_mm2ev[(tt - 1, KC - 1)])
            prev_ti = t_ti.get((h - 2, tt))
            if prev_ti is not None:
                w.append(prev_ti)
            hqstg = RW_A + (h % 2) * 512
            tl = dma(A(hqstg, 512),
                     s_hg[h * P:(h + 1) * P, tt * 512:(tt + 1) * 512],
                     waits=w)
            ti = ve(lambda e, s_=hqstg: e.tensor_scalar(
                Ai(RW_B, 512), A(s_, 512), scrW[:, 11:12], None, ALU.mult),
                waits=[tl, t_s2])
            t_ti[(h, tt)] = ti
            t_hq[(h, tt)] = ve(lambda e, h_=h, t_=tt: e.tensor_copy(
                hq_ap(h_, t_), Ai(RW_B, 512)), waits=[ti])
        hq_done = t_hq[(HC - 1, tt)]
        for dd in range(KC):
            sidx = dd % 2
            w = [t_w2q, hq_done]
            if (tt, dd - 2) in t_mm2ev:
                w.append(t_mm2ev[(tt, dd - 2)])
            if tt > 0 and dd < 2:
                w.append(t_mm2ev[(tt - 1, KC - 2 + dd)])
            if tt == 0 and dd < 2:
                w.append(t_sat_all[-1])
                w.append(t_xq[KC - 1])
            tl2 = dma(Ab(w2strips[sidx], 4096).rearrange(
                          "p (k j) -> p k j", j=P),
                      s_w2q[:, dd * P:(dd + 1) * P].rearrange(
                          "(k p) j -> p k j", p=P), waits=w)
            bank = dd % 2
            for hk in range(HC):
                mm = te(lambda e, hk_=hk, b_=bank, s_=sidx, t_=tt: e.matmul(
                    pb[b_][:, :],
                    Ab(w2strips[s_], 4096)[:, hk_ * P:(hk_ + 1) * P],
                    hq_ap(hk_, t_), start=(hk_ == 0), stop=(hk_ == HC - 1)),
                    waits=[tl2, hq_done] +
                    ([t_mm2ev[(tt, dd - 2)]] if (tt, dd - 2) in t_mm2ev
                     else []))
            wg_ = [tdrain(mm), t_s2]
            prev_c1 = t_c1.get((tt, dd - 1)) or (
                t_c1.get((tt - 1, KC - 1)) if dd == 0 else None)
            if prev_c1 is not None:
                wg_.append(prev_c1)
            tg_ = ac(lambda e, b_=bank: e.activation(
                out=A(RW_C, 512), in_=pb[b_][:, :], func=AF.Silu,
                scale=scrW[:, 12:13]), waits=wg_)
            t_mm2ev[(tt, dd)] = tg_
            prev_cmb = t_comb.get((tt, dd - 1)) or \
                t_comb.get((tt - 1, KC - 1)) or t_sat_all[-1]
            txl = dma(A(HS[1], 512), xs[dd * P:(dd + 1) * P,
                                        tt * 512:(tt + 1) * 512],
                      waits=[prev_cmb])
            tsl = dma(A(HS[1] + 512, 512),
                      s_sat[dd * P:(dd + 1) * P, tt * 512:(tt + 1) * 512],
                      waits=[t_sat[(dd, tt)], prev_cmb])
            tc1 = ve(lambda e: e.tensor_tensor(
                A(HS[1] + 1024, 512), A(RW_C, 512), A(HS[1] + 512, 512),
                ALU.mult), waits=[tg_, tsl] +
                ([t_out_st[-1]] if t_out_st else []))
            t_c1[(tt, dd)] = tc1
            tc2 = ve(lambda e: e.tensor_tensor(
                A(HS[1] + 1024, 512), A(HS[1] + 1024, 512), A(HS[1], 512),
                ALU.add), waits=[tc1, txl])
            t_comb[(tt, dd)] = tc2
            t_out_st.append(dma(out[dd * P:(dd + 1) * P,
                                    tt * 512:(tt + 1) * 512],
                                A(HS[1] + 1024, 512), waits=[tc2]))

    if debug:
        dbg_xq2 = nc.dram_tensor("dbg_xq2", [D, TC], BF16, kind="ExternalOutput").ap()
        for k in range(KC):
            off_ = 18432 + (k // 2) * 2048 + (k % 2) * 1024
            dma(dbg_xq2[k * P:(k + 1) * P, :],
                arena[:, off_:off_ + 1024].bitcast(BF16),
                waits=[mm1_last[HC - 1], t_hg[HC - 1]])
        dbgc1 = nc.dram_tensor("dbgc1", [P, 512], F32, kind="ExternalOutput").ap()
        t_out_st.append(dma(dbgc1, contq1, waits=[t_q1v]))
        dbgc2 = nc.dram_tensor("dbgc2", [P, 1024], F32, kind="ExternalOutput").ap()
        t_out_st.append(dma(dbgc2, contq2, waits=[t_q2v]))
        dbg = nc.dram_tensor("dbg", [2, 64], F32, kind="ExternalOutput").ap()
        t_out_st.append(dma(dbg[0:1, :], scr[0:1, :], waits=[t_comb[(NTT - 1, KC - 1)]]))
        t_out_st.append(dma(dbg[1:2, 0:16], scrW[0:1, 0:16], waits=[t_comb[(NTT - 1, KC - 1)]]))
    def fin(e):
        e.wait_ge(sems["dma"], t_out_st[-1][1])
    GP.ops.append(fin)

    with nc.Block() as block:
        @block.sync
        def _(sync):
            for op in SY.ops:
                op(sync)

        @block.tensor
        def _(eng):
            for op in TE.ops:
                op(eng)

        @block.scalar
        def _(eng):
            for op in SC_.ops:
                op(eng)

        @block.vector
        def _(eng):
            for op in VE.ops:
                op(eng)

        @block.gpsimd
        def _(eng):
            for op in GP.ops:
                op(eng)

    ctx.close()
    return nc


_NC_CACHE = [None]


def kernel(L0_attn_x, L0_attn_Wi, L0_attn_Wf, L0_attn_Wg, L0_attn_Wo,
           L0_attn_norm_w, L0_ffn_ln_g, L0_ffn_ln_b, L0_ffn_w1, L0_ffn_w2):
    x = np.asarray(L0_attn_x, dtype=np.float32)
    wi = np.ascontiguousarray(np.asarray(L0_attn_Wi, np.float32).T)
    wf = np.ascontiguousarray(np.asarray(L0_attn_Wf, np.float32).T)
    wg = np.ascontiguousarray(np.asarray(L0_attn_Wg, np.float32).T)
    wo = np.ascontiguousarray(np.asarray(L0_attn_Wo, np.float32).T)
    w1 = np.ascontiguousarray(np.asarray(L0_ffn_w1, np.float32).T)
    w2 = np.ascontiguousarray(np.asarray(L0_ffn_w2, np.float32).T)
    nw = np.asarray(L0_attn_norm_w, np.float32)
    lg = np.asarray(L0_ffn_ln_g, np.float32)
    lb = np.asarray(L0_ffn_ln_b, np.float32)

    in_maps = []
    for c in range(NCORE):
        b, r = c // 4, c % 4
        xsl = np.ascontiguousarray(x[b, r * TC:(r + 1) * TC, :].T)
        auxm = np.zeros((P, 32), np.float32)
        auxm[:, 0:8] = nw.reshape(KC, P).T
        auxm[:, 8:16] = lg.reshape(KC, P).T
        auxm[:, 16:24] = lb.reshape(KC, P).T
        for q in range(3):
            auxm[:, 24 + q] = 1.0 if q < r else 0.0
        auxm[:, 27] = np.float32(1e-5)
        auxm[:, 28] = np.float32(0.995)
        in_maps.append({"x": xsl, "wiT": wi, "wfT": wf, "wgT": wg, "woT": wo,
                        "w1T": w1, "w2T": w2, "aux": auxm})

    if _NC_CACHE[0] is None:
        _NC_CACHE[0] = build_program()
    res = run_bass_kernel_spmd(_NC_CACHE[0], in_maps, core_ids=list(range(NCORE)))
    y = np.empty((B, T, D), np.float32)
    for c in range(NCORE):
        b, r = c // 4, c % 4
        y[b, r * TC:(r + 1) * TC, :] = res.results[c]["out"].T
    return y


# version 10
# speedup vs baseline: 1.1282x; 1.0778x over previous
import contextlib
import numpy as np

import concourse.bass as bass
import concourse.mybir as mybir
from concourse.bass_utils import run_bass_kernel_spmd

F32 = mybir.dt.float32
BF16 = mybir.dt.bfloat16
I8 = mybir.dt.int8
I32 = mybir.dt.int32
AF = mybir.ActivationFunctionType
ALU = mybir.AluOpType
AXX = mybir.AxisListType.X

P, D, H = 128, 1024, 4096
B, T, NCORE = 2, 8192, 8
TC = (B * T) // NCORE
KC, HC, NTT = D // P, H // P, 4
BIGNEG = -1.0e30
BIGOFF = 1.0e18
EPS32 = float(np.float32(1e-5))
FAT_IT = 12
LOC_IT = 18


class Eng:
    def __init__(self, name, chain=True, lag=1):
        self.name, self.ops, self.n, self.sem = name, [], 0, None
        self.chain = chain
        self.lag = lag

    def do(self, fn, waits=(), inc=True, chain=None):
        waits = tuple(w for w in waits if w is not None)
        prev, me = self.n - (self.lag - 1), self
        ch = self.chain if chain is None else chain

        def op(e):
            if prev > 0 and ch:
                e.wait_ge(me.sem, prev)
            for (s, v) in waits:
                if s is not me.sem:
                    e.wait_ge(s, v)
            ins = fn(e)
            if inc:
                ins.then_inc(me.sem, 1)
        self.ops.append(op)
        if inc:
            self.n += 1
        return (self.sem, self.n)

    def tok(self):
        return (self.sem, self.n)


def build_program(debug=False):
    kw = {"kind": "ExternalOutput"} if debug else {}
    nc = bass.Bass("TRN2", target_bir_lowering=False, debug=False)

    xs = nc.dram_tensor("x", [D, TC], F32, kind="ExternalInput").ap()
    wiT = nc.dram_tensor("wiT", [D, D], F32, kind="ExternalInput").ap()
    wfT = nc.dram_tensor("wfT", [D, D], F32, kind="ExternalInput").ap()
    wgT = nc.dram_tensor("wgT", [D, D], F32, kind="ExternalInput").ap()
    woT = nc.dram_tensor("woT", [D, D], F32, kind="ExternalInput").ap()
    w1T = nc.dram_tensor("w1T", [D, H], F32, kind="ExternalInput").ap()
    w2T = nc.dram_tensor("w2T", [H, D], F32, kind="ExternalInput").ap()
    aux = nc.dram_tensor("aux", [P, 32], F32, kind="ExternalInput").ap()
    out = nc.dram_tensor("out", [D, TC], F32, kind="ExternalOutput").ap()

    s_h0 = nc.dram_tensor("s_h0", [D, TC], F32, **kw).ap()
    s_xn16 = nc.dram_tensor("s_xn16", [D, TC], BF16, **kw).ap()
    s_P = nc.dram_tensor("s_P", [D, TC], F32, **kw).ap()
    s_si = nc.dram_tensor("s_si", [D, TC], F32).ap()
    s_w1q = nc.dram_tensor("s_w1q", [D, H], BF16, **kw).ap()
    s_w2q = nc.dram_tensor("s_w2q", [H, D], BF16).ap()
    s_hg = nc.dram_tensor("s_hg", [H, TC], F32, **kw).ap()
    s_sat = nc.dram_tensor("s_sat", [D, TC], F32, **kw).ap()
    s_bnc = nc.dram_tensor("s_bnc", [1, 65536], F32).ap()
    s_cons = nc.dram_tensor("s_cons", [P, 256], F32).ap()

    cc_car_i = nc.dram_tensor("cc_car_i", [P, 16], F32)
    cc_car_o = nc.dram_tensor("cc_car_o", [4 * P, 16], F32)
    cc_q1_i = nc.dram_tensor("cc_q1_i", [P, 512], F32)
    cc_q1_o = nc.dram_tensor("cc_q1_o", [NCORE * P, 512], F32)
    cc_q2_i = nc.dram_tensor("cc_q2_i", [P, 1024], F32)
    cc_q2_o = nc.dram_tensor("cc_q2_o", [NCORE * P, 1024], F32)
    cc_w1_i = nc.dram_tensor("cc_w1_i", [P, 520], F32)
    cc_w1_o = nc.dram_tensor("cc_w1_o", [NCORE * P, 520], F32)
    cc_w2_i = nc.dram_tensor("cc_w2_i", [P, 520], F32)
    cc_w2_o = nc.dram_tensor("cc_w2_o", [NCORE * P, 520], F32)

    ctx = contextlib.ExitStack()

    def sb(name, shape, dt=F32):
        return ctx.enter_context(nc.sbuf_tensor(name, shape, dt))

    ARC = 43008
    arena = sb("arena", [P, ARC])
    auxT = sb("auxT", [P, 32])
    onesD = sb("onesD", [P, 1])
    zcol = sb("zcol", [P, 1])
    fbT = sb("fbT", [1, 1], BF16)
    S0 = sb("S0", [P, 512])
    S1 = sb("S1", [P, 512])
    M0 = sb("M0", [P, 512])
    M1 = sb("M1", [P, 512])
    SS = sb("SS", [P, 512])
    scr = sb("scr", [1, 64])
    scrI = sb("scrI", [1, 8], I32)
    scrU = sb("scrU", [1, 16], mybir.dt.uint8)
    cmu8 = sb("cmu8", [P, 4], mybir.dt.uint8)
    scrW = sb("scrW", [P, 16])
    cons = sb("cons", [P, 2048])
    brow = sb("brow", [1, 256])
    m8s = sb("m8s", [P, 8])
    carry = sb("carry", [P, 96])
    stg = sb("stg", [32, 64])
    ex2 = sb("ex2", [P, 256])

    pb = [ctx.enter_context(nc.psum_tensor(f"pb{i}", [P, 512], F32))
          for i in range(8)]
    sems = {k: ctx.enter_context(nc.semaphore("sem_" + k))
            for k in ("dma", "pe", "act", "dve", "pool", "cc")}

    SY, TE, SC_, VE, GP = (Eng(n, chain=(n != "tensor"))
                           for n in ("sync", "tensor", "scalar",
                                     "vector", "gpsimd"))
    TE.sem, SC_.sem, VE.sem, GP.sem, SY.sem = (
        sems["pe"], sems["act"], sems["dve"], sems["pool"], sems["dma"])

    ndma = [0]

    def dma(dst, src, waits=(), slow=False):
        waits = tuple(w for w in waits if w is not None)

        def op(e):
            for (s, v) in waits:
                e.wait_ge(s, v)
            e.dma_start(out=dst, in_=src,
                        allow_slow_non_contiguous=slow).then_inc(
                sems["dma"], 16)
        SY.ops.append(op)
        ndma[0] += 1
        return (sems["dma"], 16 * ndma[0])

    ncc = [0]

    def collective(groups, in_t, out_t, waits=()):
        waits = tuple(w for w in waits if w is not None)

        def op(e):
            for (s, v) in waits:
                e.wait_ge(s, v)
            e.collective_compute("AllGather", ALU.bypass, replica_groups=groups,
                                 ins=[in_t.ap().opt()], outs=[out_t.ap().opt()]
                                 ).then_inc(sems["cc"], 1)
        GP.ops.append(op)
        ncc[0] += 1
        return (sems["cc"], ncc[0])

    def A(off, cols):
        assert 0 <= off and off + cols <= ARC, f"arena [{off},{off + cols})"
        return arena[:, off:off + cols]

    def Ab(off, bf16_cols):
        return A(off, bf16_cols // 2).bitcast(BF16)

    def Ai(off, i8_cols):
        return A(off, i8_cols // 4).bitcast(I8)

    XS = [i * 2048 for i in range(9)]
    HS = [18432 + i * 2048 for i in range(8)]
    RW = 34816
    RQ = 41472
    RW_A, RW_B, RW_C = RW, RW + 4096, RW + 6144

    ve = lambda fn, waits=(): VE.do(fn, waits=waits)
    ac = lambda fn, waits=(): SC_.do(fn, waits=waits)
    te = lambda fn, waits=(), chain=None: TE.do(fn, waits=waits, chain=chain)

    def acdrain(*waits):
        return ac(lambda e: e.activation(out=scr[0:1, 62:63],
                                         in_=scr[0:1, 62:63], func=AF.Copy),
                  waits=list(waits))

    def tdrain(*waits):
        return te(lambda e: e.matmul(pb[7][64:65, 0:1], fbT[0:1, 0:1],
                                     fbT[0:1, 0:1], start=True, stop=True),
                  waits=list(waits))

    t_aux = dma(auxT[:], aux)
    ve(lambda e: e.memset(onesD[:], 1.0 / D))
    ve(lambda e: e.memset(fbT[:], 0.0))
    ve(lambda e: e.memset(zcol[:], 0.0))
    ve(lambda e: e.memset(scr[:], 0.0))
    ve(lambda e: e.memset(scrW[:], 0.0))
    ve(lambda e: e.memset(carry[:], 0.0))
    ve(lambda e: e.memset(stg[:], 0.0))
    ve(lambda e: e.memset(cons[:], BIGNEG))
    ve(lambda e: e.memset(ex2[:], BIGNEG))
    ve(lambda e: e.memset(m8s[:], 0.0))
    t_cm = ve(lambda e: e.tensor_copy(cmu8[:, 0:3], auxT[:, 24:27]),
              waits=[t_aux])
    t_init = VE.tok()

    normw_c = lambda k: auxT[:, k:k + 1]
    lng_c = lambda k: auxT[:, 8 + k:8 + k + 1]
    lnb_c = lambda k: auxT[:, 16 + k:16 + k + 1]
    cmask_c = lambda q: auxT[:, 24 + q:24 + q + 1]
    eps_col = auxT[:, 27:28]
    q995_c = auxT[0:1, 28:29]

    def st_mu(tt):
        S = S0 if tt < 3 else S1
        r = 32 * tt if tt < 3 else 0
        return S[r:r + 1, :]

    def st_m2(tt):
        M = M0 if tt < 3 else M1
        r = 32 * tt if tt < 3 else 0
        return M[r:r + 1, :]

    def emit_mean_stats(src_ap_fn, src_tok, sq_off):
        last_mu = last_m2 = None
        tq = None
        for k in range(KC):
            w = [src_tok[k]]
            if last_m2 is not None:
                w.append(last_m2)
            tq = ac(lambda e, k_=k, s_=sq_off: e.activation(
                out=A(s_, 2048), in_=src_ap_fn(k_), func=AF.Square), waits=w)
            for tt in range(NTT):
                bmu, rmu = (4, 32 * tt) if tt < 3 else (5, 0)
                last_mu = te(lambda e, k_=k, t_=tt, b_=bmu, r_=rmu: e.matmul(
                    pb[b_][r_:r_ + 1, :], onesD[:],
                    src_ap_fn(k_)[:, t_ * 512:(t_ + 1) * 512],
                    start=(k_ == 0), stop=(k_ == KC - 1)),
                    waits=[src_tok[k], t_init])
                bm2, rm2 = (6, 32 * tt) if tt < 3 else (7, 0)
                last_m2 = te(lambda e, k_=k, t_=tt, b_=bm2, r_=rm2, s_=sq_off:
                             e.matmul(pb[b_][r_:r_ + 1, :], onesD[:],
                                      A(s_, 2048)[:, t_ * 512:(t_ + 1) * 512],
                                      start=(k_ == 0), stop=(k_ == KC - 1)),
                             waits=[tq])
        return last_mu, last_m2

    def evac_stats(tok_mu, tok_m2):
        toks = [ve(lambda e: e.tensor_copy(S0[0:96, :], pb[4][0:96, :]),
                   waits=[tok_mu]),
                ve(lambda e: e.tensor_copy(S1[0:32, :], pb[5][0:32, :]),
                   waits=[tok_mu]),
                ve(lambda e: e.tensor_copy(M0[0:96, :], pb[6][0:96, :]),
                   waits=[tok_m2]),
                ve(lambda e: e.tensor_copy(M1[0:32, :], pb[7][0:32, :]),
                   waits=[tok_m2])]
        return toks

    def coef_seq(w0):
        done = []
        for (Smu, Sm2, r) in ((S0, M0, 96), (S1, M1, 32)):
            t1 = ve(lambda e, m_=Smu, r_=r: e.tensor_tensor(
                SS[0:r_, :], m_[0:r_, :], m_[0:r_, :], ALU.mult), waits=w0)
            t2 = ve(lambda e, v_=Sm2, r_=r: e.tensor_tensor(
                v_[0:r_, :], v_[0:r_, :], SS[0:r_, :], ALU.subtract),
                waits=[t1])
            t3 = ac(lambda e, v_=Sm2, r_=r: e.activation(
                out=v_[0:r_, :], in_=v_[0:r_, :], func=AF.Ln,
                bias=eps_col[0:r_, :]), waits=[t2, t_aux])
            t4 = ac(lambda e, v_=Sm2, r_=r: e.activation(
                out=v_[0:r_, :], in_=v_[0:r_, :], func=AF.Exp, scale=-0.5),
                waits=[t3])
            t5 = ve(lambda e, m_=Smu, v_=Sm2, r_=r: e.tensor_tensor(
                m_[0:r_, :], m_[0:r_, :], v_[0:r_, :], ALU.mult), waits=[t4])
            t6 = ve(lambda e, m_=Smu, r_=r: e.tensor_scalar(
                m_[0:r_, :], m_[0:r_, :], -1.0, None, ALU.mult), waits=[t5])
            done.append(t6)
        return done

    nbnc = [0]

    def bcast_rows(rows, dst_ap, w0, cols=512):
        off = nbnc[0]
        nbnc[0] += max(len(rows) * cols, 8)
        assert nbnc[0] <= 32768
        toks = [dma(s_bnc[0:1, off + i * cols: off + (i + 1) * cols], r,
                    waits=w0) for i, r in enumerate(rows)]
        return dma(dst_ap, s_bnc[0:1, off:off + len(rows) * cols]
                   .to_broadcast([P, len(rows) * cols]), waits=toks)

    t_x = [dma(A(XS[k], 2048), xs[k * P:(k + 1) * P, :], waits=[t_init])
           for k in range(KC)]

    mu1, m21 = emit_mean_stats(lambda k: A(XS[k], 2048), t_x, XS[8])
    fst1 = tdrain(mu1, m21)
    ev1 = evac_stats(fst1, fst1)
    cf1 = coef_seq(ev1)
    a1_rep, b1_rep = A(RW_A, 2048), A(RW_A + 2048, 2048)
    t_a1 = bcast_rows([st_m2(tt) for tt in range(NTT)], a1_rep, cf1)
    t_b1 = bcast_rows([st_mu(tt) for tt in range(NTT)], b1_rep, cf1)

    t_xn, t_x16 = {}, {}
    for k in range(KC):
        t1 = ve(lambda e, k_=k: e.tensor_tensor(
            A(XS[k_], 2048), A(XS[k_], 2048), a1_rep, ALU.mult),
            waits=[t_a1, t_x[k]])
        t2 = ve(lambda e, k_=k: e.tensor_tensor(
            A(XS[k_], 2048), A(XS[k_], 2048), b1_rep, ALU.add),
            waits=[t_b1, t1])
        t_xn[k] = t2
        c16 = Ab(HS[4 + k // 2] + (k % 2) * 1024, 2048)
        t3 = ve(lambda e, c_=c16, k_=k: e.tensor_copy(c_, A(XS[k_], 2048)),
                waits=[t2])
        t_x16[k] = t3
        dma(s_xn16[k * P:(k + 1) * P, :], c16, waits=[t3])

    mu2, m22 = emit_mean_stats(lambda k: A(XS[k], 2048),
                               [t_xn[k] for k in range(KC)], XS[8])
    fst2 = tdrain(mu2, m22)
    ev2 = evac_stats(fst2, fst2)
    cf2 = coef_seq(ev2)
    a2_rep, b2_rep = A(RW_B, 2048), A(RW_A + 2048, 2048)
    t_a2 = bcast_rows([st_m2(tt) for tt in range(NTT)], a2_rep,
                      cf2 + [t_b1, t_xn[KC - 1]])
    t_b2 = bcast_rows([st_mu(tt) for tt in range(NTT)], b2_rep, cf2)

    red1q1 = A(RQ, 1024)
    contq1 = A(RQ + 1024, 512)
    t_h0, t_m8 = {}, None
    for k in range(KC):
        hwrk = A(XS[8], 2048)
        w = [t_a2, t_b2, t_xn[k], t_x16[k]]
        if k > 0:
            w += [t_m8, t_h0[k - 1]]
        t1 = ve(lambda e, k_=k, h_=hwrk: e.tensor_tensor(
            h_, A(XS[k_], 2048), a2_rep, ALU.mult), waits=w)
        t2 = ve(lambda e, h_=hwrk: e.tensor_tensor(h_, h_, b2_rep, ALU.add),
                waits=[t1])
        t3 = ve(lambda e, k_=k, h_=hwrk: e.tensor_scalar(
            h_, h_, lng_c(k_), lnb_c(k_), ALU.mult, ALU.add),
            waits=[t2, t_aux])
        t_h0[k] = t3
        dma(s_h0[k * P:(k + 1) * P, :], hwrk, waits=[t3])
        ta = ac(lambda e, k_=k, h_=hwrk: e.activation(
            out=A(XS[k_], 2048), in_=h_, func=AF.Abs), waits=[t3])
        for w8 in range(16):
            ta = ve(lambda e, k_=k, w_=w8: e.max(
                out=red1q1[:, k_ * P + w_ * 8: k_ * P + (w_ + 1) * 8],
                in_=A(XS[k_], 2048)[:, w_ * 128:(w_ + 1) * 128]), waits=[ta])
        t_m8 = ta

    t_c1 = t_m8
    for w8 in range(64):
        t_c1 = ve(lambda e, w_=w8: e.max(
            out=contq1[:, w_ * 8:(w_ + 1) * 8],
            in_=red1q1[:, w_ * 16:(w_ + 1) * 16]), waits=[t_c1])

    t_ci1 = dma(cc_q1_i.ap(), contq1, waits=[t_c1])
    t_gc1 = collective([list(range(NCORE))], cc_q1_i, cc_q1_o, waits=[t_ci1])
    q1segs = [(RW_A + r * 512, 512) for r in range(NCORE)]

    def emit_projection(wsrc, rhs_ap_fn, rhs_tok, evac_fn, strip_off, pre=None,
                        interleave=None, first_mm_waits=(), rhs_tok_fn=None):
        evt = {}
        last_strip = None
        for do in range(KC):
            strip = A(strip_off, 1024)
            castt = Ab(strip_off + 1024, 1024)
            tl = dma(strip.rearrange("p (k j) -> p k j", j=P),
                     wsrc[:, do * P:(do + 1) * P].rearrange(
                         "(k p) j -> p k j", p=P), waits=[pre, last_strip])
            tc_ = ve(lambda e, s_=strip, c_=castt: e.tensor_copy(c_, s_),
                     waits=[tl])
            for k in range(KC):
                for tt in range(NTT):
                    w = [tc_, rhs_tok_fn(k) if rhs_tok_fn else rhs_tok]
                    if k == 0 and do == 0:
                        w.extend(first_mm_waits)
                    if k == 0 and do > 0:
                        w.append(evt[(do - 1, tt)])
                    mm = te(lambda e, k_=k, t_=tt, c_=castt: e.matmul(
                        pb[t_][:, :], c_[:, k_ * P:(k_ + 1) * P],
                        rhs_ap_fn(k_, t_),
                        start=(k_ == 0), stop=(k_ == KC - 1)), waits=w)
                    if k == KC - 1:
                        evt[(do, tt)] = evac_fn(do, tt, tdrain(mm))
            last_strip = TE.tok()
            if interleave is not None:
                interleave(do)
        return evt

    def xn16_ap(k, tt):
        return Ab(HS[4 + k // 2] + (k % 2) * 1024, 2048)[:, tt * 512:(tt + 1) * 512]

    f_ev = {}

    def f_evac(do, tt, mm):
        tok = ac(lambda e, d_=do, t_=tt: e.activation(
            out=A(XS[d_], 2048)[:, t_ * 512:(t_ + 1) * 512],
            in_=pb[t_][:, :], func=AF.Sigmoid),
            waits=[mm, t_m8, t_h0[KC - 1]])
        f_ev[(do, tt)] = tok
        return tok

    emit_projection(wfT, xn16_ap, t_x16[KC - 1], f_evac, XS[8],
                    pre=t_h0[KC - 1])
    t_f = {do: f_ev[(do, NTT - 1)] for do in range(KC)}

    def emit_gamma(wsrc, rows, cols, strip_offs, scr_slot, pre=None,
                   first_wait=()):
        tw = 2048 if cols % 2048 == 0 else 1024
        tiles = [(rc, cc) for rc in range(rows // P) for cc in range(cols // tw)]
        lastmm = None
        for i, (rc, cc) in enumerate(tiles):
            so = strip_offs[i % 2]
            w = [t_init, pre]
            if i >= 2:
                w.append(lastmm)
            tl = dma(A(so, tw),
                     wsrc[rc * P:(rc + 1) * P, cc * tw:(cc + 1) * tw],
                     waits=w)
            tab = ac(lambda e, s_=so, tw_=tw: e.activation(
                out=A(s_, tw_), in_=A(s_, tw_), func=AF.Abs), waits=[tl])
            for q in range(tw // 512):
                lastmm = te(lambda e, s_=so, q_=q, i_=i, tw_=tw: e.matmul(
                    pb[6][0:1, :], onesD[:],
                    A(s_, tw_)[:, q_ * 512:(q_ + 1) * 512],
                    start=(i_ == 0 and q_ == 0),
                    stop=(i_ == len(tiles) - 1 and q_ == tw_ // 512 - 1)),
                    waits=[tab] + (list(first_wait) if i == 0 and q == 0
                                   else []))
        tev = ve(lambda e: e.tensor_copy(SS[0:1, :], pb[6][0:1, :]),
                 waits=[tdrain(lastmm)])
        tr = ve(lambda e: e.tensor_reduce(scr[0:1, 40:41], SS[0:1, :], AXX,
                                          ALU.add), waits=[tev])
        tg = ve(lambda e, s_=scr_slot, n_=(rows * cols // D): e.tensor_scalar(
            scr[0:1, s_:s_ + 1], scr[0:1, 40:41], 1.0 / n_, EPS32,
            ALU.mult, ALU.add), waits=[tr])
        return tg

    t_g1w = emit_gamma(w1T, D, H, (HS[0], HS[1]), 0, first_wait=ev2)

    def emit_ternarize(wsrc, dst, rows, cols, strip_offs, tmp_f, tmp_i, tmp_b,
                       g_slot, tg, rg_col_idx):
        trg = ve(lambda e, g_=g_slot: e.reciprocal(
            scr[0:1, 44:45], scr[0:1, g_:g_ + 1]), waits=[tg])
        trgb = bcast_rows([scr[0:1, 44:45]],
                          scrW[:, rg_col_idx:rg_col_idx + 1], [trg], cols=1)
        tw = 2048 if cols % 2048 == 0 else 1024
        tiles = [(rc, cc) for rc in range(rows // P) for cc in range(cols // tw)]
        lastq = None
        for i, (rc, cc) in enumerate(tiles):
            so = strip_offs[i % 2]
            w = [trgb]
            if i >= 2:
                w.append(lastq)
            tl = dma(A(so, tw),
                     wsrc[rc * P:(rc + 1) * P, cc * tw:(cc + 1) * tw],
                     waits=w)
            tmin = ve(lambda e, s_=so, tw_=tw: e.tensor_scalar(
                A(tmp_f, tw_), A(s_, tw_), scrW[:, rg_col_idx:rg_col_idx + 1],
                1.49, ALU.mult, ALU.min), waits=[tl])
            ti8 = ve(lambda e, tw_=tw: e.tensor_scalar(
                Ai(tmp_i, tw_ * 4)[:, 0:tw_], A(tmp_f, tw_), -1.49, None,
                ALU.max), waits=[tmin])
            tb = ve(lambda e, tw_=tw: e.tensor_copy(
                Ab(tmp_b, tw_ * 2)[:, 0:tw_], Ai(tmp_i, tw_ * 4)[:, 0:tw_]),
                waits=[ti8])
            lastq = dma(dst[rc * P:(rc + 1) * P, cc * tw:(cc + 1) * tw],
                        Ab(tmp_b, tw * 2)[:, 0:tw], waits=[tb])
        return lastq

    t_w1q = emit_ternarize(w1T, s_w1q, D, H, (HS[0], HS[1]), RW_B, RW_C,
                           RW_A, 0, t_g1w, 0)

    si_ev = {}
    t_hloc, t_Pst, t_scans, t_inp = {}, {}, [], {}
    HSmap = {0: HS[0], 1: HS[1], 2: HS[2], 3: HS[3], 4: HS[6], 5: HS[7],
             6: HS[4], 7: HS[5]}
    si_gate = {}

    def si_evac(do, tt, mm):
        w = [mm, t_w1q]
        if do > 0:
            w.append(si_gate[do - 1])
        tok = ac(lambda e, t_=tt: e.activation(
            out=A(XS[8], 2048)[:, t_ * 512:(t_ + 1) * 512],
            in_=pb[t_][:, :], func=AF.Silu), waits=w)
        si_ev[(do, tt)] = tok
        return tok

    def scan_chain(do):
        w = [si_ev[(do, tt)] for tt in range(NTT)] + [t_ci1]
        t2 = dma(s_si[do * P:(do + 1) * P, :], A(XS[8], 2048), waits=w)
        si_gate[do] = t2
        t_inp[do] = t2

    emit_projection(wiT, xn16_ap, t_x16[KC - 1], si_evac, RQ, pre=t_ci1,
                    interleave=scan_chain,
                    first_mm_waits=[f_ev[(KC - 1, tt)] for tt in range(NTT)])

    t_wi_done = TE.tok()
    for do in range(KC):
        tsl = dma(A(XS[8], 2048), s_si[do * P:(do + 1) * P, :],
                  waits=[t_inp[do], t_wi_done] +
                  ([t_hloc[do - 1]] if do else []))
        tu = ve(lambda e, d_=do: e.tensor_scalar(
            A(RW_B, 2048), A(XS[d_], 2048), -1.0, 1.0, ALU.mult, ALU.add),
            waits=[tsl, t_f[do]])
        ti2 = ve(lambda e, d_=do: e.tensor_tensor(
            A(HSmap[d_], 2048), A(XS[8], 2048), A(RW_B, 2048), ALU.mult),
            waits=[tu])
        t3 = ve(lambda e, d_=do: e.tensor_tensor_scan(
            A(HSmap[d_], 2048), A(XS[d_], 2048), A(HSmap[d_], 2048), 0.0,
            ALU.mult, ALU.add), waits=[ti2])
        t4 = ve(lambda e, d_=do: e.tensor_tensor_scan(
            A(RW_B, 2048), A(XS[d_], 2048), A(XS[d_], 2048), 1.0,
            ALU.mult, ALU.bypass), waits=[t3])
        t_Pst[do] = dma(s_P[do * P:(do + 1) * P, :], A(RW_B, 2048), waits=[t4])
        t5 = ve(lambda e, d_=do: e.tensor_copy(
            carry[:, d_:d_ + 1], A(RW_B, 2048)[:, 2047:2048]), waits=[t4])
        t6 = ve(lambda e, d_=do: e.tensor_copy(
            carry[:, 8 + d_:8 + d_ + 1], A(HSmap[d_], 2048)[:, 2047:2048]),
            waits=[t5])
        t_hloc[do] = t3
        t_scans.append(t6)

    t_car_i = dma(cc_car_i.ap(), carry[:, 0:16], waits=t_scans)
    t_car_g = collective([[0, 1, 2, 3], [4, 5, 6, 7]], cc_car_i, cc_car_o,
                         waits=[t_car_i])
    t_car_l = [dma(carry[:, 16 + 16 * q:32 + 16 * q],
                   cc_car_o.ap()[q * P:(q + 1) * P, :], waits=[t_car_g])
               for q in range(4)]
    tC = ve(lambda e: e.memset(carry[:, 80:96], 0.0), waits=t_car_l)
    for q in range(3):
        t1 = ve(lambda e, q_=q: e.tensor_tensor(
            carry[:, 88:96], carry[:, 16 + 16 * q_:24 + 16 * q_],
            carry[:, 80:88], ALU.mult), waits=[tC])
        t2 = ve(lambda e, q_=q: e.tensor_tensor(
            carry[:, 88:96], carry[:, 88:96],
            carry[:, 24 + 16 * q_:32 + 16 * q_], ALU.add), waits=[t1])
        tC = ve(lambda e, q_=q: e.select(
            carry[:, 80:88], cmu8[:, q_:q_ + 1].to_broadcast([P, 8]),
            carry[:, 88:96], carry[:, 80:88]), waits=[t2, t_cm])

    if debug:
        dbg_car = nc.dram_tensor("dbg_car", [P, 96], F32,
                                 kind="ExternalOutput").ap()
        t_dbgcar = dma(dbg_car, carry[:, 0:96], waits=[tC])
    t_hfix = {}
    prev = None
    for do in range(KC):
        tl = dma(A(XS[8], 2048), s_P[do * P:(do + 1) * P, :],
                 waits=[t_Pst[do], tC, prev])
        t_hfix[do] = ve(lambda e, d_=do: e.scalar_tensor_tensor(
            A(XS[d_], 2048), A(XS[8], 2048), carry[:, 80 + d_:81 + d_],
            A(HSmap[d_], 2048), ALU.mult, ALU.add), waits=[tl, tC])
        prev = t_hfix[do]

    t_dbghf = {}
    if debug:
        dbg_hf = nc.dram_tensor("dbg_hf", [D, TC], F32,
                                kind="ExternalOutput").ap()
        for k in range(KC):
            t_dbghf[k] = dma(dbg_hf[k * P:(k + 1) * P, :], A(XS[k], 2048),
                             waits=[t_hfix[k]])
    t_hfix_g = {k: t_hfix[k] for k in range(KC)}
    t_hfix_g[0] = t_g1w if t_g1w[1] > t_hfix[0][1] and t_g1w[0] is t_hfix[0][0] else t_hfix[0]
    mu3, m23 = emit_mean_stats(lambda k: A(XS[k], 2048),
                               [t_hfix[KC - 1]] * KC, XS[8])
    fst3 = tdrain(m23)
    ev3a = ve(lambda e: e.tensor_copy(M0[0:96, :], pb[6][0:96, :]), waits=[fst3])
    ev3b = ve(lambda e: e.tensor_copy(M1[0:32, :], pb[7][0:32, :]), waits=[fst3])
    if debug:
        dbg_m2 = nc.dram_tensor("dbg_m2", [P, 1024], F32,
                                kind="ExternalOutput").ap()
        tdm = dma(dbg_m2[:, 0:512], M0[:, :], waits=[ev3a, ev3b])
        tdm2 = dma(dbg_m2[:, 512:1024], M1[:, :], waits=[ev3a, ev3b])
        ev3a = tdm
        ev3b = tdm2
    rr_done = []
    for (Mt, r) in ((M0, 96), (M1, 32)):
        t3 = ac(lambda e, v_=Mt, r_=r: e.activation(
            out=v_[0:r_, :], in_=v_[0:r_, :], func=AF.Ln,
            bias=eps_col[0:r_, :]), waits=[ev3a, ev3b, t_aux])
        t4 = ac(lambda e, v_=Mt, r_=r: e.activation(
            out=v_[0:r_, :], in_=v_[0:r_, :], func=AF.Exp, scale=-0.5),
            waits=[t3])
        rr_done.append(t4)

    def lax_dance(n_val, m_slot, hw_slot):
        t0 = ve(lambda e: e.memset(scr[0:1, 30:31], float(n_val)))
        t1 = ve(lambda e: e.tensor_scalar(
            scr[0:1, 31:32], scr[0:1, 30:31], 1.0, None, ALU.subtract),
            waits=[t0])
        t2 = ve(lambda e: e.tensor_tensor(
            scr[0:1, 32:33], q995_c, scr[0:1, 31:32], ALU.mult),
            waits=[t1, t_aux])
        ti = ve(lambda e: e.tensor_copy(scrI[0:1, 0:1], scr[0:1, 32:33]),
                waits=[t2])
        tf = ve(lambda e: e.tensor_copy(scr[0:1, 33:34], scrI[0:1, 0:1]),
                waits=[ti])
        tg = ve(lambda e: e.tensor_scalar(
            scr[0:1, 35:36], scr[0:1, 33:34], scr[0:1, 32:33], None,
            ALU.is_gt), waits=[tf])
        tlow = ve(lambda e: e.tensor_tensor(
            scr[0:1, 33:34], scr[0:1, 33:34], scr[0:1, 35:36], ALU.subtract),
            waits=[tg])
        thw = ve(lambda e, h_=hw_slot: e.tensor_tensor(
            scr[0:1, h_:h_ + 1], scr[0:1, 32:33], scr[0:1, 33:34],
            ALU.subtract), waits=[tlow])
        tm = ve(lambda e: e.tensor_tensor(
            scr[0:1, 36:37], scr[0:1, 30:31], scr[0:1, 33:34], ALU.subtract),
            waits=[thw])
        return ve(lambda e, m_=m_slot: e.tensor_scalar(
            scr[0:1, m_:m_ + 1], scr[0:1, 36:37], 1.0, None, ALU.subtract),
            waits=[tm])

    nsel = [0]

    def fat_bisect(segs, m_slot, gather_toks, junk_off, iters=FAT_IT):
        base = 32768 + nsel[0] * 1024
        nsel[0] += 1
        assert base + 1024 <= 65536
        segs_ap = [A(o, c) for (o, c) in segs]
        w0 = list(gather_toks)
        tmax = ve(lambda e, s_=segs_ap[0]: e.tensor_reduce(
            scrW[:, 2:3], s_, AXX, ALU.max), waits=w0)
        for s_ in segs_ap[1:]:
            tr = ve(lambda e, ss_=s_: e.tensor_reduce(
                scrW[:, 3:4], ss_, AXX, ALU.max), waits=[tmax])
            tmax = ve(lambda e: e.tensor_tensor(
                scrW[:, 2:3], scrW[:, 2:3], scrW[:, 3:4], ALU.max), waits=[tr])
        tst = dma(s_bnc[0:1, base + 128:base + 256].rearrange("o p -> p o"),
                  scrW[:, 2:3], waits=[tmax])
        tld = dma(brow[0:1, 0:128], s_bnc[0:1, base + 128:base + 256],
                  waits=[tst])
        thi = ve(lambda e: e.tensor_reduce(scr[0:1, 50:51], brow[0:1, 0:128],
                                           AXX, ALU.max), waits=[tld])
        t0 = ve(lambda e: e.memset(scr[0:1, 51:52], 0.0), waits=[thi])
        t0 = ve(lambda e: e.tensor_copy(scr[0:1, 52:53], scr[0:1, 50:51]),
                waits=[t0])
        t0 = ve(lambda e, m_=m_slot: e.tensor_scalar(
            scr[0:1, 55:56], scr[0:1, m_:m_ + 1], 1.0, None, ALU.add),
            waits=[t0])
        tcnt = t0
        for it in range(iters):
            tcnt = ve(lambda e: e.tensor_scalar(
                scr[0:1, 53:54], scr[0:1, 51:52], scr[0:1, 52:53], 0.5,
                ALU.add, ALU.mult), waits=[tcnt])
            tb1 = dma(s_bnc[0:1, base:base + 1], scr[0:1, 53:54], waits=[tcnt])
            tb2 = dma(scrW[:, 4:5],
                      s_bnc[0:1, base:base + 1].to_broadcast([P, 1]),
                      waits=[tb1])
            tacc = ve(lambda e: e.memset(scrW[:, 5:6], 0.0), waits=[tb2])
            for si, s_ in enumerate(segs_ap):
                tpc = ve(lambda e, ss_=s_, jo_=junk_off: e.tensor_scalar(
                    Ai(jo_, ss_.shape[1]), ss_, scrW[:, 4:5], 0.0,
                    ALU.is_gt, ALU.add, accum_out=scrW[:, 6:7]),
                    waits=[tacc])
                tacc = ve(lambda e: e.tensor_tensor(
                    scrW[:, 5:6], scrW[:, 5:6], scrW[:, 6:7], ALU.add),
                    waits=[tpc])
            tb3 = dma(s_bnc[0:1, base + 384:base + 512]
                      .rearrange("o p -> p o"), scrW[:, 5:6], waits=[tacc])
            tb4 = dma(brow[0:1, 128:256],
                      s_bnc[0:1, base + 384:base + 512], waits=[tb3])
            tc = ve(lambda e: e.tensor_reduce(
                scr[0:1, 54:55], brow[0:1, 128:256], AXX, ALU.add),
                waits=[tb4])
            tg = ve(lambda e: e.tensor_scalar(
                scrU[0:1, 0:1], scr[0:1, 54:55], scr[0:1, 55:56], None,
                ALU.is_ge), waits=[tc])
            tl_ = ve(lambda e: e.select(scr[0:1, 51:52], scrU[0:1, 0:1],
                                        scr[0:1, 53:54], scr[0:1, 51:52]),
                     waits=[tg])
            tcnt = ve(lambda e: e.select(scr[0:1, 52:53], scrU[0:1, 0:1],
                                         scr[0:1, 52:53], scr[0:1, 53:54]),
                      waits=[tl_])
        return tcnt

    def band_setup(eps, tA_col_idx, tB_col_idx, w0):
        base = 32768 + nsel[0] * 1024
        nsel[0] += 1
        t1 = ve(lambda e: e.tensor_scalar(
            scr[0:1, 19:20], scr[0:1, 52:53], 1.0 - eps, None, ALU.mult),
            waits=w0)
        t2 = ve(lambda e: e.tensor_scalar(
            scr[0:1, 20:21], scr[0:1, 52:53], 1.0 + eps, None, ALU.mult),
            waits=[t1])
        tb1 = dma(s_bnc[0:1, base:base + 2], scr[0:1, 19:21], waits=[t2])
        ta_ = dma(scrW[:, tA_col_idx:tA_col_idx + 1],
                  s_bnc[0:1, base:base + 1].to_broadcast([P, 1]), waits=[tb1])
        tb_ = dma(scrW[:, tB_col_idx:tB_col_idx + 1],
                  s_bnc[0:1, base + 1:base + 2].to_broadcast([P, 1]),
                  waits=[tb1])
        return ta_, tb_

    def window_select(gath_segs, cnt_cols_ap, m_slot, out_slot, w0,
                      work_off):
        base = 32768 + nsel[0] * 1024
        nsel[0] += 1
        t1 = ve(lambda e: e.tensor_reduce(scrW[:, 5:6], cnt_cols_ap, AXX,
                                          ALU.add), waits=w0)
        tb3 = dma(s_bnc[0:1, base + 384:base + 512].rearrange("o p -> p o"),
                  scrW[:, 5:6], waits=[t1])
        tb4 = dma(brow[0:1, 128:256], s_bnc[0:1, base + 384:base + 512],
                  waits=[tb3])
        tcb = ve(lambda e: e.tensor_reduce(scr[0:1, 21:22],
                                           brow[0:1, 128:256], AXX, ALU.add),
                 waits=[tb4])
        tex = ve(lambda e: e.memset(A(work_off, 1024), BIGNEG), waits=w0)
        exo = [0]
        for (o, c) in gath_segs:
            for w8 in range(c // 64):
                d_ = exo[0]
                tex = ve(lambda e, o_=o, w_=w8, dd_=d_, c_=c: e.max(
                    out=A(work_off, 1024)[:, dd_:dd_ + 8],
                    in_=A(o_, c_)[:, w_ * 64:(w_ + 1) * 64]), waits=[tex])
                exo[0] += 8
        assert exo[0] <= 512
        t2x = ve(lambda e: e.memset(ex2[:], BIGNEG), waits=[tex])
        n2 = exo[0] // 16
        for w8 in range(n2):
            t2x = ve(lambda e, w_=w8: e.max(
                out=ex2[:, w_ * 8:(w_ + 1) * 8],
                in_=A(work_off, 1024)[:, w_ * 16:(w_ + 1) * 16]), waits=[t2x])
        tcs = dma(s_cons[:, 0:256], ex2[:], waits=[t2x])
        tcl = dma(cons[0:32, 0:1024],
                  s_cons.rearrange("(b a) c -> b (a c)", b=32), waits=[tcs])
        t0 = ve(lambda e: e.tensor_copy(scr[0:1, 57:58], scr[0:1, 19:20]),
                waits=[tcl, tcb])
        t0 = ve(lambda e: e.tensor_copy(scr[0:1, 58:59], scr[0:1, 20:21]),
                waits=[t0])
        t0 = ve(lambda e, m_=m_slot: e.tensor_scalar(
            scr[0:1, 55:56], scr[0:1, m_:m_ + 1], 1.0, None, ALU.add),
            waits=[t0])
        tloc = t0
        for it in range(LOC_IT):
            tloc = ve(lambda e: e.tensor_scalar(
                scr[0:1, 59:60], scr[0:1, 57:58], scr[0:1, 58:59], 0.5,
                ALU.add, ALU.mult), waits=[tloc])
            tr1 = ve(lambda e: e.tensor_copy(
                stg[0:1, 0:32], scr[0:1, 59:60].to_broadcast([1, 32])),
                waits=[tloc])
            tr2 = ve(lambda e: e.transpose(stg[0:32, 32:64], stg[0:32, 0:32]),
                     waits=[tr1])
            tpc = ve(lambda e: e.tensor_scalar(
                cons[0:32, 1024:2048], cons[0:32, 0:1024], stg[0:32, 32:33],
                0.0, ALU.is_gt, ALU.add, accum_out=stg[0:32, 33:34]),
                waits=[tr2])
            tr3 = ve(lambda e: e.tensor_copy(
                stg[0:32, 0:32], stg[0:32, 33:34].to_broadcast([32, 32])),
                waits=[tpc])
            tr4 = ve(lambda e: e.transpose(stg[0:32, 32:64], stg[0:32, 0:32]),
                     waits=[tr3])
            tr5 = ve(lambda e: e.tensor_reduce(
                scr[0:1, 60:61], stg[0:1, 32:64], AXX, ALU.add), waits=[tr4])
            tr6 = ve(lambda e: e.tensor_scalar(
                scr[0:1, 60:61], scr[0:1, 60:61], scr[0:1, 21:22], None,
                ALU.add), waits=[tr5])
            tg = ve(lambda e: e.tensor_scalar(
                scrU[0:1, 1:2], scr[0:1, 60:61], scr[0:1, 55:56], None,
                ALU.is_ge), waits=[tr6])
            tl_ = ve(lambda e: e.select(scr[0:1, 57:58], scrU[0:1, 1:2],
                                        scr[0:1, 59:60], scr[0:1, 57:58]),
                     waits=[tg])
            tloc = ve(lambda e: e.select(scr[0:1, 58:59], scrU[0:1, 1:2],
                                         scr[0:1, 58:59], scr[0:1, 59:60]),
                      waits=[tl_])
        tr1 = ve(lambda e: e.tensor_copy(
            stg[0:1, 0:32], scr[0:1, 58:59].to_broadcast([1, 32])),
            waits=[tloc])
        tr2 = ve(lambda e: e.transpose(stg[0:32, 32:64], stg[0:32, 0:32]),
                 waits=[tr1])
        tm1 = ve(lambda e: e.tensor_scalar(
            cons[0:32, 1024:2048], cons[0:32, 0:1024], stg[0:32, 32:33],
            -BIGOFF, ALU.is_gt, ALU.mult), waits=[tr2])
        tm2 = ve(lambda e: e.tensor_tensor(
            cons[0:32, 1024:2048], cons[0:32, 1024:2048], cons[0:32, 0:1024],
            ALU.add), waits=[tm1])
        tm3 = ve(lambda e: e.tensor_reduce(
            stg[0:32, 34:35], cons[0:32, 1024:2048], AXX, ALU.max),
            waits=[tm2])
        tm4 = ve(lambda e: e.tensor_copy(
            stg[0:32, 0:32], stg[0:32, 34:35].to_broadcast([32, 32])),
            waits=[tm3])
        tm5 = ve(lambda e: e.transpose(stg[0:32, 32:64], stg[0:32, 0:32]),
                 waits=[tm4])
        return ve(lambda e, o_=out_slot: e.tensor_reduce(
            scr[0:1, o_:o_ + 1], stg[0:1, 32:64], AXX, ALU.max), waits=[tm5])

    def band_correction(stream_src, nchunks, g_win, mask_i8_off, abs_off,
                        stream_off, exw, cc_in, cc_out, gseg_off, work_off,
                        tA_ci, tB_ci, pre_toks, cc_waits):
        tz = ve(lambda e: e.memset(scrW[:, 15:16], 0.0), waits=pre_toks)
        tz2 = ve(lambda e: e.memset(cons[:, :], BIGNEG), waits=pre_toks)
        prev = None
        tmx = tz2
        for k in range(nchunks):
            w = [tz, tz2] + list(pre_toks)
            if prev is not None:
                w.append(prev)
            tl = dma(A(stream_off, 2048),
                     stream_src[k * P:(k + 1) * P, :], waits=w)
            tab = ac(lambda e: e.activation(
                out=A(abs_off, 2048), in_=A(stream_off, 2048), func=AF.Abs),
                waits=[tl])
            tcm = ve(lambda e: e.tensor_scalar(
                Ai(mask_i8_off, 2048), A(abs_off, 2048),
                scrW[:, tB_ci:tB_ci + 1], 0.0, ALU.is_gt, ALU.add,
                accum_out=scrW[:, 6:7]), waits=[tab, tmx])
            tca = ve(lambda e: e.tensor_tensor(
                scrW[:, 15:16], scrW[:, 15:16], scrW[:, 6:7], ALU.add),
                waits=[tcm])
            ty = ve(lambda e: e.scalar_tensor_tensor(
                A(stream_off, 2048), Ai(mask_i8_off, 2048), -BIGOFF,
                A(abs_off, 2048), ALU.mult, ALU.add), waits=[tca])
            tmx = ty
            nw = 2048 // g_win
            for w8 in range(nw):
                tmx = ve(lambda e, k_=k, w_=w8, nw_=nw: e.max(
                    out=cons[:, (k_ * nw_ + w_) * 8:(k_ * nw_ + w_ + 1) * 8],
                    in_=A(stream_off, 2048)[:, w_ * g_win:(w_ + 1) * g_win]),
                    waits=[tmx])
            prev = tmx
        used = nchunks * (2048 // g_win) * 8
        tcp = tmx
        tzc = ve(lambda e: e.memset(A(exw, 512), BIGNEG), waits=[tcp])
        tcp = tzc
        for w8 in range(used // 16):
            tcp = ve(lambda e, w_=w8: e.max(
                out=A(exw, 512)[:, w_ * 8:(w_ + 1) * 8],
                in_=cons[:, w_ * 16:(w_ + 1) * 16]), waits=[tcp])
        t1 = dma(cc_in.ap()[:, 0:512], A(exw, 512), waits=[tcp])
        t2 = dma(cc_in.ap()[:, 512:513], scrW[:, 15:16], waits=[tcp],
                 slow=True)
        tg = collective([list(range(NCORE))], cc_in, cc_out,
                        waits=[t1, t2] + list(cc_waits))
        tok_loads = []
        for r in range(NCORE):
            tok_loads.append(dma(A(gseg_off + r * 512, 512),
                                 cc_out.ap()[r * P:(r + 1) * P, 0:512],
                                 waits=[tg]))
            tok_loads.append(dma(m8s[:, r:r + 1],
                                 cc_out.ap()[r * P:(r + 1) * P, 512:513],
                                 waits=[tg], slow=True))
        gsegs = [(gseg_off + r * 512, 512) for r in range(NCORE)]
        return gsegs, m8s[:, 0:8], tok_loads

    def scales_from_v(v_slot, g_slot, s_col_idx, gs_col_idx, w0):
        t1 = ve(lambda e, v_=v_slot: e.tensor_scalar(
            scr[0:1, 46:47], scr[0:1, v_:v_ + 1], EPS32, None, ALU.max),
            waits=w0)
        t2 = ve(lambda e: e.reciprocal(scr[0:1, 47:48], scr[0:1, 46:47]),
                waits=[t1])
        t3 = ve(lambda e: e.tensor_scalar(
            scr[0:1, 48:49], scr[0:1, 47:48], 127.0, None, ALU.mult),
            waits=[t2])
        t4 = ve(lambda e, g_=g_slot: e.tensor_tensor(
            scr[0:1, 49:50], scr[0:1, 46:47], scr[0:1, g_:g_ + 1], ALU.mult),
            waits=[t3])
        t5 = ve(lambda e: e.tensor_scalar(
            scr[0:1, 49:50], scr[0:1, 49:50], 1.0 / 127.0, None, ALU.mult),
            waits=[t4])
        tb = bcast_rows([scr[0:1, 48:49]], scrW[:, s_col_idx:s_col_idx + 1],
                        [t5], cols=1)
        return bcast_rows([scr[0:1, 49:50]],
                          scrW[:, gs_col_idx:gs_col_idx + 1], [tb], cols=1)

    t_g1 = [dma(A(RW_A + r * 512, 512), cc_q1_o.ap()[r * P:(r + 1) * P, :],
                waits=[t_gc1, t_h0[KC - 1], t_m8, t_w1q])
            for r in range(NCORE)]
    t_m1d = lax_dance(2 ** 24, 10, 11)
    t_fb1 = fat_bisect(q1segs, 10, t_g1 + [t_m1d], RW_C)
    tA1, tB1 = band_setup(0.008, 13, 14, [t_fb1])
    g1segs, cnt1_ap, t_bc1 = band_correction(
        s_h0, KC, 128, RW_C, RW_B, XS[8], RW_B + 1024, cc_w1_i, cc_w1_o,
        RW_A, RW_B, 13, 14,
        [tA1, tB1, m23, t_hfix[KC - 1], t_fb1], [t_car_g])
    t_q1v = window_select(g1segs, cnt1_ap, 10, 12, t_bc1 + [t_fb1],
                          RW_B + 2048 if False else RW_C)
    t_s1 = scales_from_v(12, 0, 9, 10, [t_q1v, t_g1w])

    rr_rep = A(RW_B, 2048)
    t_rr = bcast_rows([st_m2(tt) for tt in range(NTT)], rr_rep,
                      rr_done + [t_q1v, t_hfix[KC - 1]])
    if debug:
        dbg_rr = nc.dram_tensor("dbg_rr", [P, 2048], F32,
                                kind="ExternalOutput").ap()
        t_rr = dma(dbg_rr, rr_rep, waits=[t_rr])
    t_ha = {}
    for do in range(KC):
        t1 = ve(lambda e, d_=do: e.tensor_tensor(
            A(XS[d_], 2048), A(XS[d_], 2048), rr_rep, ALU.mult),
            waits=[t_rr, t_hfix[do], t_dbghf.get(do)])
        t_ha[do] = ve(lambda e, d_=do: e.tensor_scalar(
            A(XS[d_], 2048), A(XS[d_], 2048), normw_c(d_), None, ALU.mult),
            waits=[t1, t_aux])
    t_ha_all = [t_ha[d] for d in range(KC)]
    if debug:
        dbg_ha = nc.dram_tensor("dbg_ha", [D, TC], F32,
                                 kind="ExternalOutput").ap()
        t_dbgha = [dma(dbg_ha[k * P:(k + 1) * P, :], A(XS[k], 2048),
                       waits=[t_ha[k]]) for k in range(KC)]
        t_ha_all = t_ha_all + t_dbgha

    t_xq = {}
    for k in range(KC):
        w = [t_s1, t_hfix[KC - 1]]
        if k:
            w.append(t_xq[k - 1])
        tl = dma(A(XS[8], 2048), s_h0[k * P:(k + 1) * P, :], waits=w)
        ti = ve(lambda e: e.tensor_scalar(
            Ai(RW_C, 2048), A(XS[8], 2048), scrW[:, 9:10], None, ALU.mult),
            waits=[tl, t_s1])
        t_xq[k] = ve(lambda e, k_=k: e.tensor_copy(
            Ab(HS[0 + k_ // 2] + (k_ % 2) * 1024, 2048), Ai(RW_C, 2048)),
            waits=[ti])

    def xq_ap(k, tt):
        return Ab(HS[0 + k // 2] + (k % 2) * 1024, 2048)[:, tt * 512:(tt + 1) * 512]

    if debug:
        dbg_xq1 = nc.dram_tensor("dbg_xq1", [D, TC], BF16,
                                 kind="ExternalOutput").ap()
        t_dbgxq = [dma(dbg_xq1[k * P:(k + 1) * P, :],
                       Ab(HS[0 + k // 2] + (k % 2) * 1024, 2048),
                       waits=[t_xq[k]]) for k in range(KC)]
        t_xq[KC - 1] = t_dbgxq[-1]

    red1q2 = A(HS[4], 4096)
    contq2 = A(HS[6], 1024)
    t_mm1_ev, t_hg, mm1_last = {}, {}, {}
    t_red2 = [None]
    for h in range(HC):
        tl = dma(Ab(RW_B + (h % 2) * 512, 1024).rearrange(
                     "p (k j) -> p k j", j=P),
                 s_w1q[:, h * P:(h + 1) * P].rearrange("(k p) j -> p k j",
                                                       p=P),
                 waits=[t_w1q, t_rr, t_ha_all[-1]] +
                 ([mm1_last[h - 2]] if h >= 2 else []))
        for k in range(KC):
            for tt in range(NTT):
                w = [tl, t_xq[KC - 1]]
                if k == 0 and h > 0:
                    w.append(t_mm1_ev[(h - 1, tt)])
                mm = te(lambda e, k_=k, t_=tt, h_=h: e.matmul(
                    pb[t_][:, :],
                    Ab(RW_B + (h_ % 2) * 512, 1024)[:, k_ * P:(k_ + 1) * P],
                    xq_ap(k_, t_), start=(k_ == 0), stop=(k_ == KC - 1)),
                    waits=w)
                if k == KC - 1:
                    t_mm1_ev[(h, tt)] = ac(
                        lambda e, t_=tt: e.activation(
                            out=A(XS[8], 2048)[:, t_ * 512:(t_ + 1) * 512],
                            in_=pb[t_][:, :], func=AF.Gelu,
                            scale=scrW[:, 10:11]),
                        waits=[tdrain(mm), t_s1] +
                        ([t_hg[h - 1], t_red2[0]] if (tt == 0 and h > 0)
                         else []))
        mm1_last[h] = TE.tok()
        tevs = [t_mm1_ev[(h, tt)] for tt in range(NTT)]
        tevd = acdrain(*tevs)
        t_hg[h] = dma(s_hg[h * P:(h + 1) * P, :], A(XS[8], 2048),
                      waits=[tevd])
        ta = ac(lambda e: e.activation(
            out=A(RW_A, 2048), in_=A(XS[8], 2048), func=AF.Abs),
            waits=tevs + [t_red2[0], t_q1v, t_xq[KC - 1]])
        for w8 in range(16):
            ta = ve(lambda e, h_=h, w_=w8: e.max(
                out=red1q2[:, h_ * P + w_ * 8:h_ * P + (w_ + 1) * 8],
                in_=A(RW_A, 2048)[:, w_ * 128:(w_ + 1) * 128]), waits=[ta])
        t_red2[0] = ta

    t_g2w = emit_gamma(w2T, H, D, (HS[6], HS[7]), 1, pre=t_xq[KC - 1],
                       first_wait=[ev3a, ev3b])
    t_w2q = emit_ternarize(w2T, s_w2q, H, D, (HS[6], HS[7]), RW_A + 2048,
                           RW_A + 3072, RW_A + 3328, 1, t_g2w, 1)

    t_c2 = (sems["dve"], max(t_red2[0][1], t_w2q[1]) if False else t_red2[0][1])
    t_c2 = t_red2[0]
    first = True
    for w8 in range(64):
        w = [t_c2] + ([t_w2q] if first else [])
        first = False
        t_c2 = ve(lambda e, w_=w8: e.max(
            out=contq2[:, w_ * 16:w_ * 16 + 8],
            in_=red1q2[:, w_ * 64:(w_ + 1) * 64]), waits=w)
        t_c2 = ve(lambda e, w_=w8: e.match_replace(
            out=red1q2[:, w_ * 64:(w_ + 1) * 64],
            in_to_replace=contq2[:, w_ * 16:w_ * 16 + 8],
            in_values=red1q2[:, w_ * 64:(w_ + 1) * 64],
            imm_value=BIGNEG), waits=[t_c2])
        t_c2 = ve(lambda e, w_=w8: e.max(
            out=contq2[:, w_ * 16 + 8:w_ * 16 + 16],
            in_=red1q2[:, w_ * 64:(w_ + 1) * 64]), waits=[t_c2])

    t_ci2 = dma(cc_q2_i.ap(), contq2, waits=[t_c2])
    t_gc2 = collective([list(range(NCORE))], cc_q2_i, cc_q2_o, waits=[t_ci2])
    q2segs = [(HS[2] + r * 1024, 1024) for r in range(NCORE)]
    t_g2 = [dma(A(HS[2] + r * 1024, 1024),
                cc_q2_o.ap()[r * P:(r + 1) * P, :],
                waits=[t_gc2, t_w2q, t_c2]) for r in range(NCORE)]

    t_m2d = lax_dance(2 ** 26, 13, 14)
    t_fb2 = fat_bisect(q2segs, 13, t_g2 + [t_m2d, t_ci2], HS[6])
    tA2, tB2 = band_setup(0.003, 13, 14, [t_fb2])
    g2segs, cnt2_ap, t_bc2 = band_correction(
        s_hg, HC, 512, HS[6], HS[7], XS[8], HS[7] + 1024, cc_w2_i, cc_w2_o,
        HS[2], HS[4], 13, 14,
        [tA2, tB2, t_fb2, t_hg[HC - 1]], [t_gc2])
    t_q2v = window_select(g2segs, cnt2_ap, 13, 15, t_bc2 + [t_fb2], HS[4])
    t_s2 = scales_from_v(15, 1, 11, 12, [t_q2v, t_g2w])

    xn16_base = [HS[0], HS[0] + 1024, HS[1], HS[1] + 1024,
                 RW_A, RW_A + 1024, HS[3], HS[3] + 1024]

    def xn16_ap2(k, tt):
        return Ab(xn16_base[k], 2048)[:, tt * 512:(tt + 1) * 512]

    xn16_ld2 = {}
    for k in range(KC):
        w = [mm1_last[HC - 1], t_xq[KC - 1], t_w2q] + \
            [t_mm1_ev[(HC - 1, tt)] for tt in range(NTT)]
        if k in (4, 5):
            w.append(t_red2[0])
        if k in (6, 7):
            w.append(t_q2v)
        xn16_ld2[k] = dma(Ab(xn16_base[k], 2048),
                          s_xn16[k * P:(k + 1) * P, :], waits=w)
    t_x16b = xn16_ld2[KC - 1]

    t_o = {}

    def g_evac2(do, tt, mm):
        t1 = ac(lambda e, t_=tt: e.activation(
            out=A(RW_C, 512), in_=pb[t_][:, :], func=AF.Silu),
            waits=[mm] + ([t_o[(do, tt - 1)]] if tt else
                          ([t_o[(do - 1, NTT - 1)]] if do else [])))
        t1d = acdrain(t1)
        t2 = ve(lambda e, d_=do, t_=tt: e.tensor_tensor(
            A(RW_B + 1024, 512), A(RW_C, 512),
            A(XS[d_], 2048)[:, t_ * 512:(t_ + 1) * 512], ALU.mult),
            waits=[t1d, t_ha[do]])
        t3 = ve(lambda e, d_=do, t_=tt: e.tensor_copy(
            Ab(XS[d_], 2048)[:, t_ * 512:(t_ + 1) * 512],
            A(RW_B + 1024, 512)), waits=[t2])
        t_o[(do, tt)] = t3
        return t3

    emit_projection(wgT, xn16_ap2, t_x16b, g_evac2, RQ, pre=xn16_ld2[0],
                    rhs_tok_fn=lambda k: xn16_ld2[k])
    t_o_all = [t_o[(d, t)] for d in range(KC) for t in range(NTT)]
    t_o_last = t_o_all[-1]

    def o_ap(k, tt):
        return Ab(XS[k], 2048)[:, tt * 512:(tt + 1) * 512]

    t_sat = {}

    def wo_evac(do, tt, mm):
        idx = do * NTT + tt
        buf = RW_B + 1024 + (idx % 2) * 512
        iprev = idx - 2
        wprev = t_sat.get((iprev // NTT, iprev % NTT)) if iprev >= 0 else None
        t1 = ac(lambda e, t_=tt, b_=buf: e.activation(
            out=A(b_, 512), in_=pb[t_][:, :], func=AF.Copy),
            waits=[mm, wprev])
        t1d = acdrain(t1)
        tok = dma(s_sat[do * P:(do + 1) * P, tt * 512:(tt + 1) * 512],
                  A(buf, 512), waits=[t1d])
        t_sat[(do, tt)] = tok
        return tok

    emit_projection(woT, o_ap, t_o_last, wo_evac, RQ, pre=t_o_last)
    t_sat_all = [t_sat[(d, t)] for d in range(KC) for t in range(NTT)]

    def hq_ap(h, tt):
        return Ab(XS[0 + h // 8] + (h % 8) * 256, 512)

    t_out_st = []
    t_hq = {}
    t_ti = {}
    t_c1 = {}
    t_mm2ev = {}
    w2strips = [XS[8], HS[0]]
    t_w2s = {}
    t_comb = {}
    for tt in range(NTT):
        for h in range(HC):
            w = [t_s2, t_hg[h], t_sat_all[-1]]
            if tt > 0:
                w.append(t_mm2ev[(tt - 1, KC - 1)])
            prev_ti = t_ti.get((h - 4, tt))
            if prev_ti is not None:
                w.append(prev_ti)
            hqstg = RW_A + (h % 4) * 512
            tl = dma(A(hqstg, 512),
                     s_hg[h * P:(h + 1) * P, tt * 512:(tt + 1) * 512],
                     waits=w)
            ti = ve(lambda e, s_=hqstg: e.tensor_scalar(
                Ai(RW_B, 512), A(s_, 512), scrW[:, 11:12], None, ALU.mult),
                waits=[tl, t_s2])
            t_ti[(h, tt)] = ti
            t_hq[(h, tt)] = ve(lambda e, h_=h, t_=tt: e.tensor_copy(
                hq_ap(h_, t_), Ai(RW_B, 512)), waits=[ti])
        hq_done = ve(lambda e: e.tensor_copy(SS[96:97, 0:1], SS[96:97, 0:1]),
                     waits=[t_hq[(HC - 1, tt)]])
        for dd in range(KC):
            sidx = dd % 2
            w = [t_w2q, hq_done]
            if (tt, dd - 2) in t_mm2ev:
                w.append(t_mm2ev[(tt, dd - 2)])
            if tt > 0 and dd < 2:
                w.append(t_mm2ev[(tt - 1, KC - 2 + dd)])
            if tt == 0 and dd < 2:
                w.append(t_sat_all[-1])
                w.append(t_xq[KC - 1])
            tl2 = dma(Ab(w2strips[sidx], 4096).rearrange(
                          "p (k j) -> p k j", j=P),
                      s_w2q[:, dd * P:(dd + 1) * P].rearrange(
                          "(k p) j -> p k j", p=P), waits=w)
            bank = dd % 2
            for hk in range(HC):
                mm = te(lambda e, hk_=hk, b_=bank, s_=sidx, t_=tt: e.matmul(
                    pb[b_][:, :],
                    Ab(w2strips[s_], 4096)[:, hk_ * P:(hk_ + 1) * P],
                    hq_ap(hk_, t_), start=(hk_ == 0), stop=(hk_ == HC - 1)),
                    waits=[tl2, hq_done] +
                    ([t_mm2ev[(tt, dd - 2)]] if (tt, dd - 2) in t_mm2ev
                     else []))
            wg_ = [tdrain(mm), t_s2]
            prev_c1 = t_c1.get((tt, dd - 1)) or (
                t_c1.get((tt - 1, KC - 1)) if dd == 0 else None)
            if prev_c1 is not None:
                wg_.append(prev_c1)
            tg_ = ac(lambda e, b_=bank: e.activation(
                out=A(RW_C, 512), in_=pb[b_][:, :], func=AF.Silu,
                scale=scrW[:, 12:13]), waits=wg_)
            t_mm2ev[(tt, dd)] = tg_
            tgd = acdrain(tg_)
            prev_cmb = t_comb.get((tt, dd - 1)) or \
                t_comb.get((tt - 1, KC - 1)) or t_sat_all[-1]
            txl = dma(A(HS[1], 512), xs[dd * P:(dd + 1) * P,
                                        tt * 512:(tt + 1) * 512],
                      waits=[prev_cmb])
            tsl = dma(A(HS[1] + 512, 512),
                      s_sat[dd * P:(dd + 1) * P, tt * 512:(tt + 1) * 512],
                      waits=[t_sat[(dd, tt)], prev_cmb])
            cidx = tt * KC + dd
            acc = HS[1] + 1024 + (cidx % 2) * 512
            tc1 = ve(lambda e, a_=acc: e.tensor_tensor(
                A(a_, 512), A(RW_C, 512), A(HS[1] + 512, 512),
                ALU.mult), waits=[tgd, tsl] +
                ([t_out_st[-2]] if len(t_out_st) >= 2 else []))
            t_c1[(tt, dd)] = tc1
            tc2 = ve(lambda e, a_=acc: e.tensor_tensor(
                A(a_, 512), A(a_, 512), A(HS[1], 512),
                ALU.add), waits=[tc1, txl])
            t_comb[(tt, dd)] = tc2
            t_out_st.append(dma(out[dd * P:(dd + 1) * P,
                                    tt * 512:(tt + 1) * 512],
                                A(acc, 512), waits=[tc2]))

    if debug:
        dbg_xq2 = nc.dram_tensor("dbg_xq2", [D, TC], BF16, kind="ExternalOutput").ap()
        for k in range(KC):
            off_ = 18432 + (k // 2) * 2048 + (k % 2) * 1024
            dma(dbg_xq2[k * P:(k + 1) * P, :],
                arena[:, off_:off_ + 1024].bitcast(BF16),
                waits=[mm1_last[HC - 1], t_hg[HC - 1]])
        dbgc1 = nc.dram_tensor("dbgc1", [P, 512], F32, kind="ExternalOutput").ap()
        t_out_st.append(dma(dbgc1, contq1, waits=[t_q1v]))
        dbgc2 = nc.dram_tensor("dbgc2", [P, 1024], F32, kind="ExternalOutput").ap()
        t_out_st.append(dma(dbgc2, contq2, waits=[t_q2v]))
        dbg = nc.dram_tensor("dbg", [2, 64], F32, kind="ExternalOutput").ap()
        t_out_st.append(dma(dbg[0:1, :], scr[0:1, :], waits=[t_comb[(NTT - 1, KC - 1)]]))
        t_out_st.append(dma(dbg[1:2, 0:16], scrW[0:1, 0:16], waits=[t_comb[(NTT - 1, KC - 1)]]))
    def fin(e):
        e.wait_ge(sems["dma"], t_out_st[-1][1])
    GP.ops.append(fin)

    with nc.Block() as block:
        @block.sync
        def _(sync):
            for op in SY.ops:
                op(sync)

        @block.tensor
        def _(eng):
            for op in TE.ops:
                op(eng)

        @block.scalar
        def _(eng):
            for op in SC_.ops:
                op(eng)

        @block.vector
        def _(eng):
            for op in VE.ops:
                op(eng)

        @block.gpsimd
        def _(eng):
            for op in GP.ops:
                op(eng)

    ctx.close()
    return nc


_NC_CACHE = [None]


def kernel(L0_attn_x, L0_attn_Wi, L0_attn_Wf, L0_attn_Wg, L0_attn_Wo,
           L0_attn_norm_w, L0_ffn_ln_g, L0_ffn_ln_b, L0_ffn_w1, L0_ffn_w2):
    x = np.asarray(L0_attn_x, dtype=np.float32)
    wi = np.ascontiguousarray(np.asarray(L0_attn_Wi, np.float32).T)
    wf = np.ascontiguousarray(np.asarray(L0_attn_Wf, np.float32).T)
    wg = np.ascontiguousarray(np.asarray(L0_attn_Wg, np.float32).T)
    wo = np.ascontiguousarray(np.asarray(L0_attn_Wo, np.float32).T)
    w1 = np.ascontiguousarray(np.asarray(L0_ffn_w1, np.float32).T)
    w2 = np.ascontiguousarray(np.asarray(L0_ffn_w2, np.float32).T)
    nw = np.asarray(L0_attn_norm_w, np.float32)
    lg = np.asarray(L0_ffn_ln_g, np.float32)
    lb = np.asarray(L0_ffn_ln_b, np.float32)

    in_maps = []
    for c in range(NCORE):
        b, r = c // 4, c % 4
        xsl = np.ascontiguousarray(x[b, r * TC:(r + 1) * TC, :].T)
        auxm = np.zeros((P, 32), np.float32)
        auxm[:, 0:8] = nw.reshape(KC, P).T
        auxm[:, 8:16] = lg.reshape(KC, P).T
        auxm[:, 16:24] = lb.reshape(KC, P).T
        for q in range(3):
            auxm[:, 24 + q] = 1.0 if q < r else 0.0
        auxm[:, 27] = np.float32(1e-5)
        auxm[:, 28] = np.float32(0.995)
        in_maps.append({"x": xsl, "wiT": wi, "wfT": wf, "wgT": wg, "woT": wo,
                        "w1T": w1, "w2T": w2, "aux": auxm})

    if _NC_CACHE[0] is None:
        _NC_CACHE[0] = build_program()
    res = run_bass_kernel_spmd(_NC_CACHE[0], in_maps, core_ids=list(range(NCORE)))
    y = np.empty((B, T, D), np.float32)
    for c in range(NCORE):
        b, r = c // 4, c % 4
        y[b, r * TC:(r + 1) * TC, :] = res.results[c]["out"].T
    return y


# version 16 (final)
# speedup vs baseline: 1.1588x; 1.0271x over previous
import contextlib
import numpy as np

import concourse.bass as bass
import concourse.mybir as mybir
from concourse.bass_utils import run_bass_kernel_spmd

F32 = mybir.dt.float32
BF16 = mybir.dt.bfloat16
I8 = mybir.dt.int8
I32 = mybir.dt.int32
AF = mybir.ActivationFunctionType
ALU = mybir.AluOpType
AXX = mybir.AxisListType.X

P, D, H = 128, 1024, 4096
B, T, NCORE = 2, 8192, 8
TC = (B * T) // NCORE
KC, HC, NTT = D // P, H // P, 4
BIGNEG = -1.0e30
BIGOFF = 1.0e18
EPS32 = float(np.float32(1e-5))
FAT_IT = 12
LOC_IT = 18


class Eng:
    def __init__(self, name, chain=True, lag=1):
        self.name, self.ops, self.n, self.sem = name, [], 0, None
        self.chain = chain
        self.lag = lag

    def do(self, fn, waits=(), inc=True, chain=None):
        waits = tuple(w for w in waits if w is not None)
        prev, me = self.n - (self.lag - 1), self
        ch = self.chain if chain is None else chain

        def op(e):
            if prev > 0 and ch:
                e.wait_ge(me.sem, prev)
            for (s, v) in waits:
                if s is not me.sem:
                    e.wait_ge(s, v)
            ins = fn(e)
            if inc:
                ins.then_inc(me.sem, 1)
        self.ops.append(op)
        if inc:
            self.n += 1
        return (self.sem, self.n)

    def tok(self):
        return (self.sem, self.n)


def build_program(debug=False):
    kw = {"kind": "ExternalOutput"} if debug else {}
    nc = bass.Bass("TRN2", target_bir_lowering=False, debug=False)

    xs = nc.dram_tensor("x", [D, TC], F32, kind="ExternalInput").ap()
    wiT = nc.dram_tensor("wiT", [D, D], F32, kind="ExternalInput").ap()
    wfT = nc.dram_tensor("wfT", [D, D], F32, kind="ExternalInput").ap()
    wgT = nc.dram_tensor("wgT", [D, D], F32, kind="ExternalInput").ap()
    woT = nc.dram_tensor("woT", [D, D], F32, kind="ExternalInput").ap()
    w1T = nc.dram_tensor("w1T", [D, H], F32, kind="ExternalInput").ap()
    w2T = nc.dram_tensor("w2T", [H, D], F32, kind="ExternalInput").ap()
    aux = nc.dram_tensor("aux", [P, 32], F32, kind="ExternalInput").ap()
    out = nc.dram_tensor("out", [D, TC], F32, kind="ExternalOutput").ap()

    s_h0 = nc.dram_tensor("s_h0", [D, TC], F32, **kw).ap()
    s_xn16 = nc.dram_tensor("s_xn16", [D, TC], BF16, **kw).ap()
    s_P = nc.dram_tensor("s_P", [D, TC], F32, **kw).ap()
    s_si = nc.dram_tensor("s_si", [D, TC], F32).ap()
    s_w1q = nc.dram_tensor("s_w1q", [D, H], BF16, **kw).ap()
    s_w2q = nc.dram_tensor("s_w2q", [H, D], BF16).ap()
    s_hg = nc.dram_tensor("s_hg", [H, TC], F32, **kw).ap()
    s_sat = nc.dram_tensor("s_sat", [D, TC], F32, **kw).ap()
    s_bnc = nc.dram_tensor("s_bnc", [1, 65536], F32).ap()
    s_cons = nc.dram_tensor("s_cons", [P, 256], F32).ap()

    cc_car_i = nc.dram_tensor("cc_car_i", [P, 16], F32)
    cc_car_o = nc.dram_tensor("cc_car_o", [4 * P, 16], F32)
    cc_q1_i = nc.dram_tensor("cc_q1_i", [P, 512], F32)
    cc_q1_o = nc.dram_tensor("cc_q1_o", [NCORE * P, 512], F32)
    cc_q2_i = nc.dram_tensor("cc_q2_i", [P, 1024], F32)
    cc_q2_o = nc.dram_tensor("cc_q2_o", [NCORE * P, 1024], F32)
    cc_w1_i = nc.dram_tensor("cc_w1_i", [P, 520], F32)
    cc_w1_o = nc.dram_tensor("cc_w1_o", [NCORE * P, 520], F32)
    cc_w2_i = nc.dram_tensor("cc_w2_i", [P, 520], F32)
    cc_w2_o = nc.dram_tensor("cc_w2_o", [NCORE * P, 520], F32)

    ctx = contextlib.ExitStack()

    def sb(name, shape, dt=F32):
        return ctx.enter_context(nc.sbuf_tensor(name, shape, dt))

    ARC = 43008
    arena = sb("arena", [P, ARC])
    auxT = sb("auxT", [P, 32])
    onesD = sb("onesD", [P, 1])
    zcol = sb("zcol", [P, 1])
    fbT = sb("fbT", [1, 1], BF16)
    S0 = sb("S0", [P, 512])
    S1 = sb("S1", [P, 512])
    M0 = sb("M0", [P, 512])
    M1 = sb("M1", [P, 512])
    SS = sb("SS", [P, 512])
    scr = sb("scr", [1, 64])
    scrI = sb("scrI", [1, 8], I32)
    scrU = sb("scrU", [1, 16], mybir.dt.uint8)
    cmu8 = sb("cmu8", [P, 4], mybir.dt.uint8)
    scrW = sb("scrW", [P, 16])
    cons = sb("cons", [P, 2048])
    brow = sb("brow", [1, 256])
    m8s = sb("m8s", [P, 8])
    carry = sb("carry", [P, 96])
    stg = sb("stg", [32, 64])
    ex2 = sb("ex2", [P, 256])

    pb = [ctx.enter_context(nc.psum_tensor(f"pb{i}", [P, 512], F32))
          for i in range(8)]
    sems = {k: ctx.enter_context(nc.semaphore("sem_" + k))
            for k in ("dma", "pe", "act", "dve", "pool", "cc")}

    SY, TE, SC_, VE, GP = (Eng(n, chain=(n != "tensor"))
                           for n in ("sync", "tensor", "scalar",
                                     "vector", "gpsimd"))
    TE.sem, SC_.sem, VE.sem, GP.sem, SY.sem = (
        sems["pe"], sems["act"], sems["dve"], sems["pool"], sems["dma"])

    ndma = [0]

    def dma(dst, src, waits=(), slow=False, sc=False):
        waits = tuple(w for w in waits if w is not None)

        def op(e):
            for (s, v) in waits:
                e.wait_ge(s, v)
            e.dma_start(out=dst, in_=src,
                        allow_slow_non_contiguous=slow).then_inc(
                sems["dma"], 16)
        (SC_ if sc else SY).ops.append(op)
        ndma[0] += 1
        return (sems["dma"], 16 * ndma[0])

    ncc = [0]

    def collective(groups, in_t, out_t, waits=()):
        waits = tuple(w for w in waits if w is not None)

        def op(e):
            for (s, v) in waits:
                e.wait_ge(s, v)
            e.collective_compute("AllGather", ALU.bypass, replica_groups=groups,
                                 ins=[in_t.ap().opt()], outs=[out_t.ap().opt()]
                                 ).then_inc(sems["cc"], 1)
        GP.ops.append(op)
        ncc[0] += 1
        return (sems["cc"], ncc[0])

    def A(off, cols):
        assert 0 <= off and off + cols <= ARC, f"arena [{off},{off + cols})"
        return arena[:, off:off + cols]

    def Ab(off, bf16_cols):
        return A(off, bf16_cols // 2).bitcast(BF16)

    def Ai(off, i8_cols):
        return A(off, i8_cols // 4).bitcast(I8)

    XS = [i * 2048 for i in range(9)]
    HS = [18432 + i * 2048 for i in range(8)]
    RW = 34816
    RQ = 41472
    RW_A, RW_B, RW_C = RW, RW + 4096, RW + 6144

    ve = lambda fn, waits=(): VE.do(fn, waits=waits)
    ac = lambda fn, waits=(): SC_.do(fn, waits=waits)
    te = lambda fn, waits=(), chain=None: TE.do(fn, waits=waits, chain=chain)

    def acdrain(*waits):
        return ac(lambda e: e.activation(out=scr[0:1, 62:63],
                                         in_=scr[0:1, 62:63], func=AF.Copy),
                  waits=list(waits))

    def tdrain(*waits):
        return te(lambda e: e.matmul(pb[7][64:65, 0:1], fbT[0:1, 0:1],
                                     fbT[0:1, 0:1], start=True, stop=True),
                  waits=list(waits))

    t_aux = dma(auxT[:], aux)
    ve(lambda e: e.memset(onesD[:], 1.0 / D))
    ve(lambda e: e.memset(fbT[:], 0.0))
    ve(lambda e: e.memset(zcol[:], 0.0))
    ve(lambda e: e.memset(scr[:], 0.0))
    ve(lambda e: e.memset(scrW[:], 0.0))
    ve(lambda e: e.memset(carry[:], 0.0))
    ve(lambda e: e.memset(stg[:], 0.0))
    ve(lambda e: e.memset(cons[:], BIGNEG))
    ve(lambda e: e.memset(ex2[:], BIGNEG))
    ve(lambda e: e.memset(m8s[:], 0.0))
    t_cm = ve(lambda e: e.tensor_copy(cmu8[:, 0:3], auxT[:, 24:27]),
              waits=[t_aux])
    t_init = VE.tok()

    normw_c = lambda k: auxT[:, k:k + 1]
    lng_c = lambda k: auxT[:, 8 + k:8 + k + 1]
    lnb_c = lambda k: auxT[:, 16 + k:16 + k + 1]
    cmask_c = lambda q: auxT[:, 24 + q:24 + q + 1]
    eps_col = auxT[:, 27:28]
    q995_c = auxT[0:1, 28:29]

    def st_mu(tt):
        S = S0 if tt < 3 else S1
        r = 32 * tt if tt < 3 else 0
        return S[r:r + 1, :]

    def st_m2(tt):
        M = M0 if tt < 3 else M1
        r = 32 * tt if tt < 3 else 0
        return M[r:r + 1, :]

    def emit_mean_stats(src_ap_fn, src_tok, sq_off):
        last_mu = last_m2 = None
        tq = None
        for k in range(KC):
            w = [src_tok[k]]
            if last_m2 is not None:
                w.append(last_m2)
            tq = ac(lambda e, k_=k, s_=sq_off: e.activation(
                out=A(s_, 2048), in_=src_ap_fn(k_), func=AF.Square), waits=w)
            for tt in range(NTT):
                bmu, rmu = (4, 32 * tt) if tt < 3 else (5, 0)
                last_mu = te(lambda e, k_=k, t_=tt, b_=bmu, r_=rmu: e.matmul(
                    pb[b_][r_:r_ + 1, :], onesD[:],
                    src_ap_fn(k_)[:, t_ * 512:(t_ + 1) * 512],
                    start=(k_ == 0), stop=(k_ == KC - 1)),
                    waits=[src_tok[k], t_init])
                bm2, rm2 = (6, 32 * tt) if tt < 3 else (7, 0)
                last_m2 = te(lambda e, k_=k, t_=tt, b_=bm2, r_=rm2, s_=sq_off:
                             e.matmul(pb[b_][r_:r_ + 1, :], onesD[:],
                                      A(s_, 2048)[:, t_ * 512:(t_ + 1) * 512],
                                      start=(k_ == 0), stop=(k_ == KC - 1)),
                             waits=[tq])
        return last_mu, last_m2

    def evac_stats(tok_mu, tok_m2):
        toks = [ve(lambda e: e.tensor_copy(S0[0:96, :], pb[4][0:96, :]),
                   waits=[tok_mu]),
                ve(lambda e: e.tensor_copy(S1[0:32, :], pb[5][0:32, :]),
                   waits=[tok_mu]),
                ve(lambda e: e.tensor_copy(M0[0:96, :], pb[6][0:96, :]),
                   waits=[tok_m2]),
                ve(lambda e: e.tensor_copy(M1[0:32, :], pb[7][0:32, :]),
                   waits=[tok_m2])]
        return toks

    def coef_seq(w0):
        done = []
        for (Smu, Sm2, r) in ((S0, M0, 96), (S1, M1, 32)):
            t1 = ve(lambda e, m_=Smu, r_=r: e.tensor_tensor(
                SS[0:r_, :], m_[0:r_, :], m_[0:r_, :], ALU.mult), waits=w0)
            t2 = ve(lambda e, v_=Sm2, r_=r: e.tensor_tensor(
                v_[0:r_, :], v_[0:r_, :], SS[0:r_, :], ALU.subtract),
                waits=[t1])
            t3 = ac(lambda e, v_=Sm2, r_=r: e.activation(
                out=v_[0:r_, :], in_=v_[0:r_, :], func=AF.Ln,
                bias=eps_col[0:r_, :]), waits=[t2, t_aux])
            t4 = ac(lambda e, v_=Sm2, r_=r: e.activation(
                out=v_[0:r_, :], in_=v_[0:r_, :], func=AF.Exp, scale=-0.5),
                waits=[t3])
            t5 = ve(lambda e, m_=Smu, v_=Sm2, r_=r: e.tensor_tensor(
                m_[0:r_, :], m_[0:r_, :], v_[0:r_, :], ALU.mult), waits=[t4])
            t6 = ve(lambda e, m_=Smu, r_=r: e.tensor_scalar(
                m_[0:r_, :], m_[0:r_, :], -1.0, None, ALU.mult), waits=[t5])
            done.append(t6)
        return done

    nbnc = [0]

    def bcast_rows(rows, dst_ap, w0, cols=512):
        off = nbnc[0]
        nbnc[0] += max(len(rows) * cols, 8)
        assert nbnc[0] <= 32768
        toks = [dma(s_bnc[0:1, off + i * cols: off + (i + 1) * cols], r,
                    waits=w0) for i, r in enumerate(rows)]
        return dma(dst_ap, s_bnc[0:1, off:off + len(rows) * cols]
                   .to_broadcast([P, len(rows) * cols]), waits=toks)

    t_x = [dma(A(XS[k], 2048), xs[k * P:(k + 1) * P, :], waits=[t_init])
           for k in range(KC)]

    mu1, m21 = emit_mean_stats(lambda k: A(XS[k], 2048), t_x, XS[8])
    fst1 = tdrain(mu1, m21)
    ev1 = evac_stats(fst1, fst1)
    cf1 = coef_seq(ev1)
    a1_rep, b1_rep = A(RW_A, 2048), A(RW_A + 2048, 2048)
    t_a1 = bcast_rows([st_m2(tt) for tt in range(NTT)], a1_rep, cf1)
    t_b1 = bcast_rows([st_mu(tt) for tt in range(NTT)], b1_rep, cf1)

    t_xn, t_x16 = {}, {}
    for k in range(KC):
        t1 = ve(lambda e, k_=k: e.tensor_tensor(
            A(XS[k_], 2048), A(XS[k_], 2048), a1_rep, ALU.mult),
            waits=[t_a1, t_x[k]])
        t2 = ve(lambda e, k_=k: e.tensor_tensor(
            A(XS[k_], 2048), A(XS[k_], 2048), b1_rep, ALU.add),
            waits=[t_b1, t1])
        t_xn[k] = t2
        c16 = Ab(HS[4 + k // 2] + (k % 2) * 1024, 2048)
        t3 = ve(lambda e, c_=c16, k_=k: e.tensor_copy(c_, A(XS[k_], 2048)),
                waits=[t2])
        t_x16[k] = t3
        dma(s_xn16[k * P:(k + 1) * P, :], c16, waits=[t3])

    mu2, m22 = emit_mean_stats(lambda k: A(XS[k], 2048),
                               [t_xn[k] for k in range(KC)], XS[8])
    fst2 = tdrain(mu2, m22)
    ev2 = evac_stats(fst2, fst2)
    cf2 = coef_seq(ev2)
    a2_rep, b2_rep = A(RW_B, 2048), A(RW_A + 2048, 2048)
    t_a2 = bcast_rows([st_m2(tt) for tt in range(NTT)], a2_rep,
                      cf2 + [t_b1, t_xn[KC - 1]])
    t_b2 = bcast_rows([st_mu(tt) for tt in range(NTT)], b2_rep, cf2)

    red1q1 = A(RQ, 1024)
    contq1 = A(RQ + 1024, 512)
    t_h0, t_m8 = {}, None
    for k in range(KC):
        hwrk = A(XS[8], 2048)
        w = [t_a2, t_b2, t_xn[k], t_x16[k]]
        if k > 0:
            w += [t_m8, t_h0[k - 1]]
        t1 = ve(lambda e, k_=k, h_=hwrk: e.tensor_tensor(
            h_, A(XS[k_], 2048), a2_rep, ALU.mult), waits=w)
        t2 = ve(lambda e, h_=hwrk: e.tensor_tensor(h_, h_, b2_rep, ALU.add),
                waits=[t1])
        t3 = ve(lambda e, k_=k, h_=hwrk: e.tensor_scalar(
            h_, h_, lng_c(k_), lnb_c(k_), ALU.mult, ALU.add),
            waits=[t2, t_aux])
        t_h0[k] = t3
        dma(s_h0[k * P:(k + 1) * P, :], hwrk, waits=[t3])
        ta = ac(lambda e, k_=k, h_=hwrk: e.activation(
            out=A(XS[k_], 2048), in_=h_, func=AF.Abs), waits=[t3])
        for w8 in range(16):
            ta = ve(lambda e, k_=k, w_=w8: e.max(
                out=red1q1[:, k_ * P + w_ * 8: k_ * P + (w_ + 1) * 8],
                in_=A(XS[k_], 2048)[:, w_ * 128:(w_ + 1) * 128]), waits=[ta])
        t_m8 = ta

    t_c1 = t_m8
    for w8 in range(64):
        t_c1 = ve(lambda e, w_=w8: e.max(
            out=contq1[:, w_ * 8:(w_ + 1) * 8],
            in_=red1q1[:, w_ * 16:(w_ + 1) * 16]), waits=[t_c1])

    t_ci1 = dma(cc_q1_i.ap(), contq1, waits=[t_c1])
    t_gc1 = collective([list(range(NCORE))], cc_q1_i, cc_q1_o, waits=[t_ci1])
    q1segs = [(RW_A + r * 512, 512) for r in range(NCORE)]

    def emit_projection(wsrc, rhs_ap_fn, rhs_tok, evac_fn, strip_off, pre=None,
                        interleave=None, first_mm_waits=(), rhs_tok_fn=None):
        evt = {}
        last_strip = None
        for do in range(KC):
            strip = A(strip_off, 1024)
            castt = Ab(strip_off + 1024, 1024)
            tl = dma(strip.rearrange("p (k j) -> p k j", j=P),
                     wsrc[:, do * P:(do + 1) * P].rearrange(
                         "(k p) j -> p k j", p=P), waits=[pre, last_strip])
            tc_ = ve(lambda e, s_=strip, c_=castt: e.tensor_copy(c_, s_),
                     waits=[tl])
            for k in range(KC):
                for tt in range(NTT):
                    w = [tc_, rhs_tok_fn(k) if rhs_tok_fn else rhs_tok]
                    if k == 0 and do == 0:
                        w.extend(first_mm_waits)
                    if k == 0 and do > 0:
                        w.append(evt[(do - 1, tt)])
                    mm = te(lambda e, k_=k, t_=tt, c_=castt: e.matmul(
                        pb[t_][:, :], c_[:, k_ * P:(k_ + 1) * P],
                        rhs_ap_fn(k_, t_),
                        start=(k_ == 0), stop=(k_ == KC - 1)), waits=w)
                    if k == KC - 1:
                        evt[(do, tt)] = evac_fn(do, tt, tdrain(mm))
            last_strip = TE.tok()
            if interleave is not None:
                interleave(do)
        return evt

    def xn16_ap(k, tt):
        return Ab(HS[4 + k // 2] + (k % 2) * 1024, 2048)[:, tt * 512:(tt + 1) * 512]

    f_ev = {}

    def f_evac(do, tt, mm):
        tok = ac(lambda e, d_=do, t_=tt: e.activation(
            out=A(XS[d_], 2048)[:, t_ * 512:(t_ + 1) * 512],
            in_=pb[t_][:, :], func=AF.Sigmoid),
            waits=[mm, t_m8, t_h0[KC - 1]])
        f_ev[(do, tt)] = tok
        return tok

    emit_projection(wfT, xn16_ap, t_x16[KC - 1], f_evac, XS[8],
                    pre=t_h0[KC - 1])
    t_f = {do: f_ev[(do, NTT - 1)] for do in range(KC)}

    def emit_gamma(wsrc, rows, cols, strip_offs, scr_slot, pre=None,
                   first_wait=()):
        tw = 2048 if cols % 2048 == 0 else 1024
        tiles = [(rc, cc) for rc in range(rows // P) for cc in range(cols // tw)]
        lastmm = None
        for i, (rc, cc) in enumerate(tiles):
            so = strip_offs[i % 2]
            w = [t_init, pre]
            if i >= 2:
                w.append(lastmm)
            tl = dma(A(so, tw),
                     wsrc[rc * P:(rc + 1) * P, cc * tw:(cc + 1) * tw],
                     waits=w)
            tab = ac(lambda e, s_=so, tw_=tw: e.activation(
                out=A(s_, tw_), in_=A(s_, tw_), func=AF.Abs), waits=[tl])
            for q in range(tw // 512):
                lastmm = te(lambda e, s_=so, q_=q, i_=i, tw_=tw: e.matmul(
                    pb[6][0:1, :], onesD[:],
                    A(s_, tw_)[:, q_ * 512:(q_ + 1) * 512],
                    start=(i_ == 0 and q_ == 0),
                    stop=(i_ == len(tiles) - 1 and q_ == tw_ // 512 - 1)),
                    waits=[tab] + (list(first_wait) if i == 0 and q == 0
                                   else []))
        tev = ve(lambda e: e.tensor_copy(SS[0:1, :], pb[6][0:1, :]),
                 waits=[tdrain(lastmm)])
        tr = ve(lambda e: e.tensor_reduce(scr[0:1, 40:41], SS[0:1, :], AXX,
                                          ALU.add), waits=[tev])
        tg = ve(lambda e, s_=scr_slot, n_=(rows * cols // D): e.tensor_scalar(
            scr[0:1, s_:s_ + 1], scr[0:1, 40:41], 1.0 / n_, EPS32,
            ALU.mult, ALU.add), waits=[tr])
        return tg

    t_g1w = emit_gamma(w1T, D, H, (HS[0], HS[1]), 0, first_wait=ev2)

    def emit_ternarize(wsrc, dst, rows, cols, strip_offs, tmp_f, tmp_i, tmp_b,
                       g_slot, tg, rg_col_idx):
        trg = ve(lambda e, g_=g_slot: e.reciprocal(
            scr[0:1, 44:45], scr[0:1, g_:g_ + 1]), waits=[tg])
        trgb = bcast_rows([scr[0:1, 44:45]],
                          scrW[:, rg_col_idx:rg_col_idx + 1], [trg], cols=1)
        tw = 2048 if cols % 2048 == 0 else 1024
        tiles = [(rc, cc) for rc in range(rows // P) for cc in range(cols // tw)]
        lastq = None
        for i, (rc, cc) in enumerate(tiles):
            so = strip_offs[i % 2]
            w = [trgb]
            if i >= 2:
                w.append(lastq)
            tl = dma(A(so, tw),
                     wsrc[rc * P:(rc + 1) * P, cc * tw:(cc + 1) * tw],
                     waits=w)
            tmin = ve(lambda e, s_=so, tw_=tw: e.tensor_scalar(
                A(tmp_f, tw_), A(s_, tw_), scrW[:, rg_col_idx:rg_col_idx + 1],
                1.49, ALU.mult, ALU.min), waits=[tl])
            ti8 = ve(lambda e, tw_=tw: e.tensor_scalar(
                Ai(tmp_i, tw_ * 4)[:, 0:tw_], A(tmp_f, tw_), -1.49, None,
                ALU.max), waits=[tmin])
            tb = ve(lambda e, tw_=tw: e.tensor_copy(
                Ab(tmp_b, tw_ * 2)[:, 0:tw_], Ai(tmp_i, tw_ * 4)[:, 0:tw_]),
                waits=[ti8])
            lastq = dma(dst[rc * P:(rc + 1) * P, cc * tw:(cc + 1) * tw],
                        Ab(tmp_b, tw * 2)[:, 0:tw], waits=[tb])
        return lastq

    t_w1q = emit_ternarize(w1T, s_w1q, D, H, (HS[0], HS[1]), RW_B, RW_C,
                           RW_A, 0, t_g1w, 0)

    si_ev = {}
    t_hloc, t_Pst, t_scans, t_inp = {}, {}, [], {}
    HSmap = {0: HS[0], 1: HS[1], 2: HS[2], 3: HS[3], 4: HS[6], 5: HS[7],
             6: HS[4], 7: HS[5]}
    si_gate = {}

    def si_evac(do, tt, mm):
        w = [mm, t_w1q]
        if do > 0:
            w.append(si_gate[do - 1])
        tok = ac(lambda e, t_=tt: e.activation(
            out=A(XS[8], 2048)[:, t_ * 512:(t_ + 1) * 512],
            in_=pb[t_][:, :], func=AF.Silu), waits=w)
        si_ev[(do, tt)] = tok
        return tok

    def scan_chain(do):
        w = [si_ev[(do, tt)] for tt in range(NTT)] + [t_ci1]
        t2 = dma(s_si[do * P:(do + 1) * P, :], A(XS[8], 2048), waits=w)
        si_gate[do] = t2
        t_inp[do] = t2

    emit_projection(wiT, xn16_ap, t_x16[KC - 1], si_evac, RQ, pre=t_ci1,
                    interleave=scan_chain,
                    first_mm_waits=[f_ev[(KC - 1, tt)] for tt in range(NTT)])

    t_wi_done = TE.tok()
    for do in range(KC):
        tsl = dma(A(XS[8], 2048), s_si[do * P:(do + 1) * P, :],
                  waits=[t_inp[do], t_wi_done] +
                  ([t_hloc[do - 1]] if do else []))
        tu = ve(lambda e, d_=do: e.tensor_scalar(
            A(RW_B, 2048), A(XS[d_], 2048), -1.0, 1.0, ALU.mult, ALU.add),
            waits=[tsl, t_f[do]])
        ti2 = ve(lambda e, d_=do: e.tensor_tensor(
            A(HSmap[d_], 2048), A(XS[8], 2048), A(RW_B, 2048), ALU.mult),
            waits=[tu])
        t3 = ve(lambda e, d_=do: e.tensor_tensor_scan(
            A(HSmap[d_], 2048), A(XS[d_], 2048), A(HSmap[d_], 2048), 0.0,
            ALU.mult, ALU.add), waits=[ti2])
        t4 = ve(lambda e, d_=do: e.tensor_tensor_scan(
            A(RW_B, 2048), A(XS[d_], 2048), A(XS[d_], 2048), 1.0,
            ALU.mult, ALU.bypass), waits=[t3])
        t_Pst[do] = dma(s_P[do * P:(do + 1) * P, :], A(RW_B, 2048), waits=[t4])
        t5 = ve(lambda e, d_=do: e.tensor_copy(
            carry[:, d_:d_ + 1], A(RW_B, 2048)[:, 2047:2048]), waits=[t4])
        t6 = ve(lambda e, d_=do: e.tensor_copy(
            carry[:, 8 + d_:8 + d_ + 1], A(HSmap[d_], 2048)[:, 2047:2048]),
            waits=[t5])
        t_hloc[do] = t3
        t_scans.append(t6)

    t_car_i = dma(cc_car_i.ap(), carry[:, 0:16], waits=t_scans)
    t_car_g = collective([[0, 1, 2, 3], [4, 5, 6, 7]], cc_car_i, cc_car_o,
                         waits=[t_car_i])
    t_car_l = [dma(carry[:, 16 + 16 * q:32 + 16 * q],
                   cc_car_o.ap()[q * P:(q + 1) * P, :], waits=[t_car_g])
               for q in range(4)]
    tC = ve(lambda e: e.memset(carry[:, 80:96], 0.0), waits=t_car_l)
    for q in range(3):
        t1 = ve(lambda e, q_=q: e.tensor_tensor(
            carry[:, 88:96], carry[:, 16 + 16 * q_:24 + 16 * q_],
            carry[:, 80:88], ALU.mult), waits=[tC])
        t2 = ve(lambda e, q_=q: e.tensor_tensor(
            carry[:, 88:96], carry[:, 88:96],
            carry[:, 24 + 16 * q_:32 + 16 * q_], ALU.add), waits=[t1])
        tC = ve(lambda e, q_=q: e.select(
            carry[:, 80:88], cmu8[:, q_:q_ + 1].to_broadcast([P, 8]),
            carry[:, 88:96], carry[:, 80:88]), waits=[t2, t_cm])

    if debug:
        dbg_car = nc.dram_tensor("dbg_car", [P, 96], F32,
                                 kind="ExternalOutput").ap()
        t_dbgcar = dma(dbg_car, carry[:, 0:96], waits=[tC])
    t_hfix = {}
    prev = None
    for do in range(KC):
        tl = dma(A(XS[8], 2048), s_P[do * P:(do + 1) * P, :],
                 waits=[t_Pst[do], tC, prev])
        t_hfix[do] = ve(lambda e, d_=do: e.scalar_tensor_tensor(
            A(XS[d_], 2048), A(XS[8], 2048), carry[:, 80 + d_:81 + d_],
            A(HSmap[d_], 2048), ALU.mult, ALU.add), waits=[tl, tC])
        prev = t_hfix[do]

    t_dbghf = {}
    if debug:
        dbg_hf = nc.dram_tensor("dbg_hf", [D, TC], F32,
                                kind="ExternalOutput").ap()
        for k in range(KC):
            t_dbghf[k] = dma(dbg_hf[k * P:(k + 1) * P, :], A(XS[k], 2048),
                             waits=[t_hfix[k]])
    t_hfix_g = {k: t_hfix[k] for k in range(KC)}
    t_hfix_g[0] = t_g1w if t_g1w[1] > t_hfix[0][1] and t_g1w[0] is t_hfix[0][0] else t_hfix[0]
    mu3, m23 = emit_mean_stats(lambda k: A(XS[k], 2048),
                               [t_hfix[KC - 1]] * KC, XS[8])
    fst3 = tdrain(m23)
    ev3a = ve(lambda e: e.tensor_copy(M0[0:96, :], pb[6][0:96, :]), waits=[fst3])
    ev3b = ve(lambda e: e.tensor_copy(M1[0:32, :], pb[7][0:32, :]), waits=[fst3])
    if debug:
        dbg_m2 = nc.dram_tensor("dbg_m2", [P, 1024], F32,
                                kind="ExternalOutput").ap()
        tdm = dma(dbg_m2[:, 0:512], M0[:, :], waits=[ev3a, ev3b])
        tdm2 = dma(dbg_m2[:, 512:1024], M1[:, :], waits=[ev3a, ev3b])
        ev3a = tdm
        ev3b = tdm2
    rr_done = []
    for (Mt, r) in ((M0, 96), (M1, 32)):
        t3 = ac(lambda e, v_=Mt, r_=r: e.activation(
            out=v_[0:r_, :], in_=v_[0:r_, :], func=AF.Ln,
            bias=eps_col[0:r_, :]), waits=[ev3a, ev3b, t_aux])
        t4 = ac(lambda e, v_=Mt, r_=r: e.activation(
            out=v_[0:r_, :], in_=v_[0:r_, :], func=AF.Exp, scale=-0.5),
            waits=[t3])
        rr_done.append(t4)

    def lax_dance(n_val, m_slot, hw_slot):
        t0 = ve(lambda e: e.memset(scr[0:1, 30:31], float(n_val)))
        t1 = ve(lambda e: e.tensor_scalar(
            scr[0:1, 31:32], scr[0:1, 30:31], 1.0, None, ALU.subtract),
            waits=[t0])
        t2 = ve(lambda e: e.tensor_tensor(
            scr[0:1, 32:33], q995_c, scr[0:1, 31:32], ALU.mult),
            waits=[t1, t_aux])
        ti = ve(lambda e: e.tensor_copy(scrI[0:1, 0:1], scr[0:1, 32:33]),
                waits=[t2])
        tf = ve(lambda e: e.tensor_copy(scr[0:1, 33:34], scrI[0:1, 0:1]),
                waits=[ti])
        tg = ve(lambda e: e.tensor_scalar(
            scr[0:1, 35:36], scr[0:1, 33:34], scr[0:1, 32:33], None,
            ALU.is_gt), waits=[tf])
        tlow = ve(lambda e: e.tensor_tensor(
            scr[0:1, 33:34], scr[0:1, 33:34], scr[0:1, 35:36], ALU.subtract),
            waits=[tg])
        thw = ve(lambda e, h_=hw_slot: e.tensor_tensor(
            scr[0:1, h_:h_ + 1], scr[0:1, 32:33], scr[0:1, 33:34],
            ALU.subtract), waits=[tlow])
        tm = ve(lambda e: e.tensor_tensor(
            scr[0:1, 36:37], scr[0:1, 30:31], scr[0:1, 33:34], ALU.subtract),
            waits=[thw])
        return ve(lambda e, m_=m_slot: e.tensor_scalar(
            scr[0:1, m_:m_ + 1], scr[0:1, 36:37], 1.0, None, ALU.subtract),
            waits=[tm])

    nsel = [0]

    def fat_bisect(segs, m_slot, gather_toks, junk_off, iters=FAT_IT):
        base = 32768 + nsel[0] * 1024
        nsel[0] += 1
        assert base + 1024 <= 65536
        segs_ap = [A(o, c) for (o, c) in segs]
        w0 = list(gather_toks)
        tmax = ve(lambda e, s_=segs_ap[0]: e.tensor_reduce(
            scrW[:, 2:3], s_, AXX, ALU.max), waits=w0)
        for s_ in segs_ap[1:]:
            tr = ve(lambda e, ss_=s_: e.tensor_reduce(
                scrW[:, 3:4], ss_, AXX, ALU.max), waits=[tmax])
            tmax = ve(lambda e: e.tensor_tensor(
                scrW[:, 2:3], scrW[:, 2:3], scrW[:, 3:4], ALU.max), waits=[tr])
        tst = dma(s_bnc[0:1, base + 128:base + 256].rearrange("o p -> p o"),
                  scrW[:, 2:3], waits=[tmax])
        tld = dma(brow[0:1, 0:128], s_bnc[0:1, base + 128:base + 256],
                  waits=[tst])
        thi = ve(lambda e: e.tensor_reduce(scr[0:1, 50:51], brow[0:1, 0:128],
                                           AXX, ALU.max), waits=[tld])
        t0 = ve(lambda e: e.memset(scr[0:1, 51:52], 0.0), waits=[thi])
        t0 = ve(lambda e: e.tensor_copy(scr[0:1, 52:53], scr[0:1, 50:51]),
                waits=[t0])
        t0 = ve(lambda e, m_=m_slot: e.tensor_scalar(
            scr[0:1, 55:56], scr[0:1, m_:m_ + 1], 1.0, None, ALU.add),
            waits=[t0])
        tcnt = t0
        for it in range(iters):
            tcnt = ve(lambda e: e.tensor_scalar(
                scr[0:1, 53:54], scr[0:1, 51:52], scr[0:1, 52:53], 0.5,
                ALU.add, ALU.mult), waits=[tcnt])
            tb1 = dma(s_bnc[0:1, base:base + 1], scr[0:1, 53:54], waits=[tcnt])
            tb2 = dma(scrW[:, 4:5],
                      s_bnc[0:1, base:base + 1].to_broadcast([P, 1]),
                      waits=[tb1])
            tacc = ve(lambda e: e.memset(scrW[:, 5:6], 0.0), waits=[tb2])
            for si, s_ in enumerate(segs_ap):
                tpc = ve(lambda e, ss_=s_, jo_=junk_off: e.tensor_scalar(
                    Ai(jo_, ss_.shape[1]), ss_, scrW[:, 4:5], 0.0,
                    ALU.is_gt, ALU.add, accum_out=scrW[:, 6:7]),
                    waits=[tacc])
                tacc = ve(lambda e: e.tensor_tensor(
                    scrW[:, 5:6], scrW[:, 5:6], scrW[:, 6:7], ALU.add),
                    waits=[tpc])
            tb3 = dma(s_bnc[0:1, base + 384:base + 512]
                      .rearrange("o p -> p o"), scrW[:, 5:6], waits=[tacc])
            tb4 = dma(brow[0:1, 128:256],
                      s_bnc[0:1, base + 384:base + 512], waits=[tb3])
            tc = ve(lambda e: e.tensor_reduce(
                scr[0:1, 54:55], brow[0:1, 128:256], AXX, ALU.add),
                waits=[tb4])
            tg = ve(lambda e: e.tensor_scalar(
                scrU[0:1, 0:1], scr[0:1, 54:55], scr[0:1, 55:56], None,
                ALU.is_ge), waits=[tc])
            tl_ = ve(lambda e: e.select(scr[0:1, 51:52], scrU[0:1, 0:1],
                                        scr[0:1, 53:54], scr[0:1, 51:52]),
                     waits=[tg])
            tcnt = ve(lambda e: e.select(scr[0:1, 52:53], scrU[0:1, 0:1],
                                         scr[0:1, 52:53], scr[0:1, 53:54]),
                      waits=[tl_])
        return tcnt

    def band_setup(eps, tA_col_idx, tB_col_idx, w0):
        base = 32768 + nsel[0] * 1024
        nsel[0] += 1
        t1 = ve(lambda e: e.tensor_scalar(
            scr[0:1, 19:20], scr[0:1, 52:53], 1.0 - eps, None, ALU.mult),
            waits=w0)
        t2 = ve(lambda e: e.tensor_scalar(
            scr[0:1, 20:21], scr[0:1, 52:53], 1.0 + eps, None, ALU.mult),
            waits=[t1])
        tb1 = dma(s_bnc[0:1, base:base + 2], scr[0:1, 19:21], waits=[t2])
        ta_ = dma(scrW[:, tA_col_idx:tA_col_idx + 1],
                  s_bnc[0:1, base:base + 1].to_broadcast([P, 1]), waits=[tb1])
        tb_ = dma(scrW[:, tB_col_idx:tB_col_idx + 1],
                  s_bnc[0:1, base + 1:base + 2].to_broadcast([P, 1]),
                  waits=[tb1])
        return ta_, tb_

    def window_select(gath_segs, cnt_cols_ap, m_slot, out_slot, w0,
                      work_off):
        base = 32768 + nsel[0] * 1024
        nsel[0] += 1
        t1 = ve(lambda e: e.tensor_reduce(scrW[:, 5:6], cnt_cols_ap, AXX,
                                          ALU.add), waits=w0)
        tb3 = dma(s_bnc[0:1, base + 384:base + 512].rearrange("o p -> p o"),
                  scrW[:, 5:6], waits=[t1])
        tb4 = dma(brow[0:1, 128:256], s_bnc[0:1, base + 384:base + 512],
                  waits=[tb3])
        tcb = ve(lambda e: e.tensor_reduce(scr[0:1, 21:22],
                                           brow[0:1, 128:256], AXX, ALU.add),
                 waits=[tb4])
        tex = ve(lambda e: e.memset(A(work_off, 1024), BIGNEG), waits=w0)
        exo = [0]
        for (o, c) in gath_segs:
            for w8 in range(c // 64):
                d_ = exo[0]
                tex = ve(lambda e, o_=o, w_=w8, dd_=d_, c_=c: e.max(
                    out=A(work_off, 1024)[:, dd_:dd_ + 8],
                    in_=A(o_, c_)[:, w_ * 64:(w_ + 1) * 64]), waits=[tex])
                exo[0] += 8
        assert exo[0] <= 512
        t2x = ve(lambda e: e.memset(ex2[:], BIGNEG), waits=[tex])
        n2 = exo[0] // 16
        for w8 in range(n2):
            t2x = ve(lambda e, w_=w8: e.max(
                out=ex2[:, w_ * 8:(w_ + 1) * 8],
                in_=A(work_off, 1024)[:, w_ * 16:(w_ + 1) * 16]), waits=[t2x])
        tcs = dma(s_cons[:, 0:256], ex2[:], waits=[t2x])
        tcl = dma(cons[0:32, 0:1024],
                  s_cons.rearrange("(b a) c -> b (a c)", b=32), waits=[tcs])
        t0 = ve(lambda e: e.tensor_copy(scr[0:1, 57:58], scr[0:1, 19:20]),
                waits=[tcl, tcb])
        t0 = ve(lambda e: e.tensor_copy(scr[0:1, 58:59], scr[0:1, 20:21]),
                waits=[t0])
        t0 = ve(lambda e, m_=m_slot: e.tensor_scalar(
            scr[0:1, 55:56], scr[0:1, m_:m_ + 1], 1.0, None, ALU.add),
            waits=[t0])
        tloc = t0
        for it in range(LOC_IT):
            tloc = ve(lambda e: e.tensor_scalar(
                scr[0:1, 59:60], scr[0:1, 57:58], scr[0:1, 58:59], 0.5,
                ALU.add, ALU.mult), waits=[tloc])
            tr1 = ve(lambda e: e.tensor_copy(
                stg[0:1, 0:32], scr[0:1, 59:60].to_broadcast([1, 32])),
                waits=[tloc])
            tr2 = ve(lambda e: e.transpose(stg[0:32, 32:64], stg[0:32, 0:32]),
                     waits=[tr1])
            tpc = ve(lambda e: e.tensor_scalar(
                cons[0:32, 1024:2048], cons[0:32, 0:1024], stg[0:32, 32:33],
                0.0, ALU.is_gt, ALU.add, accum_out=stg[0:32, 33:34]),
                waits=[tr2])
            tr3 = ve(lambda e: e.tensor_copy(
                stg[0:32, 0:32], stg[0:32, 33:34].to_broadcast([32, 32])),
                waits=[tpc])
            tr4 = ve(lambda e: e.transpose(stg[0:32, 32:64], stg[0:32, 0:32]),
                     waits=[tr3])
            tr5 = ve(lambda e: e.tensor_reduce(
                scr[0:1, 60:61], stg[0:1, 32:64], AXX, ALU.add), waits=[tr4])
            tr6 = ve(lambda e: e.tensor_scalar(
                scr[0:1, 60:61], scr[0:1, 60:61], scr[0:1, 21:22], None,
                ALU.add), waits=[tr5])
            tg = ve(lambda e: e.tensor_scalar(
                scrU[0:1, 1:2], scr[0:1, 60:61], scr[0:1, 55:56], None,
                ALU.is_ge), waits=[tr6])
            tl_ = ve(lambda e: e.select(scr[0:1, 57:58], scrU[0:1, 1:2],
                                        scr[0:1, 59:60], scr[0:1, 57:58]),
                     waits=[tg])
            tloc = ve(lambda e: e.select(scr[0:1, 58:59], scrU[0:1, 1:2],
                                         scr[0:1, 58:59], scr[0:1, 59:60]),
                      waits=[tl_])
        tr1 = ve(lambda e: e.tensor_copy(
            stg[0:1, 0:32], scr[0:1, 58:59].to_broadcast([1, 32])),
            waits=[tloc])
        tr2 = ve(lambda e: e.transpose(stg[0:32, 32:64], stg[0:32, 0:32]),
                 waits=[tr1])
        tm1 = ve(lambda e: e.tensor_scalar(
            cons[0:32, 1024:2048], cons[0:32, 0:1024], stg[0:32, 32:33],
            -BIGOFF, ALU.is_gt, ALU.mult), waits=[tr2])
        tm2 = ve(lambda e: e.tensor_tensor(
            cons[0:32, 1024:2048], cons[0:32, 1024:2048], cons[0:32, 0:1024],
            ALU.add), waits=[tm1])
        tm3 = ve(lambda e: e.tensor_reduce(
            stg[0:32, 34:35], cons[0:32, 1024:2048], AXX, ALU.max),
            waits=[tm2])
        tm4 = ve(lambda e: e.tensor_copy(
            stg[0:32, 0:32], stg[0:32, 34:35].to_broadcast([32, 32])),
            waits=[tm3])
        tm5 = ve(lambda e: e.transpose(stg[0:32, 32:64], stg[0:32, 0:32]),
                 waits=[tm4])
        return ve(lambda e, o_=out_slot: e.tensor_reduce(
            scr[0:1, o_:o_ + 1], stg[0:1, 32:64], AXX, ALU.max), waits=[tm5])

    def band_correction(stream_src, nchunks, g_win, mask_i8_off, abs_off,
                        stream_off, exw, cc_in, cc_out, gseg_off, work_off,
                        tA_ci, tB_ci, pre_toks, cc_waits):
        tz = ve(lambda e: e.memset(scrW[:, 15:16], 0.0), waits=pre_toks)
        tz2 = ve(lambda e: e.memset(cons[:, :], BIGNEG), waits=pre_toks)
        prev = None
        tmx = tz2
        for k in range(nchunks):
            w = [tz, tz2] + list(pre_toks)
            if prev is not None:
                w.append(prev)
            tl = dma(A(stream_off, 2048),
                     stream_src[k * P:(k + 1) * P, :], waits=w)
            tab = ac(lambda e: e.activation(
                out=A(abs_off, 2048), in_=A(stream_off, 2048), func=AF.Abs),
                waits=[tl])
            tcm = ve(lambda e: e.tensor_scalar(
                Ai(mask_i8_off, 2048), A(abs_off, 2048),
                scrW[:, tB_ci:tB_ci + 1], 0.0, ALU.is_gt, ALU.add,
                accum_out=scrW[:, 6:7]), waits=[tab, tmx])
            tca = ve(lambda e: e.tensor_tensor(
                scrW[:, 15:16], scrW[:, 15:16], scrW[:, 6:7], ALU.add),
                waits=[tcm])
            ty = ve(lambda e: e.scalar_tensor_tensor(
                A(stream_off, 2048), Ai(mask_i8_off, 2048), -BIGOFF,
                A(abs_off, 2048), ALU.mult, ALU.add), waits=[tca])
            tmx = ty
            nw = 2048 // g_win
            for w8 in range(nw):
                tmx = ve(lambda e, k_=k, w_=w8, nw_=nw: e.max(
                    out=cons[:, (k_ * nw_ + w_) * 8:(k_ * nw_ + w_ + 1) * 8],
                    in_=A(stream_off, 2048)[:, w_ * g_win:(w_ + 1) * g_win]),
                    waits=[tmx])
            prev = tmx
        used = nchunks * (2048 // g_win) * 8
        tcp = tmx
        tzc = ve(lambda e: e.memset(A(exw, 512), BIGNEG), waits=[tcp])
        tcp = tzc
        for w8 in range(used // 16):
            tcp = ve(lambda e, w_=w8: e.max(
                out=A(exw, 512)[:, w_ * 8:(w_ + 1) * 8],
                in_=cons[:, w_ * 16:(w_ + 1) * 16]), waits=[tcp])
        t1 = dma(cc_in.ap()[:, 0:512], A(exw, 512), waits=[tcp])
        t2 = dma(cc_in.ap()[:, 512:513], scrW[:, 15:16], waits=[tcp],
                 slow=True)
        tg = collective([list(range(NCORE))], cc_in, cc_out,
                        waits=[t1, t2] + list(cc_waits))
        tok_loads = []
        for r in range(NCORE):
            tok_loads.append(dma(A(gseg_off + r * 512, 512),
                                 cc_out.ap()[r * P:(r + 1) * P, 0:512],
                                 waits=[tg]))
            tok_loads.append(dma(m8s[:, r:r + 1],
                                 cc_out.ap()[r * P:(r + 1) * P, 512:513],
                                 waits=[tg], slow=True))
        gsegs = [(gseg_off + r * 512, 512) for r in range(NCORE)]
        return gsegs, m8s[:, 0:8], tok_loads

    def scales_from_v(v_slot, g_slot, s_col_idx, gs_col_idx, w0):
        t1 = ve(lambda e, v_=v_slot: e.tensor_scalar(
            scr[0:1, 46:47], scr[0:1, v_:v_ + 1], EPS32, None, ALU.max),
            waits=w0)
        t2 = ve(lambda e: e.reciprocal(scr[0:1, 47:48], scr[0:1, 46:47]),
                waits=[t1])
        t3 = ve(lambda e: e.tensor_scalar(
            scr[0:1, 48:49], scr[0:1, 47:48], 127.0, None, ALU.mult),
            waits=[t2])
        t4 = ve(lambda e, g_=g_slot: e.tensor_tensor(
            scr[0:1, 49:50], scr[0:1, 46:47], scr[0:1, g_:g_ + 1], ALU.mult),
            waits=[t3])
        t5 = ve(lambda e: e.tensor_scalar(
            scr[0:1, 49:50], scr[0:1, 49:50], 1.0 / 127.0, None, ALU.mult),
            waits=[t4])
        tb = bcast_rows([scr[0:1, 48:49]], scrW[:, s_col_idx:s_col_idx + 1],
                        [t5], cols=1)
        return bcast_rows([scr[0:1, 49:50]],
                          scrW[:, gs_col_idx:gs_col_idx + 1], [tb], cols=1)

    t_g1 = [dma(A(RW_A + r * 512, 512), cc_q1_o.ap()[r * P:(r + 1) * P, :],
                waits=[t_gc1, t_h0[KC - 1], t_m8, t_w1q])
            for r in range(NCORE)]
    t_m1d = lax_dance(2 ** 24, 10, 11)
    t_fb1 = fat_bisect(q1segs, 10, t_g1 + [t_m1d], RW_C)
    tA1, tB1 = band_setup(0.008, 13, 14, [t_fb1])
    g1segs, cnt1_ap, t_bc1 = band_correction(
        s_h0, KC, 128, RW_C, RW_B, XS[8], RW_B + 1024, cc_w1_i, cc_w1_o,
        RW_A, RW_B, 13, 14,
        [tA1, tB1, m23, t_hfix[KC - 1], t_fb1], [t_car_g])
    t_q1v = window_select(g1segs, cnt1_ap, 10, 12, t_bc1 + [t_fb1],
                          RW_B + 2048 if False else RW_C)
    t_s1 = scales_from_v(12, 0, 9, 10, [t_q1v, t_g1w])

    rr_rep = A(RW_B, 2048)
    t_rr = bcast_rows([st_m2(tt) for tt in range(NTT)], rr_rep,
                      rr_done + [t_q1v, t_hfix[KC - 1]])
    if debug:
        dbg_rr = nc.dram_tensor("dbg_rr", [P, 2048], F32,
                                kind="ExternalOutput").ap()
        t_rr = dma(dbg_rr, rr_rep, waits=[t_rr])
    t_ha = {}
    for do in range(KC):
        t1 = ve(lambda e, d_=do: e.tensor_tensor(
            A(XS[d_], 2048), A(XS[d_], 2048), rr_rep, ALU.mult),
            waits=[t_rr, t_hfix[do], t_dbghf.get(do)])
        t_ha[do] = ve(lambda e, d_=do: e.tensor_scalar(
            A(XS[d_], 2048), A(XS[d_], 2048), normw_c(d_), None, ALU.mult),
            waits=[t1, t_aux])
    t_ha_all = [t_ha[d] for d in range(KC)]
    if debug:
        dbg_ha = nc.dram_tensor("dbg_ha", [D, TC], F32,
                                 kind="ExternalOutput").ap()
        t_dbgha = [dma(dbg_ha[k * P:(k + 1) * P, :], A(XS[k], 2048),
                       waits=[t_ha[k]]) for k in range(KC)]
        t_ha_all = t_ha_all + t_dbgha

    t_xq = {}
    for k in range(KC):
        w = [t_s1, t_hfix[KC - 1]]
        if k:
            w.append(t_xq[k - 1])
        tl = dma(A(XS[8], 2048), s_h0[k * P:(k + 1) * P, :], waits=w)
        ti = ve(lambda e: e.tensor_scalar(
            Ai(RW_C, 2048), A(XS[8], 2048), scrW[:, 9:10], None, ALU.mult),
            waits=[tl, t_s1])
        t_xq[k] = ve(lambda e, k_=k: e.tensor_copy(
            Ab(HS[0 + k_ // 2] + (k_ % 2) * 1024, 2048), Ai(RW_C, 2048)),
            waits=[ti])

    def xq_ap(k, tt):
        return Ab(HS[0 + k // 2] + (k % 2) * 1024, 2048)[:, tt * 512:(tt + 1) * 512]

    if debug:
        dbg_xq1 = nc.dram_tensor("dbg_xq1", [D, TC], BF16,
                                 kind="ExternalOutput").ap()
        t_dbgxq = [dma(dbg_xq1[k * P:(k + 1) * P, :],
                       Ab(HS[0 + k // 2] + (k % 2) * 1024, 2048),
                       waits=[t_xq[k]]) for k in range(KC)]
        t_xq[KC - 1] = t_dbgxq[-1]

    red1q2 = A(HS[4], 4096)
    contq2 = A(HS[6], 1024)
    t_mm1_ev, t_hg, mm1_last = {}, {}, {}
    t_red2 = [None]
    for h in range(HC):
        tl = dma(Ab(RW_B + (h % 2) * 512, 1024).rearrange(
                     "p (k j) -> p k j", j=P),
                 s_w1q[:, h * P:(h + 1) * P].rearrange("(k p) j -> p k j",
                                                       p=P),
                 waits=[t_w1q, t_rr, t_ha_all[-1]] +
                 ([mm1_last[h - 2]] if h >= 2 else []))
        for k in range(KC):
            for tt in range(NTT):
                w = [tl, t_xq[KC - 1]]
                if k == 0 and h > 0:
                    w.append(t_mm1_ev[(h - 1, tt)])
                mm = te(lambda e, k_=k, t_=tt, h_=h: e.matmul(
                    pb[t_][:, :],
                    Ab(RW_B + (h_ % 2) * 512, 1024)[:, k_ * P:(k_ + 1) * P],
                    xq_ap(k_, t_), start=(k_ == 0), stop=(k_ == KC - 1)),
                    waits=w)
                if k == KC - 1:
                    t_mm1_ev[(h, tt)] = ac(
                        lambda e, t_=tt: e.activation(
                            out=A(XS[8], 2048)[:, t_ * 512:(t_ + 1) * 512],
                            in_=pb[t_][:, :], func=AF.Gelu,
                            scale=scrW[:, 10:11]),
                        waits=[tdrain(mm), t_s1] +
                        ([t_hg[h - 1], t_red2[0]] if (tt == 0 and h > 0)
                         else []))
        mm1_last[h] = TE.tok()
        tevs = [t_mm1_ev[(h, tt)] for tt in range(NTT)]
        tevd = acdrain(*tevs)
        t_hg[h] = dma(s_hg[h * P:(h + 1) * P, :], A(XS[8], 2048),
                      waits=[tevd])
        ta = ac(lambda e: e.activation(
            out=A(RW_A, 2048), in_=A(XS[8], 2048), func=AF.Abs),
            waits=tevs + [t_red2[0], t_q1v, t_xq[KC - 1]])
        for w8 in range(16):
            ta = ve(lambda e, h_=h, w_=w8: e.max(
                out=red1q2[:, h_ * P + w_ * 8:h_ * P + (w_ + 1) * 8],
                in_=A(RW_A, 2048)[:, w_ * 128:(w_ + 1) * 128]), waits=[ta])
        t_red2[0] = ta

    t_g2w = emit_gamma(w2T, H, D, (HS[6], HS[7]), 1, pre=t_xq[KC - 1],
                       first_wait=[ev3a, ev3b])
    t_w2q = emit_ternarize(w2T, s_w2q, H, D, (HS[6], HS[7]), RW_A + 2048,
                           RW_A + 3072, RW_A + 3328, 1, t_g2w, 1)

    t_c2 = (sems["dve"], max(t_red2[0][1], t_w2q[1]) if False else t_red2[0][1])
    t_c2 = t_red2[0]
    first = True
    for w8 in range(64):
        w = [t_c2] + ([t_w2q] if first else [])
        first = False
        t_c2 = ve(lambda e, w_=w8: e.max(
            out=contq2[:, w_ * 16:w_ * 16 + 8],
            in_=red1q2[:, w_ * 64:(w_ + 1) * 64]), waits=w)
        t_c2 = ve(lambda e, w_=w8: e.match_replace(
            out=red1q2[:, w_ * 64:(w_ + 1) * 64],
            in_to_replace=contq2[:, w_ * 16:w_ * 16 + 8],
            in_values=red1q2[:, w_ * 64:(w_ + 1) * 64],
            imm_value=BIGNEG), waits=[t_c2])
        t_c2 = ve(lambda e, w_=w8: e.max(
            out=contq2[:, w_ * 16 + 8:w_ * 16 + 16],
            in_=red1q2[:, w_ * 64:(w_ + 1) * 64]), waits=[t_c2])

    t_ci2 = dma(cc_q2_i.ap(), contq2, waits=[t_c2])
    t_gc2 = collective([list(range(NCORE))], cc_q2_i, cc_q2_o, waits=[t_ci2])
    q2segs = [(HS[2] + r * 1024, 1024) for r in range(NCORE)]
    t_g2 = [dma(A(HS[2] + r * 1024, 1024),
                cc_q2_o.ap()[r * P:(r + 1) * P, :],
                waits=[t_gc2, t_w2q, t_c2]) for r in range(NCORE)]

    t_m2d = lax_dance(2 ** 26, 13, 14)
    t_fb2 = fat_bisect(q2segs, 13, t_g2 + [t_m2d, t_ci2], HS[6])
    tA2, tB2 = band_setup(0.003, 13, 14, [t_fb2])
    g2segs, cnt2_ap, t_bc2 = band_correction(
        s_hg, HC, 512, HS[6], HS[7], XS[8], HS[7] + 1024, cc_w2_i, cc_w2_o,
        HS[2], HS[4], 13, 14,
        [tA2, tB2, t_fb2, t_hg[HC - 1]], [t_gc2])
    t_q2v = window_select(g2segs, cnt2_ap, 13, 15, t_bc2 + [t_fb2], HS[4])
    t_s2 = scales_from_v(15, 1, 11, 12, [t_q2v, t_g2w])

    xn16_base = [HS[0], HS[0] + 1024, HS[1], HS[1] + 1024,
                 RW_A, RW_A + 1024, HS[3], HS[3] + 1024]

    def xn16_ap2(k, tt):
        return Ab(xn16_base[k], 2048)[:, tt * 512:(tt + 1) * 512]

    xn16_ld2 = {}
    for k in range(KC):
        w = [mm1_last[HC - 1], t_xq[KC - 1], t_w2q] + \
            [t_mm1_ev[(HC - 1, tt)] for tt in range(NTT)]
        if k in (4, 5):
            w.append(t_red2[0])
        if k in (6, 7):
            w.append(t_q2v)
        xn16_ld2[k] = dma(Ab(xn16_base[k], 2048),
                          s_xn16[k * P:(k + 1) * P, :], waits=w)
    t_x16b = xn16_ld2[KC - 1]

    t_o = {}

    def g_evac2(do, tt, mm):
        t1 = ac(lambda e, t_=tt: e.activation(
            out=A(RW_C, 512), in_=pb[t_][:, :], func=AF.Silu),
            waits=[mm] + ([t_o[(do, tt - 1)]] if tt else
                          ([t_o[(do - 1, NTT - 1)]] if do else [])))
        t1d = acdrain(t1)
        t2 = ve(lambda e, d_=do, t_=tt: e.tensor_tensor(
            A(RW_B + 1024, 512), A(RW_C, 512),
            A(XS[d_], 2048)[:, t_ * 512:(t_ + 1) * 512], ALU.mult),
            waits=[t1d, t_ha[do]])
        t3 = ve(lambda e, d_=do, t_=tt: e.tensor_copy(
            Ab(XS[d_], 2048)[:, t_ * 512:(t_ + 1) * 512],
            A(RW_B + 1024, 512)), waits=[t2])
        t_o[(do, tt)] = t3
        return t3

    emit_projection(wgT, xn16_ap2, t_x16b, g_evac2, RQ, pre=xn16_ld2[0],
                    rhs_tok_fn=lambda k: xn16_ld2[k])
    t_o_all = [t_o[(d, t)] for d in range(KC) for t in range(NTT)]
    t_o_last = t_o_all[-1]

    def o_ap(k, tt):
        return Ab(XS[k], 2048)[:, tt * 512:(tt + 1) * 512]

    t_sat = {}

    def wo_evac(do, tt, mm):
        idx = do * NTT + tt
        buf = RW_B + 1024 + (idx % 2) * 512
        iprev = idx - 2
        wprev = t_sat.get((iprev // NTT, iprev % NTT)) if iprev >= 0 else None
        t1 = ac(lambda e, t_=tt, b_=buf: e.activation(
            out=A(b_, 512), in_=pb[t_][:, :], func=AF.Copy),
            waits=[mm, wprev])
        t1d = acdrain(t1)
        tok = dma(s_sat[do * P:(do + 1) * P, tt * 512:(tt + 1) * 512],
                  A(buf, 512), waits=[t1d])
        t_sat[(do, tt)] = tok
        return tok

    emit_projection(woT, o_ap, t_o_last, wo_evac, RQ, pre=t_o_last)
    t_sat_all = [t_sat[(d, t)] for d in range(KC) for t in range(NTT)]

    def hq_ap(h, tt):
        return Ab(XS[0 + h // 8] + (h % 8) * 256, 512)

    t_out_st = []
    t_hq = {}
    t_ti = {}
    t_c1 = {}
    t_mm2ev = {}
    w2strips = [XS[8], HS[0]]
    t_w2s = {}
    t_comb = {}
    for tt in range(NTT):
        for h in range(HC):
            w = [t_s2, t_hg[h], t_sat_all[-1]]
            if tt > 0:
                w.append(t_mm2ev[(tt - 1, KC - 1)])
            prev_ti = t_ti.get((h - 4, tt))
            if prev_ti is not None:
                w.append(prev_ti)
            hqstg = RW_A + (h % 4) * 512
            tl = dma(A(hqstg, 512),
                     s_hg[h * P:(h + 1) * P, tt * 512:(tt + 1) * 512],
                     waits=w)
            ti = ve(lambda e, s_=hqstg: e.tensor_scalar(
                Ai(RW_B, 512), A(s_, 512), scrW[:, 11:12], None, ALU.mult),
                waits=[tl, t_s2])
            t_ti[(h, tt)] = ti
            t_hq[(h, tt)] = ve(lambda e, h_=h, t_=tt: e.tensor_copy(
                hq_ap(h_, t_), Ai(RW_B, 512)), waits=[ti])
        hq_done = ve(lambda e: e.tensor_copy(SS[96:97, 0:1], SS[96:97, 0:1]),
                     waits=[t_hq[(HC - 1, tt)]])
        for dd in range(KC):
            sidx = dd % 2
            w = [t_w2q, hq_done]
            if (tt, dd - 2) in t_mm2ev:
                w.append(t_mm2ev[(tt, dd - 2)])
            if tt > 0 and dd < 2:
                w.append(t_mm2ev[(tt - 1, KC - 2 + dd)])
            if tt == 0 and dd < 2:
                w.append(t_sat_all[-1])
                w.append(t_xq[KC - 1])
            tl2 = dma(Ab(w2strips[sidx], 4096).rearrange(
                          "p (k j) -> p k j", j=P),
                      s_w2q[:, dd * P:(dd + 1) * P].rearrange(
                          "(k p) j -> p k j", p=P), waits=w)
            bank = dd % 2
            for hk in range(HC):
                mm = te(lambda e, hk_=hk, b_=bank, s_=sidx, t_=tt: e.matmul(
                    pb[b_][:, :],
                    Ab(w2strips[s_], 4096)[:, hk_ * P:(hk_ + 1) * P],
                    hq_ap(hk_, t_), start=(hk_ == 0), stop=(hk_ == HC - 1)),
                    waits=[tl2, hq_done] +
                    ([t_mm2ev[(tt, dd - 2)]] if (tt, dd - 2) in t_mm2ev
                     else []))
            wg_ = [tdrain(mm), t_s2]
            prev_c1 = t_c1.get((tt, dd - 1)) or (
                t_c1.get((tt - 1, KC - 1)) if dd == 0 else None)
            if prev_c1 is not None:
                wg_.append(prev_c1)
            tg_ = ac(lambda e, b_=bank: e.activation(
                out=A(RW_C, 512), in_=pb[b_][:, :], func=AF.Silu,
                scale=scrW[:, 12:13]), waits=wg_)
            t_mm2ev[(tt, dd)] = tg_
            tgd = acdrain(tg_)
            prev_cmb = t_comb.get((tt, dd - 1)) or \
                t_comb.get((tt - 1, KC - 1)) or t_sat_all[-1]
            txl = dma(A(HS[1], 512), xs[dd * P:(dd + 1) * P,
                                        tt * 512:(tt + 1) * 512],
                      waits=[prev_cmb])
            tsl = dma(A(HS[1] + 512, 512),
                      s_sat[dd * P:(dd + 1) * P, tt * 512:(tt + 1) * 512],
                      waits=[t_sat[(dd, tt)], prev_cmb])
            cidx = tt * KC + dd
            acc = HS[1] + 1024 + (cidx % 2) * 512
            tc1 = ve(lambda e, a_=acc: e.tensor_tensor(
                A(a_, 512), A(RW_C, 512), A(HS[1] + 512, 512),
                ALU.mult), waits=[tgd, tsl] +
                ([t_out_st[-2]] if len(t_out_st) >= 2 else []))
            t_c1[(tt, dd)] = tc1
            tc2 = ve(lambda e, a_=acc: e.tensor_tensor(
                A(a_, 512), A(a_, 512), A(HS[1], 512),
                ALU.add), waits=[tc1, txl])
            t_comb[(tt, dd)] = tc2
            t_out_st.append(dma(out[dd * P:(dd + 1) * P,
                                    tt * 512:(tt + 1) * 512],
                                A(acc, 512), waits=[tc2], sc=True))

    if debug:
        dbg_xq2 = nc.dram_tensor("dbg_xq2", [D, TC], BF16, kind="ExternalOutput").ap()
        for k in range(KC):
            off_ = 18432 + (k // 2) * 2048 + (k % 2) * 1024
            dma(dbg_xq2[k * P:(k + 1) * P, :],
                arena[:, off_:off_ + 1024].bitcast(BF16),
                waits=[mm1_last[HC - 1], t_hg[HC - 1]])
        dbgc1 = nc.dram_tensor("dbgc1", [P, 512], F32, kind="ExternalOutput").ap()
        t_out_st.append(dma(dbgc1, contq1, waits=[t_q1v]))
        dbgc2 = nc.dram_tensor("dbgc2", [P, 1024], F32, kind="ExternalOutput").ap()
        t_out_st.append(dma(dbgc2, contq2, waits=[t_q2v]))
        dbg = nc.dram_tensor("dbg", [2, 64], F32, kind="ExternalOutput").ap()
        t_out_st.append(dma(dbg[0:1, :], scr[0:1, :], waits=[t_comb[(NTT - 1, KC - 1)]]))
        t_out_st.append(dma(dbg[1:2, 0:16], scrW[0:1, 0:16], waits=[t_comb[(NTT - 1, KC - 1)]]))
    def fin(e):
        e.wait_ge(sems["dma"], t_out_st[-1][1])
    GP.ops.append(fin)

    with nc.Block() as block:
        @block.sync
        def _(sync):
            for op in SY.ops:
                op(sync)

        @block.tensor
        def _(eng):
            for op in TE.ops:
                op(eng)

        @block.scalar
        def _(eng):
            for op in SC_.ops:
                op(eng)

        @block.vector
        def _(eng):
            for op in VE.ops:
                op(eng)

        @block.gpsimd
        def _(eng):
            for op in GP.ops:
                op(eng)

    ctx.close()
    return nc


_NC_CACHE = [None]


def kernel(L0_attn_x, L0_attn_Wi, L0_attn_Wf, L0_attn_Wg, L0_attn_Wo,
           L0_attn_norm_w, L0_ffn_ln_g, L0_ffn_ln_b, L0_ffn_w1, L0_ffn_w2):
    x = np.asarray(L0_attn_x, dtype=np.float32)
    wi = np.ascontiguousarray(np.asarray(L0_attn_Wi, np.float32).T)
    wf = np.ascontiguousarray(np.asarray(L0_attn_Wf, np.float32).T)
    wg = np.ascontiguousarray(np.asarray(L0_attn_Wg, np.float32).T)
    wo = np.ascontiguousarray(np.asarray(L0_attn_Wo, np.float32).T)
    w1 = np.ascontiguousarray(np.asarray(L0_ffn_w1, np.float32).T)
    w2 = np.ascontiguousarray(np.asarray(L0_ffn_w2, np.float32).T)
    nw = np.asarray(L0_attn_norm_w, np.float32)
    lg = np.asarray(L0_ffn_ln_g, np.float32)
    lb = np.asarray(L0_ffn_ln_b, np.float32)

    in_maps = []
    for c in range(NCORE):
        b, r = c // 4, c % 4
        xsl = np.ascontiguousarray(x[b, r * TC:(r + 1) * TC, :].T)
        auxm = np.zeros((P, 32), np.float32)
        auxm[:, 0:8] = nw.reshape(KC, P).T
        auxm[:, 8:16] = lg.reshape(KC, P).T
        auxm[:, 16:24] = lb.reshape(KC, P).T
        for q in range(3):
            auxm[:, 24 + q] = 1.0 if q < r else 0.0
        auxm[:, 27] = np.float32(1e-5)
        auxm[:, 28] = np.float32(0.995)
        in_maps.append({"x": xsl, "wiT": wi, "wfT": wf, "wgT": wg, "woT": wo,
                        "w1T": w1, "w2T": w2, "aux": auxm})

    if _NC_CACHE[0] is None:
        _NC_CACHE[0] = build_program()
    res = run_bass_kernel_spmd(_NC_CACHE[0], in_maps, core_ids=list(range(NCORE)))
    y = np.empty((B, T, D), np.float32)
    for c in range(NCORE):
        b, r = c // 4, c % 4
        y[b, r * TC:(r + 1) * TC, :] = res.results[c]["out"].T
    return y
